# version 11
# baseline (speedup 1.0000x reference)
import numpy as np
import ml_dtypes
import concourse.bass as bass
import concourse.mybir as mybir
from concourse.bass_utils import run_bass_kernel_spmd
from contextlib import ExitStack

F32 = mybir.dt.float32
BF16 = mybir.dt.bfloat16
AF = mybir.ActivationFunctionType
ALU = mybir.AluOpType
AX = mybir.AxisListType

NC = 8
B, S, D = 4, 4096, 1024
H, HD = 16, 64
FF, E = 2048, 8
T = 512
NT = T // 128
NKC = S // 128
EPS = 1e-5
BF = ml_dtypes.bfloat16



def _cb(v, n):
    return np.ascontiguousarray(np.asarray(v, np.float32).reshape(n, 128).T)


def stage_common(ins):
    g = {}
    g["wqT"] = ins["L0_attn_Wq"].T
    g["wkT"] = ins["L0_attn_Wk"].T
    g["wvT"] = ins["L0_attn_Wv"].T
    g["woT"] = ins["L0_attn_Wo"].T
    g["bq"] = _cb(ins["L0_attn_bq"], 8)
    g["bk"] = _cb(ins["L0_attn_bk"], 8)
    g["bv_row"] = np.asarray(ins["L0_attn_bv"]).reshape(1, D)
    g["bo"] = _cb(ins["L0_attn_bo"], 8)
    g["fqw"] = ins["L0_ffn_q_w"]
    g["fkw"] = ins["L0_ffn_k_w"]
    g["fvw"] = ins["L0_ffn_v_w"]
    g["fow"] = ins["L0_ffn_out_w"]
    g["fgw"] = ins["L0_ffn_gate_w"]
    g["fgb"] = _cb(ins["L0_ffn_gate_b"], 8)
    sw = np.asarray(ins["L0_ffn_scale_w"], np.float32)
    swbd = np.zeros((D, 2), np.float32)
    for h in range(H):
        swbd[h * HD:(h + 1) * HD, h % 2] = sw[h]
    g["swbd"] = swbd
    g["n1g"] = _cb(ins["L0_ffn_n1_g"], 8)
    g["n1b"] = _cb(ins["L0_ffn_n1_b"], 8)
    g["n2g"] = _cb(ins["L0_ffn_n2_g"], 8)
    g["n2b"] = _cb(ins["L0_ffn_n2_b"], 8)
    g["mgw"] = ins["L0_ffn_moe_gate_w"]
    g["mgb"] = np.asarray(ins["L0_ffn_moe_gate_b"], np.float32).reshape(8, 1)
    g["w1"] = ins["L0_ffn_w1"]
    g["b1c"] = np.asarray(ins["L0_ffn_b1"], np.float32).reshape(E, 16, 128).transpose(0, 2, 1)
    g["lngc"] = np.asarray(ins["L0_ffn_ln_g"], np.float32).reshape(E, 16, 128).transpose(0, 2, 1)
    g["lnbc"] = np.asarray(ins["L0_ffn_ln_b"], np.float32).reshape(E, 16, 128).transpose(0, 2, 1)
    g["w2"] = ins["L0_ffn_w2"]
    g["b2"] = ins["L0_ffn_b2"]
    g["rsc_row"] = np.asarray(ins["L0_ffn_res_scale"], np.float32).reshape(1, 8)
    g["rsc_col"] = np.asarray(ins["L0_ffn_res_scale"], np.float32).reshape(8, 1)
    g["ipT"] = ins["L1_attn_in_proj_weight"].T
    ipb = np.asarray(ins["L1_attn_in_proj_bias"], np.float32)
    g["ipb_qk"] = _cb(ipb[:2048], 16)
    g["ipb_v"] = ipb[2048:].reshape(1, D)
    g["opT"] = ins["L1_attn_out_proj_weight"].T
    g["opb"] = _cb(ins["L1_attn_out_proj_bias"], 8)
    g["wfc"] = ins["L1_ffn_w_fc"]
    g["wfcT"] = ins["L1_ffn_w_fc"].T
    g["wpr"] = ins["L1_ffn_w_proj"]
    g["wprT"] = ins["L1_ffn_w_proj"].T
    g["l1uT"] = ins["L1_ffn_u"].T
    g["l1vT"] = ins["L1_ffn_v"].T
    g["fbT"] = ins["L1_ffn_fast_b"].T
    g["l1gate"] = np.asarray(ins["L1_ffn_gate"], np.float32).reshape(1, 1)
    g["c_ident"] = np.eye(128, dtype=np.float32)
    g["c_ones_col"] = np.ones((128, 1), np.float32)
    g["c_ones_row"] = np.ones((1, 128), np.float32)
    g["c_ones64"] = np.ones((1, 64), np.float32)
    g["c_iota8"] = np.tile(np.arange(8, dtype=np.float32)[None, :], (128, 1))
    bp = np.zeros((2, 128), np.float32)
    bp[0, 0:64] = 1.0
    bp[1, 64:128] = 1.0
    g["c_bpair"] = bp
    g["c_ones16"] = np.ones((128, 16), np.float32)
    kk = np.arange(128)[:, None]
    pp = np.arange(512)[None, :]
    g["c_diagmask"] = np.stack([(kk <= pp - 128 * j) for j in range(4)]).astype(np.float32)
    jj = np.arange(128)[:, None]
    ii = np.arange(128)[None, :]
    g["c_l1mask"] = ((jj // 4) == (ii // 4)).astype(np.float32)
    return {k: np.ascontiguousarray(np.asarray(v, np.float32)) for k, v in g.items()}


def reorder_causal(kfull, vfull, core):
    qb = core * T
    nearly = qb // 128
    order = [qb // 128 + j for j in range(4)] + list(range(nearly))
    ko = np.zeros_like(kfull)
    vo = np.zeros_like(vfull)
    for pos, kc in enumerate(order):
        ko[:, :, :, pos * 128:(pos + 1) * 128] = kfull[:, :, :, kc * 128:(kc + 1) * 128]
        vo[:, :, pos] = vfull[:, :, kc]
    return np.ascontiguousarray(ko), np.ascontiguousarray(vo)



class Ctx:
    def __init__(self, nc):
        self.nc = nc
        self.es = ExitStack()
        self.dram = {}

    def dram_t(self, name, shape, dt, kind):
        if name in self.dram:
            return self.dram[name]
        t = self.nc.dram_tensor(name, list(shape), dt, kind=kind).ap()
        self.dram[name] = t
        return t


class _Mark:
    __slots__ = ("w",)

    def __init__(self):
        self.w = None


class Seq:

    CENGS = ("tensor", "scalar", "vector", "gpsimd")
    KEYS = ("t", "sc", "v", "gp", "s", "g", "sd", "gd")

    def __init__(self, cx):
        self.cx = cx
        nc = cx.nc
        self.sems = {
            "t": cx.es.enter_context(nc.semaphore("sq_t")),
            "sc": cx.es.enter_context(nc.semaphore("sq_sc")),
            "v": cx.es.enter_context(nc.semaphore("sq_v")),
            "gp": cx.es.enter_context(nc.semaphore("sq_gp")),
            "s": cx.es.enter_context(nc.semaphore("sq_s")),
            "g": cx.es.enter_context(nc.semaphore("sq_g")),
            "sd": cx.es.enter_context(nc.semaphore("sq_sd")),
            "gd": cx.es.enter_context(nc.semaphore("sq_gd")),
        }
        self.ckey = {"tensor": "t", "scalar": "sc", "vector": "v",
                     "gpsimd": "gp"}
        self.n = {k: 0 for k in self.KEYS}
        self.w_sd = 0
        self.w_gd = 0
        self.hw = {e: {} for e in ("tensor", "scalar", "vector", "gpsimd",
                                   "sync")}
        self.steps = []

    def c(self, eng, fn, at=None):
        self.steps.append((eng, "c", fn, at))

    def d(self, eng, fn, defer=False, at=None):
        self.steps.append((eng, "d", fn, (defer, at)))

    def barrier(self):
        self.steps.append((None, "bar", None, None))

    def mark(self):
        m = _Mark()
        self.steps.append((None, "mark", None, m))
        return m

    def _waits(self):
        w = dict(self.n)
        w["sd"] = self.w_sd
        w["gd"] = self.w_gd
        return w

    def _wait_list(self, eng, w):
        out = []
        hw = self.hw[eng]
        for k in self.KEYS:
            tgt = w.get(k, 0)
            if tgt <= 0 or tgt <= hw.get(k, 0):
                continue
            out.append((self.sems[k], tgt))
            hw[k] = tgt
        return out

    def _emit_crun(self, blk, eng, entries):
        def body(e, entries=entries):
            for fn, waits in entries:
                for sem, val in waits:
                    e.wait_ge(sem, val)
                r = fn(e)
                r = r[-1] if isinstance(r, (list, tuple)) else r
                r.then_inc(self.sems[self.ckey[eng]], 1)
        getattr(blk, eng)(body)

    def flush(self, blk):
        steps = self.steps
        self.steps = []
        runs = {}
        for eng, kind, fn, extra in steps:
            if kind == "bar":
                self.w_sd = self.n["sd"]
                self.w_gd = self.n["gd"]
                continue
            if kind == "mark":
                extra.w = self._waits()
                continue
            at = extra[1] if kind == "d" else extra
            if isinstance(at, _Mark):
                at = at.w
            w = at if at is not None else self._waits()
            if kind == "c":
                waits = self._wait_list(eng, w)
                runs.setdefault(eng, []).append((fn, waits))
                self.n[self.ckey[eng]] += 1
            else:
                defer = extra[0]
                assert eng in ("sync", "gpsimd")
                if eng in runs and runs[eng]:
                    self._emit_crun(blk, eng, runs.pop(eng))
                if defer:
                    key = "sd" if eng == "sync" else "gd"
                else:
                    key = "s" if eng == "sync" else "g"
                sem = self.sems[key]
                waits = self._wait_list(eng, w)
                holder = {}

                def body(e, fn=fn, waits=waits, sem=sem, holder=holder):
                    for s_, v_ in waits:
                        e.wait_ge(s_, v_)
                    r = fn(e)
                    rl = r if isinstance(r, (list, tuple)) else [r]
                    for i in rl:
                        i.then_inc(sem, 16)
                    holder["n"] = len(rl)
                getattr(blk, eng)(body)
                self.n[key] += 16 * holder["n"]
        for eng, entries in runs.items():
            if entries:
                self._emit_crun(blk, eng, entries)

    def finish(self, blk):
        w = dict(self.n)
        waits = self._wait_list("sync", w)

        def body(e):
            for sem, val in waits:
                e.wait_ge(sem, val)
        blk.sync(body)



class LnScratch:
    def __init__(self, es, nc, tag, nch=8, width=T, repm=None, repr_=None):
        sb = lambda n, s, t: es.enter_context(nc.sbuf_tensor(n, s, t))
        ps = lambda n, s: es.enter_context(nc.psum_tensor(n, s, F32))
        self.zsq = sb(f"{tag}_zsq", [128, nch, width], F32)
        self.rows = sb(f"{tag}_rows", [1, 4, width], F32)
        self.lnt = sb(f"{tag}_lnt", [128, nch, width], F32)
        self.s1p = ps(f"{tag}_s1", [1, width])
        self.s2p = ps(f"{tag}_s2", [1, width])
        self.repm = repm if repm is not None else ps(f"{tag}_repm", [128, width])
        self.repr = repr_ if repr_ is not None else ps(f"{tag}_repr", [128, width])


def ln_fm(sq, P, consts, z_sb, nch, width, out32, outbf, gamma=None, beta=None,
          n=None):
    n = n or (nch * 128)
    ones_col, ones_row = consts["ones_col"], consts["ones_row"]

    def sqs(e):
        for c in range(nch):
            i = e.activation(out=P.zsq[:, c], in_=z_sb[:, c], func=AF.Square)
        return i
    sq.c("scalar", sqs)

    def stats(e):
        for c in range(nch):
            e.matmul(P.s1p[:], ones_col[:], z_sb[:, c], start=(c == 0), stop=(c == nch - 1))
        for c in range(nch):
            i = e.matmul(P.s2p[:], ones_col[:], P.zsq[:, c], start=(c == 0),
                         stop=(c == nch - 1))
        return i
    sq.c("tensor", stats)

    sq.c("vector", lambda e: e.tensor_single_scalar(P.rows[:, 0], P.s1p[:],
                                                    1.0 / n, ALU.mult))
    sq.c("vector", lambda e: e.tensor_tensor(P.rows[:, 1], P.rows[:, 0],
                                             P.rows[:, 0], ALU.mult))
    sq.c("vector", lambda e: e.scalar_tensor_tensor(
        out=P.rows[:, 2], in0=P.s2p[:], scalar=1.0 / n, in1=P.rows[:, 1],
        op0=ALU.mult, op1=ALU.subtract))
    sq.c("vector", lambda e: e.tensor_single_scalar(P.rows[:, 2], P.rows[:, 2],
                                                    EPS, ALU.add))
    sq.c("scalar", lambda e: e.activation(out=P.rows[:, 1], in_=P.rows[:, 2],
                                          func=AF.Ln))
    sq.c("scalar", lambda e: e.activation(out=P.rows[:, 3], in_=P.rows[:, 1],
                                          func=AF.Exp, scale=-0.5))

    def reps(e):
        e.matmul(P.repm[:], ones_row[:], P.rows[:, 0], start=True, stop=True)
        return e.matmul(P.repr[:], ones_row[:], P.rows[:, 3], start=True, stop=True)
    sq.c("tensor", reps)

    def norm1(e):
        for c in range(nch):
            i = e.tensor_tensor(P.lnt[:, c], z_sb[:, c], P.repm[:], ALU.subtract)
        return i
    sq.c("vector", norm1)

    def norm2(e):
        for c in range(nch):
            i = e.tensor_tensor(P.lnt[:, c], P.lnt[:, c], P.repr[:], ALU.mult)
        return i
    sq.c("vector", norm2)

    if gamma is not None:
        tgt = out32 if out32 is not None else P.lnt

        def gb(e):
            for c in range(nch):
                i = e.activation(out=tgt[:, c], in_=P.lnt[:, c], func=AF.Identity,
                                 scale=gamma[:, c:c + 1], bias=beta[:, c:c + 1])
            return i
        sq.c("scalar", gb)
        if outbf is not None:
            def cb(e):
                for c in range(nch):
                    i = e.tensor_copy(outbf[:, c], tgt[:, c])
                return i
            sq.c("vector", cb)
    else:
        def outs(e):
            i = None
            for c in range(nch):
                if out32 is not None:
                    i = e.tensor_copy(out32[:, c], P.lnt[:, c])
                if outbf is not None:
                    i = e.tensor_copy(outbf[:, c], P.lnt[:, c])
            return i
        sq.c("vector", outs)


def cast_load(sq, dst_ap, src_ap):
    sq.d("gpsimd", lambda e: [e.dma_start(out=dst_ap, in_=src_ap)])



SLABS = {
    "xT32": ([B, 8, 128, T], F32),
    "xnbf": ([B, 8, 128, T], BF16),
    "xnf32": ([B, 8, 128, T], F32),
    "kattn": ([B, 8, 128, T], BF16),
    "kffn": ([B, 8, 128, T], BF16),
    "vattn": ([B, 4, NT, 128, 260], BF16),
    "vffn": ([B, 4, NT, 128, 260], BF16),
    "qa": ([B, 8, 128, T], BF16),
    "qf": ([B, 8, 128, T], BF16),
    "attn0T": ([B, 8, 128, T], F32),
    "fattn0T": ([B, 8, 128, T], F32),
    "xrT32": ([B, 8, 128, T], F32),
    "xrTbf": ([B, 8, 128, T], BF16),
    "selT": ([B, NT, 128, 1024], F32),
    "mrsc": ([B, NT, 128, 8], F32),
    "mbf": ([B, NT, 128, 8], BF16),
    "ffn0T": ([B, 8, 128, T], F32),
    "x1T32": ([B, 8, 128, T], F32),
    "xn1bf": ([B, 8, 128, T], BF16),
    "xpbf": ([8, 128, 4 * T], BF16),
    "attn1p": ([8, 128, 4 * T], F32),
    "ffn1T": ([B, 8, 128, T], F32),
    "wfcE": ([8, 128, 4096], BF16),
    "wprE": ([32, 128, 1024], BF16),
    "scfc": ([1, 4096], F32),
    "scpr": ([1, 1024], F32),
}

GATHERED = {
    "kattn_r": ([B, 8, 128, S], BF16),
    "kffn_g": ([B, 8, 128, S], BF16),
    "vattn_r": ([B, 4, NKC, 128, 260], BF16),
    "vffn_g": ([B, 4, NKC, 128, 260], BF16),
}

WIDE = {
    "kaw": ([B, 8, 128, S], BF16),
    "kfw": ([B, 8, 128, S], BF16),
    "vaw": ([B, 4, NKC, 128, 260], BF16),
    "vfw": ([B, 4, NKC, 128, 260], BF16),
}

WEIGHT_SHAPES = {
    "wqT": (D, D), "wkT": (D, D), "wvT": (D, D), "woT": (D, D),
    "bq": (128, 8), "bk": (128, 8), "bv_row": (1, D), "bo": (128, 8),
    "fqw": (D, D), "fkw": (D, D), "fvw": (D, D), "fow": (D, D),
    "fgw": (D, D), "fgb": (128, 8), "swbd": (D, 2),
    "n1g": (128, 8), "n1b": (128, 8), "n2g": (128, 8), "n2b": (128, 8),
    "mgw": (D, 8), "mgb": (8, 1),
    "w1": (E, D, FF), "b1c": (E, 128, 16), "lngc": (E, 128, 16),
    "lnbc": (E, 128, 16), "w2": (E, FF, D), "b2": (E, D),
    "rsc_row": (1, 8), "rsc_col": (8, 1),
    "ipT": (D, 3 * D), "ipb_qk": (128, 16), "ipb_v": (1, D),
    "opT": (D, D), "opb": (128, 8),
    "wfc": (4096, D), "wfcT": (D, 4096), "wpr": (D, 4096), "wprT": (4096, D),
    "l1uT": (16, 4096), "l1vT": (D, 16), "fbT": (16, 16), "l1gate": (1, 1),
    "c_ident": (128, 128), "c_ones_col": (128, 1), "c_ones_row": (1, 128),
    "c_ones64": (1, 64), "c_iota8": (128, 8),
    "c_diagmask": (4, 128, 512), "c_l1mask": (128, 128),
    "c_bpair": (2, 128), "valid8": (128, 8), "c_ones16": (128, 16),
}

PHASE_WEIGHTS = {
    "p1": ["c_ident", "c_ones_col", "c_ones_row"],
    "p2": ["wkT", "wvT", "fkw", "fvw", "bk", "bv_row", "c_ones_row"],
    "p2w": ["wkT", "wvT", "fkw", "fvw", "bk", "bv_row", "c_ones_row", "valid8",
            "c_ones16", "c_ident", "c_ones_col"],
    "p4af": ["woT", "bo", "c_diagmask", "c_ones64"],
    "p4ff": ["fgw", "fgb", "fow", "c_ones64"],
    "p3": ["wqT", "fqw", "bq", "swbd", "c_bpair"],
    "p4a": ["woT", "bo", "c_diagmask", "c_ones64"],
    "p4f": ["fgw", "fgb", "fow", "c_ones64"],
    "p5": ["n1g", "n1b", "c_ones_col", "c_ones_row"],
    "p7": ["mgw", "mgb", "w1", "b1c", "lngc", "lnbc", "w2", "b2", "rsc_row",
           "rsc_col", "c_iota8", "c_ident", "n2g", "n2b", "c_ones_col",
           "c_ones_row"],
    "p8": ["c_ones_col", "c_ones_row"],
    "p9": ["ipT", "ipb_qk", "ipb_v", "opT", "opb", "c_l1mask", "c_ones64",
           "c_ident"],
    "p10": ["wfc", "wfcT", "wpr", "wprT", "l1uT", "l1vT", "fbT", "l1gate",
            "c_ident", "c_ones_row"],
    "p11": ["c_ident"],
}

PHASE_SLABS = {
    "p1": ([], ["xT32", "xnbf", "xnf32"]),
    "p2": (["xnbf"], ["kattn", "kffn", "vattn", "vffn"]),
    "p2w": ([], ["kaw", "kfw", "vaw", "vfw"]),
    "p4af": (["qa", "kaw", "vaw", "kattn", "vattn"], ["attn0T"]),
    "p4ff": (["qf", "kfw", "vfw"], ["fattn0T"]),
    "p3": (["xnbf"], ["qa", "qf"]),
    "p4a": (["qa", "kattn_r", "vattn_r"], ["attn0T"]),
    "p4f": (["qf", "kffn_g", "vffn_g"], ["fattn0T"]),
    "p5": (["xnf32", "fattn0T"], ["xrT32", "xrTbf"]),
    "p7": (["xrTbf", "xrT32"], ["selT", "mrsc", "mbf", "ffn0T"]),
    "p8": (["xT32", "attn0T", "ffn0T"], ["x1T32", "xn1bf", "xpbf"]),
    "p9": (["xpbf"], ["attn1p"]),
    "p10": (["xn1bf"], ["wfcE", "wprE", "scfc", "scpr", "ffn1T"]),
    "p11": (["x1T32", "attn1p", "ffn1T"], []),
}



def p1(cx, sq, es):
    nc = cx.nc
    dr = cx.dram
    sb = lambda n, s, t: es.enter_context(nc.sbuf_tensor(n, s, t))
    ps = lambda n, s: es.enter_context(nc.psum_tensor(n, s, F32))

    xtile = sb("p1_xtile", [128, 1024], F32)
    ident = sb("p1_ident", [128, 128], F32)
    ones_col = sb("p1_onesc", [128, 1], F32)
    ones_row = sb("p1_onesr", [1, 128], F32)
    xT = sb("p1_xT", [128, 8, T], F32)
    xn32 = sb("p1_xn32", [128, 8, T], F32)
    xnbf = sb("p1_xnbf", [128, 8, T], BF16)
    P = LnScratch(es, nc, "p1")
    consts = {"ones_col": ones_col, "ones_row": ones_row}
    tp_p = ps("p1_tp", [128, 1024])

    sq.d("sync", lambda e: [
        e.dma_start(out=ident[:], in_=dr["c_ident"][:]),
        e.dma_start(out=ones_col[:], in_=dr["c_ones_col"][:]),
        e.dma_start(out=ones_row[:], in_=dr["c_ones_row"][:]),
    ])

    for b in range(B):
        for tt in range(NT):
            sq.d("sync", lambda e, b=b, tt=tt: [
                e.dma_start(out=xtile[:], in_=dr["xsl"][b, tt * 128:(tt + 1) * 128, :])])

            def tr(e):
                for c in range(8):
                    i = e.transpose(tp_p[:, c * 128:(c + 1) * 128],
                                    xtile[:, c * 128:(c + 1) * 128], ident[:])
                return i
            sq.c("tensor", tr)

            def ev(e, tt=tt):
                return e.tensor_copy(
                    xT[:, :, tt * 128:(tt + 1) * 128],
                    tp_p[:].rearrange("p (c t) -> p c t", c=8))
            sq.c("vector", ev)

        ln_fm(sq, P, consts, xT, 8, T, xn32, xnbf)

        def store(e, b=b):
            out = []
            for c in range(8):
                out.append(e.dma_start(out=dr["xT32"][b, c], in_=xT[:, c]))
                out.append(e.dma_start(out=dr["xnf32"][b, c], in_=xn32[:, c]))
                out.append(e.dma_start(out=dr["xnbf"][b, c], in_=xnbf[:, c]))
            return out
        sq.d("sync", store)



def p2(cx, sq, es):
    nc = cx.nc
    dr = cx.dram
    sb = lambda n, s, t: es.enter_context(nc.sbuf_tensor(n, s, t))
    ps = lambda n, s: es.enter_context(nc.psum_tensor(n, s, F32))

    wk = sb("p2_wk", [128, 8, 1024], BF16)
    wv = sb("p2_wv", [128, 8, 1024], BF16)
    fk = sb("p2_fk", [128, 8, 1024], BF16)
    fv = sb("p2_fv", [128, 8, 1024], BF16)
    bk = sb("p2_bk", [128, 8], F32)
    bvr = sb("p2_bvr", [1, 1024], BF16)
    onesr = sb("p2_onesr", [1, 128], BF16)
    xnb = sb("p2_xnb", [128, 8, T], BF16)
    kbuf = sb("p2_kbuf", [128, 512], BF16)
    vbuf = sb("p2_vbuf", [128, 16, 65], BF16)
    kp = ps("p2_kp", [128, 512])
    vp = ps("p2_vp", [128, 1024])

    for w, nm in [(wk, "wkT"), (wv, "wvT"), (fk, "fkw"), (fv, "fvw")]:
        cast_load(sq, w[:], dr[nm][:].rearrange("(c p) f -> p c f", p=128))
    cast_load(sq, bvr[:], dr["bv_row"][:])
    cast_load(sq, onesr[:], dr["c_ones_row"][:])
    sq.d("sync", lambda e: [e.dma_start(out=bk[:], in_=dr["bk"][:])])
    sq.c("vector", lambda e: e.memset(vbuf[:, :, 64:65], 1.0))

    for b in range(B):
        sq.d("sync", lambda e, b=b: [
            e.dma_start(out=xnb[:], in_=dr["xnbf"][b].rearrange("c p t -> p c t"))])
        for mod in ("a", "f"):
            wmat = wk if mod == "a" else fk
            kout = dr["kattn"] if mod == "a" else dr["kffn"]
            for fc in range(8):
                def mm(e, fc=fc, wmat=wmat):
                    for dc in range(8):
                        i = e.matmul(kp[:], wmat[:, dc, fc * 128:(fc + 1) * 128],
                                     xnb[:, dc], start=(dc == 0), stop=(dc == 7))
                    return i
                sq.c("tensor", mm)

                def ev(e, fc=fc, mod=mod):
                    if mod == "a":
                        return e.activation(out=kbuf[:], in_=kp[:], func=AF.Identity,
                                            bias=bk[:, fc:fc + 1])
                    return e.activation(out=kbuf[:], in_=kp[:], func=AF.Copy)
                sq.c("scalar", ev)
                sq.d("sync", lambda e, b=b, fc=fc, kout=kout: [
                    e.dma_start(out=kout[b, fc], in_=kbuf[:])])

            wv_ = wv if mod == "a" else fv
            vout = dr["vattn"] if mod == "a" else dr["vffn"]
            for tt in range(NT):
                def mmv(e, tt=tt, wv_=wv_, mod=mod):
                    for dc in range(8):
                        for half in range(2):
                            i = e.matmul(vp[:, half * 512:(half + 1) * 512],
                                         xnb[:, dc, tt * 128:(tt + 1) * 128],
                                         wv_[:, dc, half * 512:(half + 1) * 512],
                                         start=(dc == 0),
                                         stop=(dc == 7 and mod == "f"))
                    if mod == "a":
                        for half in range(2):
                            i = e.matmul(vp[:, half * 512:(half + 1) * 512],
                                         onesr[:], bvr[:, half * 512:(half + 1) * 512],
                                         start=False, stop=True)
                    return i
                sq.c("tensor", mmv)

                def evv(e):
                    return e.activation(out=vbuf[:, :, 0:64],
                                        in_=vp[:].rearrange("p (h o) -> p h o", h=16),
                                        func=AF.Copy)
                sq.c("scalar", evv)

                def stv(e, b=b, tt=tt, vout=vout):
                    return [e.dma_start(
                        out=vout[b, :, tt].rearrange("hg p o -> p hg o"),
                        in_=vbuf[:].rearrange("p (hg hl) o -> p hg (hl o)", hg=4))]
                sq.d("sync", stv)



def p2w(cx, sq, es):
    nc = cx.nc
    dr = cx.dram
    sb = lambda n, s, t: es.enter_context(nc.sbuf_tensor(n, s, t))
    ps = lambda n, s: es.enter_context(nc.psum_tensor(n, s, F32))

    wk = sb("p2w_wk", [128, 8, 1024], BF16)
    wv = sb("p2w_wv", [128, 8, 1024], BF16)
    fk = sb("p2w_fk", [128, 8, 1024], BF16)
    fv = sb("p2w_fv", [128, 8, 1024], BF16)
    bk = sb("p2w_bk", [128, 8], F32)
    bvr = sb("p2w_bvr", [1, 1024], BF16)
    onesr = sb("p2w_onesr", [1, 128], BF16)
    ident = sb("p2w_ident", [128, 128], F32)
    ones_col = sb("p2w_onesc", [128, 1], F32)
    ones_row = sb("p2w_onesrf", [1, 128], F32)
    ones16 = sb("p2w_ones16", [128, 16], F32)
    valid = sb("p2w_valid", [128, 8], F32)
    xtile = sb("p2w_xtile", [128, 1024], F32)
    xTg = sb("p2w_xT", [128, 8, T], F32)
    xnb = sb("p2w_xnb", [128, 8, T], BF16)
    kbuf = sb("p2w_kbuf", [128, 8, 512], BF16)
    kbuf2 = sb("p2w_kbuf2", [128, 8, 512], BF16)
    vbufA = sb("p2w_vbufA", [128, 2, 16, 65], BF16)
    vbufF = sb("p2w_vbufF", [128, 2, 16, 65], BF16)
    consts = {"ones_col": ones_col, "ones_row": ones_row}
    tp_p = ps("p2w_tp", [128, 512])
    kp = [ps(f"p2w_kp{i}", [128, 512]) for i in range(2)]
    vp = ps("p2w_vp", [128, 1024])
    P = LnScratch(es, nc, "p2w", repm=kp[0], repr_=kp[1])

    for w, nm in [(wk, "wkT"), (wv, "wvT"), (fk, "fkw"), (fv, "fvw")]:
        cast_load(sq, w[:], dr[nm][:].rearrange("(c p) f -> p c f", p=128))
    cast_load(sq, bvr[:], dr["bv_row"][:])
    cast_load(sq, onesr[:], dr["c_ones_row"][:])
    sq.d("sync", lambda e: [
        e.dma_start(out=bk[:], in_=dr["bk"][:]),
        e.dma_start(out=ident[:], in_=dr["c_ident"][:]),
        e.dma_start(out=ones_col[:], in_=dr["c_ones_col"][:]),
        e.dma_start(out=ones_row[:], in_=dr["c_ones_row"][:]),
        e.dma_start(out=ones16[:], in_=dr["c_ones16"][:]),
        e.dma_start(out=valid[:], in_=dr["valid8"][:]),
    ])
    sq.c("vector", lambda e: e.memset(vbufF[:, :, :, 64:65], 1.0))

    for b in range(B):
        for g in range(8):
            sq.barrier()
            m_ev = {}
            m_tr = {}
            for tt in range(NT):
                p0 = g * 512 + tt * 128
                sq.d("sync", lambda e, b=b, p0=p0: [
                    e.dma_start(out=xtile[:], in_=dr["xfull"][b, p0:p0 + 128, :])])
                m_ld = sq.mark()
                for r in range(2):
                    j = tt * 2 + r
                    tdst = tp_p if j % 2 == 0 else kp[0]

                    def tr(e, r=r, tdst=tdst):
                        for c in range(4):
                            i = e.transpose(tdst[:, c * 128:(c + 1) * 128],
                                            xtile[:, (r * 4 + c) * 128:(r * 4 + c + 1) * 128],
                                            ident[:])
                        return i
                    sq.c("tensor", tr, at=(m_ld if j >= 2 else None))
                    m_tr[j] = sq.mark()

                    def ev(e, tt=tt, r=r, tdst=tdst):
                        return e.tensor_copy(
                            xTg[:, r * 4:(r + 1) * 4, tt * 128:(tt + 1) * 128],
                            tdst[:].rearrange("p (c t) -> p c t", c=4))
                    sq.c("vector", ev, at=m_tr[j])
                    m_ev[j] = sq.mark()
            ln_fm(sq, P, consts, xTg, 8, T, None, xnb)
            m_evk = {}
            m_mm = {}
            for mod in ("a", "f"):
                wmat = wk if mod == "a" else fk
                kout = dr["kaw"] if mod == "a" else dr["kfw"]
                kb = kbuf if mod == "a" else kbuf2
                for fc in range(8):
                    j = (0 if mod == "a" else 8) + fc

                    def mm(e, fc=fc, wmat=wmat, j=j):
                        for dc in range(8):
                            i = e.matmul(kp[j % 2][:], wmat[:, dc, fc * 128:(fc + 1) * 128],
                                         xnb[:, dc], start=(dc == 0), stop=(dc == 7))
                        return i
                    sq.c("tensor", mm, at=(m_evk[j - 2] if j >= 2 else None))
                    m_mm[j] = sq.mark()

                    def evk(e, fc=fc, mod=mod, kb=kb, j=j):
                        if mod == "a":
                            return e.activation(out=kb[:, fc], in_=kp[j % 2][:],
                                                func=AF.Identity, bias=bk[:, fc:fc + 1])
                        return e.activation(out=kb[:, fc], in_=kp[j % 2][:], func=AF.Copy)
                    sq.c("scalar", evk, at=m_mm[j])
                    m_evk[j] = sq.mark()
                sq.d("sync", lambda e, b=b, g=g, kout=kout, kb=kb: [
                    e.dma_start(out=kout[b, :, :, g * 512:(g + 1) * 512]
                                .rearrange("c p t -> p c t"),
                                in_=kb[:])], defer=True, at=m_evk[j])
            m_evv = {}
            for mod in ("a", "f"):
                wv_ = wv if mod == "a" else fv
                vout = dr["vaw"] if mod == "a" else dr["vfw"]
                vb = vbufA if mod == "a" else vbufF
                for tt in range(NT):
                    j = (0 if mod == "a" else NT) + tt
                    par = j % 2

                    def mmv(e, tt=tt, wv_=wv_, mod=mod):
                        for dc in range(8):
                            for half in range(2):
                                i = e.matmul(vp[:, half * 512:(half + 1) * 512],
                                             xnb[:, dc, tt * 128:(tt + 1) * 128],
                                             wv_[:, dc, half * 512:(half + 1) * 512],
                                             start=(dc == 0),
                                             stop=(dc == 7 and mod == "f"))
                        if mod == "a":
                            for half in range(2):
                                i = e.matmul(vp[:, half * 512:(half + 1) * 512],
                                             onesr[:], bvr[:, half * 512:(half + 1) * 512],
                                             start=False, stop=True)
                        return i
                    sq.c("tensor", mmv, at=(m_evv[j - 1] if j >= 1 else None))
                    m_mmv = sq.mark()
                    if mod == "a":
                        sq.c("vector", lambda e, g=g, par=par: e.tensor_single_scalar(
                            vbufA[:, par, :, 64:65].rearrange("p h o -> p (h o)"),
                            ones16[:], valid[:, g:g + 1], ALU.mult), at=m_mmv)

                    def evv(e, vb=vb, g=g, mod=mod, par=par):
                        if mod == "a":
                            return e.activation(out=vb[:, par, :, 0:64],
                                                in_=vp[:].rearrange("p (h o) -> p h o", h=16),
                                                func=AF.Copy, scale=valid[:, g:g + 1])
                        return e.activation(out=vb[:, par, :, 0:64],
                                            in_=vp[:].rearrange("p (h o) -> p h o", h=16),
                                            func=AF.Copy)
                    sq.c("scalar", evv, at=m_mmv)
                    m_evv[j] = sq.mark()

                    def stv(e, b=b, g=g, tt=tt, vout=vout, vb=vb, par=par):
                        return [e.dma_start(
                            out=vout[b, :, g * NT + tt].rearrange("hg p o -> p hg o"),
                            in_=vb[:, par].rearrange("p (hg hl) o -> p hg (hl o)", hg=4))]
                    sq.d("sync", stv, defer=True, at=m_evv[j])
    sq.barrier()



def p3(cx, sq, es):
    nc = cx.nc
    dr = cx.dram
    sb = lambda n, s, t: es.enter_context(nc.sbuf_tensor(n, s, t))
    ps = lambda n, s: es.enter_context(nc.psum_tensor(n, s, F32))

    wq = sb("p3_wq", [128, 8, 1024], BF16)
    fq = sb("p3_fq", [128, 8, 1024], BF16)
    swb = sb("p3_swb", [128, 8, 2], BF16)
    bpair = sb("p3_bpair", [2, 128], F32)
    bq8 = sb("p3_bq8", [128, 8], F32)
    xnb = sb("p3_xnb", [128, 8, T], BF16)
    qfb = sb("p3_qfb", [128, 8, T], BF16)
    qbuf = sb("p3_qbuf", [128, 512], BF16)
    srow = sb("p3_srow", [2, 512], F32)
    qp = ps("p3_qp", [128, 512])
    sp = ps("p3_sp", [2, 512])
    rp = ps("p3_rp", [128, 512])

    cast_load(sq, wq[:], dr["wqT"][:].rearrange("(c p) f -> p c f", p=128))
    cast_load(sq, fq[:], dr["fqw"][:].rearrange("(c p) f -> p c f", p=128))
    cast_load(sq, swb[:], dr["swbd"][:].rearrange("(c p) f -> p c f", p=128))
    sq.d("sync", lambda e: [e.dma_start(out=bq8[:], in_=dr["bq"][:]),
                            e.dma_start(out=bpair[:], in_=dr["c_bpair"][:])])
    sq.c("scalar", lambda e: e.mul(bq8[:], bq8[:], 0.125))

    for b in range(B):
        sq.d("sync", lambda e, b=b: [
            e.dma_start(out=xnb[:], in_=dr["xnbf"][b].rearrange("c p t -> p c t"))])
        for fc in range(8):
            def mm(e, fc=fc):
                for dc in range(8):
                    i = e.matmul(qp[:], wq[:, dc, fc * 128:(fc + 1) * 128],
                                 xnb[:, dc], start=(dc == 0), stop=(dc == 7))
                return i
            sq.c("tensor", mm)
            sq.c("scalar", lambda e, fc=fc: e.activation(
                out=qbuf[:], in_=qp[:], func=AF.Identity, scale=0.125,
                bias=bq8[:, fc:fc + 1]))
            sq.d("sync", lambda e, b=b, fc=fc: [
                e.dma_start(out=dr["qa"][b, fc], in_=qbuf[:])])

            def mmf(e, fc=fc):
                for dc in range(8):
                    i = e.matmul(qp[:], fq[:, dc, fc * 128:(fc + 1) * 128],
                                 xnb[:, dc], start=(dc == 0), stop=(dc == 7))
                return i
            sq.c("tensor", mmf)
            sq.c("scalar", lambda e, fc=fc: e.activation(
                out=qfb[:, fc], in_=qp[:], func=AF.Copy))
        for fc in range(8):
            sq.c("tensor", lambda e, fc=fc: e.matmul(
                sp[:], swb[:, fc], qfb[:, fc], start=True, stop=True))
            sq.c("scalar", lambda e: e.activation(out=srow[:], in_=sp[:], func=AF.Sigmoid))
            sq.c("vector", lambda e: e.tensor_single_scalar(srow[:], srow[:], 0.25, ALU.mult))
            sq.c("tensor", lambda e: e.matmul(rp[:], bpair[:], srow[:], start=True, stop=True))
            sq.c("vector", lambda e, fc=fc: e.tensor_tensor(qbuf[:], qfb[:, fc], rp[:], ALU.mult))
            sq.d("sync", lambda e, b=b, fc=fc: [
                e.dma_start(out=dr["qf"][b, fc], in_=qbuf[:])])



def attention_l0(cx, sq, es, tag, qname, kname, vname, nmask, att_sb):
    raise NotImplementedError


def p4_common(cx, sq, es, tag, qname, kname, vname, nmask, post_cb, diag=None):
    nc = cx.nc
    dr = cx.dram
    sb = lambda n, s, t: es.enter_context(nc.sbuf_tensor(n, s, t))
    ps = lambda n, s: es.enter_context(nc.psum_tensor(n, s, F32))

    kline = sb(f"{tag}_kline", [128, 2, S], BF16)
    vline = sb(f"{tag}_vline", [128, NKC, 260], BF16)
    qpair = sb(f"{tag}_qpair", [128, 2, 512], BF16)
    pt = sb(f"{tag}_pt", [128, 2, 2, 512], BF16)
    masks = sb(f"{tag}_masks", [128, 4, 512], BF16)
    rrow = sb(f"{tag}_rrow", [1, 512], F32)
    repsb = sb(f"{tag}_reps", [64, 512], F32)
    ones64 = sb(f"{tag}_o64", [1, 64], F32)
    att_sb = sb(f"{tag}_att", [128, 8, T], BF16)
    sAb = [ps(f"{tag}_sA{i}", [128, 512]) for i in range(2)]
    sBb = [ps(f"{tag}_sB{i}", [128, 512]) for i in range(2)]
    av = [ps(f"{tag}_av{i}", [65, 512]) for i in range(4)]

    if diag is not None:
        klineD = sb(f"{tag}_klineD", [128, 2, 512], BF16)
        vlineD = sb(f"{tag}_vlineD", [128, NT, 260], BF16)
    sq.d("sync", lambda e: [e.dma_start(out=ones64[:], in_=dr["c_ones64"][:])])
    if nmask or diag is not None:
        sq.d("gpsimd", lambda e: [e.dma_start(
            out=masks[:], in_=dr["c_diagmask"][:].rearrange("j p t -> p j t"))])

    npos = NKC + (NT if diag is not None else 0)
    taus = [(pos, hp) for pos in range(npos) for hp in range(2)]
    ntau = len(taus)

    for b in range(B):
        for hg in range(4):
            def loads(e, b=b, hg=hg):
                out = [
                    e.dma_start(out=kline[:, 0], in_=dr[kname][b, 2 * hg]),
                    e.dma_start(out=kline[:, 1], in_=dr[kname][b, 2 * hg + 1]),
                    e.dma_start(out=vline[:],
                                in_=dr[vname][b, hg].rearrange("k p o -> p k o")),
                    e.dma_start(out=qpair[:, 0], in_=dr[qname][b, 2 * hg]),
                    e.dma_start(out=qpair[:, 1], in_=dr[qname][b, 2 * hg + 1]),
                ]
                if diag is not None:
                    out.append(e.dma_start(out=klineD[:, 0], in_=dr[diag[0]][b, 2 * hg]))
                    out.append(e.dma_start(out=klineD[:, 1], in_=dr[diag[0]][b, 2 * hg + 1]))
                    out.append(e.dma_start(
                        out=vlineD[:], in_=dr[diag[1]][b, hg].rearrange("k p o -> p k o")))
                return out
            sq.d("sync", loads)

            def mk_smm(tau):
                pos, hp = taus[tau]
                dpos = pos - NKC
                kl = kline if dpos < 0 else klineD
                kidx = pos if dpos < 0 else dpos
                par = tau % 2

                def smm(e, kl=kl, kidx=kidx, hp=hp, par=par):
                    e.matmul(sAb[par][:], kl[0:64, hp, kidx * 128:(kidx + 1) * 128],
                             qpair[0:64, hp], start=True, stop=True,
                             tile_position=(0, 0))
                    return e.matmul(sBb[par][:],
                                    kl[64:128, hp, kidx * 128:(kidx + 1) * 128],
                                    qpair[64:128, hp], start=True, stop=True,
                                    tile_position=(64, 0))
                return smm

            m_ex = {}
            m_smm = {}
            sq.c("tensor", mk_smm(0))
            m_smm[0] = sq.mark()
            for tau in range(ntau):
                pos, hp = taus[tau]
                dpos = pos - NKC
                vl = vline if dpos < 0 else vlineD
                kidx = pos if dpos < 0 else dpos
                masked = (pos < nmask) or (dpos >= 0)
                midx = pos if dpos < 0 else dpos
                par = tau % 2

                def ex(e, par=par):
                    e.activation(out=pt[:, par, 0], in_=sAb[par][:], func=AF.Exp)
                    return e.activation(out=pt[:, par, 1], in_=sBb[par][:],
                                        func=AF.Exp)
                sq.c("scalar", ex, at=(m_smm[tau] if tau >= 2 else None))
                m_ex[tau] = sq.mark()
                dep = m_ex[tau]
                if masked:
                    def mk(e, midx=midx, par=par):
                        e.tensor_tensor(pt[:, par, 0], pt[:, par, 0],
                                        masks[:, midx], ALU.mult)
                        return e.tensor_tensor(pt[:, par, 1], pt[:, par, 1],
                                               masks[:, midx], ALU.mult)
                    sq.c("gpsimd", mk, at=m_ex[tau])
                    dep = sq.mark()
                if tau + 1 < ntau:
                    sq.c("tensor", mk_smm(tau + 1),
                         at=(m_ex[tau - 1] if tau >= 1 else None))
                    m_smm[tau + 1] = sq.mark()

                def avm(e, vl=vl, kidx=kidx, pos=pos, hp=hp, par=par):
                    for j in range(2):
                        hl = 2 * hp + j
                        i = e.matmul(av[hl][:],
                                     vl[:, kidx, hl * 65:hl * 65 + 65],
                                     pt[:, par, j], start=(pos == 0),
                                     stop=(pos == npos - 1))
                    return i
                sq.c("tensor", avm, at=(dep if tau >= 2 else None))
            for hl in range(4):
                sq.c("vector", lambda e, hl=hl: e.reciprocal(rrow[:], av[hl][64:65, :]))
                sq.c("tensor", lambda e: e.matmul(sBb[1][0:64, :], ones64[:], rrow[:],
                                                  start=True, stop=True))
                sq.c("scalar", lambda e: e.activation(out=repsb[:], in_=sBb[1][0:64, :],
                                                      func=AF.Copy))

                def div(e, hl=hl, hg=hg):
                    c = 2 * hg + hl // 2
                    r0 = (hl % 2) * 64
                    return e.tensor_tensor(att_sb[r0:r0 + 64, c], av[hl][0:64, :],
                                           repsb[:], ALU.mult)
                sq.c("vector", div)
        post_cb(b, att_sb, sAb[0])


def p4a(cx, sq, es):
    nc = cx.nc
    dr = cx.dram
    sb = lambda n, s, t: es.enter_context(nc.sbuf_tensor(n, s, t))
    ps = lambda n, s: es.enter_context(nc.psum_tensor(n, s, F32))
    wo = sb("p4a_wo", [128, 8, 1024], BF16)
    bo = sb("p4a_bo", [128, 8], F32)
    obuf = sb("p4a_obuf", [128, 512], F32)
    cast_load(sq, wo[:], dr["woT"][:].rearrange("(c p) f -> p c f", p=128))
    sq.d("sync", lambda e: [e.dma_start(out=bo[:], in_=dr["bo"][:])])

    def post(b, att_sb, scr):
        for fc in range(8):
            def mm(e, fc=fc, att_sb=att_sb, scr=scr):
                for dc in range(8):
                    i = e.matmul(scr[:], wo[:, dc, fc * 128:(fc + 1) * 128],
                                 att_sb[:, dc], start=(dc == 0), stop=(dc == 7))
                return i
            sq.c("tensor", mm)
            sq.c("scalar", lambda e, fc=fc, scr=scr: e.activation(
                out=obuf[:], in_=scr[:], func=AF.Identity, bias=bo[:, fc:fc + 1]))
            sq.d("sync", lambda e, b=b, fc=fc: [
                e.dma_start(out=dr["attn0T"][b, fc], in_=obuf[:])])
    p4_common(cx, sq, es, "p4a", "qa", "kattn_r", "vattn_r", 4, post)


def p4af(cx, sq, es):
    nc = cx.nc
    dr = cx.dram
    sb = lambda n, s, t: es.enter_context(nc.sbuf_tensor(n, s, t))
    ps = lambda n, s: es.enter_context(nc.psum_tensor(n, s, F32))
    wo = sb("p4a_wo", [128, 8, 1024], BF16)
    bo = sb("p4a_bo", [128, 8], F32)
    obuf = sb("p4a_obuf", [128, 512], F32)
    cast_load(sq, wo[:], dr["woT"][:].rearrange("(c p) f -> p c f", p=128))
    sq.d("sync", lambda e: [e.dma_start(out=bo[:], in_=dr["bo"][:])])

    def post(b, att_sb, scr):
        for fc in range(8):
            def mm(e, fc=fc, att_sb=att_sb, scr=scr):
                for dc in range(8):
                    i = e.matmul(scr[:], wo[:, dc, fc * 128:(fc + 1) * 128],
                                 att_sb[:, dc], start=(dc == 0), stop=(dc == 7))
                return i
            sq.c("tensor", mm)
            sq.c("scalar", lambda e, fc=fc, scr=scr: e.activation(
                out=obuf[:], in_=scr[:], func=AF.Identity, bias=bo[:, fc:fc + 1]))
            sq.d("sync", lambda e, b=b, fc=fc: [
                e.dma_start(out=dr["attn0T"][b, fc], in_=obuf[:])])
    p4_common(cx, sq, es, "p4a", "qa", "kaw", "vaw", 0, post,
              diag=("kattn", "vattn"))


def p4f(cx, sq, es):
    nc = cx.nc
    dr = cx.dram
    sb = lambda n, s, t: es.enter_context(nc.sbuf_tensor(n, s, t))
    ps = lambda n, s: es.enter_context(nc.psum_tensor(n, s, F32))
    gw = sb("p4f_gw", [128, 8, 1024], BF16)
    ow = sb("p4f_ow", [128, 8, 1024], BF16)
    gb = sb("p4f_gb", [128, 8], F32)
    gt = sb("p4f_gt", [128, 8, T], BF16)
    go = sb("p4f_go", [128, 8, T], BF16)
    obuf = sb("p4f_obuf", [128, 512], F32)
    cast_load(sq, gw[:], dr["fgw"][:].rearrange("(c p) f -> p c f", p=128))
    cast_load(sq, ow[:], dr["fow"][:].rearrange("(c p) f -> p c f", p=128))
    sq.d("sync", lambda e: [e.dma_start(out=gb[:], in_=dr["fgb"][:])])

    def post(b, att_sb, scr):
        for fc in range(8):
            def mm(e, fc=fc, att_sb=att_sb, scr=scr):
                for dc in range(8):
                    i = e.matmul(scr[:], gw[:, dc, fc * 128:(fc + 1) * 128],
                                 att_sb[:, dc], start=(dc == 0), stop=(dc == 7))
                return i
            sq.c("tensor", mm)
            sq.c("scalar", lambda e, fc=fc, scr=scr: e.activation(
                out=gt[:, fc], in_=scr[:], func=AF.Sigmoid, bias=gb[:, fc:fc + 1]))

        def gmul(e, att_sb=att_sb):
            for fc in range(8):
                i = e.tensor_tensor(go[:, fc], att_sb[:, fc], gt[:, fc], ALU.mult)
            return i
        sq.c("vector", gmul)
        for fc in range(8):
            def mm2(e, fc=fc, scr=scr):
                for dc in range(8):
                    i = e.matmul(scr[:], ow[:, dc, fc * 128:(fc + 1) * 128],
                                 go[:, dc], start=(dc == 0), stop=(dc == 7))
                return i
            sq.c("tensor", mm2)
            sq.c("scalar", lambda e, fc=fc, scr=scr: e.activation(
                out=obuf[:], in_=scr[:], func=AF.Copy))
            sq.d("sync", lambda e, b=b, fc=fc: [
                e.dma_start(out=dr["fattn0T"][b, fc], in_=obuf[:])])
    p4_common(cx, sq, es, "p4f", "qf", "kffn_g", "vffn_g", 0, post)


def p4ff(cx, sq, es):
    nc = cx.nc
    dr = cx.dram
    sb = lambda n, s, t: es.enter_context(nc.sbuf_tensor(n, s, t))
    ps = lambda n, s: es.enter_context(nc.psum_tensor(n, s, F32))
    gw = sb("p4f_gw", [128, 8, 1024], BF16)
    ow = sb("p4f_ow", [128, 8, 1024], BF16)
    gb = sb("p4f_gb", [128, 8], F32)
    gt = sb("p4f_gt", [128, 8, T], BF16)
    go = sb("p4f_go", [128, 8, T], BF16)
    obuf = sb("p4f_obuf", [128, 512], F32)
    cast_load(sq, gw[:], dr["fgw"][:].rearrange("(c p) f -> p c f", p=128))
    cast_load(sq, ow[:], dr["fow"][:].rearrange("(c p) f -> p c f", p=128))
    sq.d("sync", lambda e: [e.dma_start(out=gb[:], in_=dr["fgb"][:])])

    def post(b, att_sb, scr):
        for fc in range(8):
            def mm(e, fc=fc, att_sb=att_sb, scr=scr):
                for dc in range(8):
                    i = e.matmul(scr[:], gw[:, dc, fc * 128:(fc + 1) * 128],
                                 att_sb[:, dc], start=(dc == 0), stop=(dc == 7))
                return i
            sq.c("tensor", mm)
            sq.c("scalar", lambda e, fc=fc, scr=scr: e.activation(
                out=gt[:, fc], in_=scr[:], func=AF.Sigmoid, bias=gb[:, fc:fc + 1]))

        def gmul(e, att_sb=att_sb):
            for fc in range(8):
                i = e.tensor_tensor(go[:, fc], att_sb[:, fc], gt[:, fc], ALU.mult)
            return i
        sq.c("vector", gmul)
        for fc in range(8):
            def mm2(e, fc=fc, scr=scr):
                for dc in range(8):
                    i = e.matmul(scr[:], ow[:, dc, fc * 128:(fc + 1) * 128],
                                 go[:, dc], start=(dc == 0), stop=(dc == 7))
                return i
            sq.c("tensor", mm2)
            sq.c("scalar", lambda e, fc=fc, scr=scr: e.activation(
                out=obuf[:], in_=scr[:], func=AF.Copy))
            sq.d("sync", lambda e, b=b, fc=fc: [
                e.dma_start(out=dr["fattn0T"][b, fc], in_=obuf[:])])
    p4_common(cx, sq, es, "p4f", "qf", "kfw", "vfw", 0, post)



def p5(cx, sq, es):
    nc = cx.nc
    dr = cx.dram
    sb = lambda n, s, t: es.enter_context(nc.sbuf_tensor(n, s, t))

    z = sb("p5_z", [128, 8, T], F32)
    fat = sb("p5_fat", [128, 8, T], F32)
    xr32 = sb("p5_xr32", [128, 8, T], F32)
    xrbf = sb("p5_xrbf", [128, 8, T], BF16)
    g = sb("p5_g", [128, 8], F32)
    bb = sb("p5_b", [128, 8], F32)
    ones_col = sb("p5_onesc", [128, 1], F32)
    ones_row = sb("p5_onesr", [1, 128], F32)
    P = LnScratch(es, nc, "p5")
    consts = {"ones_col": ones_col, "ones_row": ones_row}

    sq.d("sync", lambda e: [
        e.dma_start(out=g[:], in_=dr["n1g"][:]),
        e.dma_start(out=bb[:], in_=dr["n1b"][:]),
        e.dma_start(out=ones_col[:], in_=dr["c_ones_col"][:]),
        e.dma_start(out=ones_row[:], in_=dr["c_ones_row"][:]),
    ])
    for b in range(B):
        sq.d("sync", lambda e, b=b: [
            e.dma_start(out=z[:], in_=dr["xnf32"][b].rearrange("c p t -> p c t")),
            e.dma_start(out=fat[:], in_=dr["fattn0T"][b].rearrange("c p t -> p c t")),
        ])

        def add(e):
            for c in range(8):
                i = e.tensor_tensor(z[:, c], z[:, c], fat[:, c], ALU.add)
            return i
        sq.c("vector", add)
        ln_fm(sq, P, consts, z, 8, T, xr32, xrbf, gamma=g, beta=bb)
        sq.d("sync", lambda e, b=b: [
            e.dma_start(out=dr["xrT32"][b].rearrange("c p t -> p c t"), in_=xr32[:]),
            e.dma_start(out=dr["xrTbf"][b].rearrange("c p t -> p c t"), in_=xrbf[:]),
        ])



def p7(cx, sq, es):
    nc = cx.nc
    dr = cx.dram
    sb = lambda n, s, t: es.enter_context(nc.sbuf_tensor(n, s, t))

    iota = sb("p7_iota", [128, 8], F32)
    ident = sb("p7_ident", [128, 128], F32)
    identb = sb("p7_identb", [128, 128], BF16)
    ones_col = sb("p7_onesc", [128, 1], F32)
    onescb = sb("p7_onescb", [128, 1], BF16)
    ones_row = sb("p7_onesr", [1, 128], F32)
    rsc_row = sb("p7_rscrow", [1, 8], F32)
    rsc_col = sb("p7_rsccol", [8, 1], F32)
    b2sb = sb("p7_b2", [8, 1024], F32)
    b2rsc = sb("p7_b2rsc", [8, 1024], BF16)
    mrsc_sb = sb("p7_mrsc", [128, B * NT, 8], F32)
    mbf_sb = sb("p7_mbf", [128, B * NT, 8], BF16)
    rscrep = sb("p7_rscrep", [128, 8], F32)
    consts = {"ones_col": ones_col, "ones_row": ones_row}

    cast_load(sq, identb[:], dr["c_ident"][:])
    cast_load(sq, onescb[:], dr["c_ones_col"][:])
    sq.d("sync", lambda e: [
        e.dma_start(out=iota[:], in_=dr["c_iota8"][:]),
        e.dma_start(out=ident[:], in_=dr["c_ident"][:]),
        e.dma_start(out=ones_col[:], in_=dr["c_ones_col"][:]),
        e.dma_start(out=ones_row[:], in_=dr["c_ones_row"][:]),
        e.dma_start(out=rsc_row[:], in_=dr["rsc_row"][:]),
        e.dma_start(out=rsc_col[:], in_=dr["rsc_col"][:]),
        e.dma_start(out=b2sb[:], in_=dr["b2"][:]),
    ])
    sq.c("vector", lambda e: e.tensor_single_scalar(b2rsc[:], b2sb[:], rsc_col[:],
                                                    ALU.mult))

    es1 = ExitStack()
    sb1 = lambda n, s, t: es1.enter_context(nc.sbuf_tensor(n, s, t))
    ps1 = lambda n, s: es1.enter_context(nc.psum_tensor(n, s, F32))
    mgw = sb1("p7_mgw", [128, 8, 8], F32)
    xr32r = sb1("p7_xr32r", [128, 8, T], F32)
    mgb = sb1("p7_mgb", [8, 1], F32)
    lgs = sb1("p7_lgs", [8, 512], F32)
    ltm = sb1("p7_ltm", [128, 8], F32)
    mx8 = sb1("p7_mx8", [128, 8], F32)
    ge2 = sb1("p7_ge2", [128, 8], F32)
    chosen = sb1("p7_chosen", [128, 1], F32)
    lgp = ps1("p7_lgp", [8, 512])
    ltp = ps1("p7_ltp", [128, 32])

    sq.d("sync", lambda e: [
        e.dma_start(out=mgw[:], in_=dr["mgw"][:].rearrange("(c p) f -> p c f", p=128)),
        e.dma_start(out=mgb[:], in_=dr["mgb"][:])])
    sq.c("tensor", lambda e: e.matmul(ltp[:, 0:8], ones_row[:], rsc_row[:],
                                      start=True, stop=True))
    sq.c("vector", lambda e: e.tensor_copy(rscrep[:], ltp[:, 0:8]))

    for b in range(B):
        sq.d("sync", lambda e, b=b: [
            e.dma_start(out=xr32r[:], in_=dr["xrT32"][b].rearrange("c p t -> p c t"))])

        def lgmm(e):
            for dc in range(8):
                i = e.matmul(lgp[:], mgw[:, dc], xr32r[:, dc], start=(dc == 0),
                             stop=(dc == 7))
            return i
        sq.c("tensor", lgmm)
        sq.c("scalar", lambda e: e.activation(out=lgs[:], in_=lgp[:], func=AF.Identity,
                                              bias=mgb[:]))

        def ltr(e):
            for tt in range(NT):
                i = e.transpose(ltp[:, tt * 8:(tt + 1) * 8],
                                lgs[:, tt * 128:(tt + 1) * 128], ident[0:8, 0:8])
            return i
        sq.c("tensor", ltr)
        for tt in range(NT):
            sq.c("vector", lambda e, tt=tt: e.tensor_copy(
                ltm[:], ltp[:, tt * 8:(tt + 1) * 8]))
            sq.c("vector", lambda e: e.max(out=mx8[:], in_=ltm[:]))
            sq.c("vector", lambda e: e.tensor_single_scalar(
                ge2[:], ltm[:], mx8[:, 1:2], ALU.is_ge))
            sq.c("vector", lambda e: e.tensor_tensor(ge2[:], ge2[:], iota[:],
                                                     ALU.mult))
            sq.c("vector", lambda e: e.reduce_max(chosen[:], ge2[:], axis=AX.X))
            sq.c("vector", lambda e, b=b, tt=tt: e.tensor_single_scalar(
                mbf_sb[:, b * NT + tt], iota[:], chosen[:], ALU.is_equal))
            sq.c("vector", lambda e, b=b, tt=tt: e.tensor_single_scalar(
                mrsc_sb[:, b * NT + tt], iota[:], chosen[:], ALU.is_equal))
            sq.c("vector", lambda e, b=b, tt=tt: e.tensor_tensor(
                mrsc_sb[:, b * NT + tt], mrsc_sb[:, b * NT + tt], rscrep[:],
                ALU.mult))
        sq.d("sync", lambda e, b=b: [
            e.dma_start(out=dr["mrsc"][b].rearrange("n p f -> p n f"),
                        in_=mrsc_sb[:, b * NT:(b + 1) * NT]),
            e.dma_start(out=dr["mbf"][b].rearrange("n p f -> p n f"),
                        in_=mbf_sb[:, b * NT:(b + 1) * NT]),
        ])
    es1.close()

    es2 = ExitStack()
    sb2 = lambda n, s, t: es2.enter_context(nc.sbuf_tensor(n, s, t))
    ps2 = lambda n, s: es2.enter_context(nc.psum_tensor(n, s, F32))
    w1sb = sb2("p7_w1", [128, 8, FF], BF16)
    w2sb = sb2("p7_w2", [128, 16, 1024], BF16)
    b1 = sb2("p7_b1", [128, 16], F32)
    lng = sb2("p7_lng", [128, 16], F32)
    lnb = sb2("p7_lnb", [128, 16], F32)
    hsb = sb2("p7_h", [128, 16, 512], BF16)
    hsq = sb2("p7_hsq", [128, 16, 512], BF16)
    gsb = sb2("p7_g", [128, 16, 512], BF16)
    rows = sb2("p7_erows", [1, 4, 512], F32)
    msb = sb2("p7_m", [128, 512], F32)
    rsb = sb2("p7_r", [128, 512], F32)
    ysel = sb2("p7_ysel", [128, 2, 1024], F32)
    s1p = ps2("p7_es1", [1, 512])
    s2p = ps2("p7_es2", [1, 512])
    hp = [ps2(f"p7_hp{i}", [128, 512]) for i in range(2)]
    yp = [ps2(f"p7_yp{i}", [128, 1024]) for i in range(2)]
    repm = hp[0]
    repr_ = hp[1]

    xrb4 = sb2("p7_xrb4", [128, B, 8, T], BF16)
    for b in range(B):
        sq.d("sync", lambda e, b=b: [
            e.dma_start(out=xrb4[:, b], in_=dr["xrTbf"][b].rearrange("c p t -> p c t"))])

    m_bnd = None
    for e_i in range(E):
        sq.d("gpsimd", lambda e, e_i=e_i: [
            e.dma_start(out=w1sb[:], in_=dr["w1"][e_i].rearrange("(c p) f -> p c f", p=128)),
            e.dma_start(out=w2sb[:], in_=dr["w2"][e_i].rearrange("(c p) f -> p c f", p=128)),
        ])
        sq.d("sync", lambda e, e_i=e_i: [
            e.dma_start(out=b1[:], in_=dr["b1c"][e_i]),
            e.dma_start(out=lng[:], in_=dr["lngc"][e_i]),
            e.dma_start(out=lnb[:], in_=dr["lnbc"][e_i]),
        ])
        m_bnd = None
        for b in range(B):
            m_evh = {}
            m_mmh = {}
            for fc in range(16):
                def mmh(e, fc=fc, b=b):
                    for dc in range(8):
                        i = e.matmul(hp[fc % 2][:], w1sb[:, dc, fc * 128:(fc + 1) * 128],
                                     xrb4[:, b, dc], start=(dc == 0), stop=(dc == 7))
                    return i
                sq.c("tensor", mmh,
                     at=(m_evh[fc - 2] if fc >= 2 else m_bnd))
                m_mmh[fc] = sq.mark()

                def evh(e, fc=fc):
                    e.activation(out=hsb[:, fc], in_=hp[fc % 2][:], func=AF.Identity,
                                 bias=b1[:, fc:fc + 1])
                    return e.activation(out=hsq[:, fc], in_=hp[fc % 2][:], func=AF.Square,
                                        bias=b1[:, fc:fc + 1])
                sq.c("scalar", evh, at=m_mmh[fc])
                m_evh[fc] = sq.mark()

            def stats(e):
                for fc in range(16):
                    e.matmul(s1p[:], onescb[:], hsb[:, fc], start=(fc == 0), stop=(fc == 15))
                for fc in range(16):
                    i = e.matmul(s2p[:], onescb[:], hsq[:, fc], start=(fc == 0),
                                 stop=(fc == 15))
                return i
            sq.c("tensor", stats)

            sq.c("vector", lambda e: e.tensor_single_scalar(rows[:, 0], s1p[:],
                                                            1.0 / FF, ALU.mult))
            sq.c("vector", lambda e: e.tensor_tensor(rows[:, 1], rows[:, 0],
                                                     rows[:, 0], ALU.mult))
            sq.c("vector", lambda e: e.scalar_tensor_tensor(
                out=rows[:, 2], in0=s2p[:], scalar=1.0 / FF, in1=rows[:, 1],
                op0=ALU.mult, op1=ALU.subtract))
            sq.c("vector", lambda e: e.tensor_single_scalar(rows[:, 2], rows[:, 2],
                                                            EPS, ALU.add))
            sq.c("scalar", lambda e: e.activation(out=rows[:, 1], in_=rows[:, 2],
                                                  func=AF.Ln))
            sq.c("scalar", lambda e: e.activation(out=rows[:, 3], in_=rows[:, 1],
                                                  func=AF.Exp, scale=-0.5))

            def reps(e):
                e.matmul(repm[:], ones_row[:], rows[:, 0], start=True, stop=True)
                return e.matmul(repr_[:], ones_row[:], rows[:, 3], start=True, stop=True)
            sq.c("tensor", reps)

            def evreps(e):
                e.tensor_copy(msb[:], repm[:])
                return e.tensor_copy(rsb[:], repr_[:])
            sq.c("vector", evreps)
            m_bnd = sq.mark()

            def normh1(e):
                for fc in range(16):
                    i = e.tensor_tensor(hsb[:, fc], hsb[:, fc], msb[:], ALU.subtract)
                return i
            sq.c("vector", normh1)

            def normh2(e):
                for fc in range(16):
                    i = e.tensor_tensor(hsb[:, fc], hsb[:, fc], rsb[:], ALU.mult)
                return i
            sq.c("vector", normh2)

            def gelu(e):
                for fc in range(16):
                    i = e.activation(out=gsb[:, fc], in_=hsb[:, fc], func=AF.Gelu,
                                     scale=lng[:, fc:fc + 1], bias=lnb[:, fc:fc + 1])
                return i
            sq.c("scalar", gelu)
            m_gelu = sq.mark()

            m_sel = {}
            for tt in range(NT):
                def mmy(e, tt=tt):
                    for fc in range(16):
                        for half in range(2):
                            i = e.matmul(yp[tt % 2][:, half * 512:(half + 1) * 512],
                                         gsb[:, fc, tt * 128:(tt + 1) * 128],
                                         w2sb[:, fc, half * 512:(half + 1) * 512],
                                         start=(fc == 0), stop=(fc == 15))
                    return i
                sq.c("tensor", mmy,
                     at=(m_sel[tt - 2] if tt >= 2 else m_gelu))
                m_mmy = sq.mark()

                def sel(e, b=b, tt=tt, e_i=e_i):
                    return e.tensor_single_scalar(
                        ysel[:, tt % 2], yp[tt % 2][:],
                        mrsc_sb[:, b * NT + tt, e_i:e_i + 1], ALU.mult)
                sq.c("vector", sel, at=m_mmy)
                m_sel[tt] = sq.mark()
                sq.d("gpsimd", lambda e, b=b, tt=tt, e_i=e_i: [
                    e.dma_start(out=dr["selT"][b, tt], in_=ysel[:, tt % 2],
                                accum_op=(ALU.bypass if e_i == 0 else ALU.add))],
                     at=m_sel[tt])
    es2.close()

    es3 = ExitStack()
    sb3 = lambda n, s, t: es3.enter_context(nc.sbuf_tensor(n, s, t))
    ps3 = lambda n, s: es3.enter_context(nc.psum_tensor(n, s, F32))
    xr32 = sb3("p7_xr32", [128, 8, T], F32)
    selsb = sb3("p7_sel", [128, 1024], F32)
    selfm = sb3("p7_selfm", [128, 8, T], F32)
    ffout = sb3("p7_ffout", [128, 8, T], F32)
    ohot = sb3("p7_ohot", [8, 128], BF16)
    n2g = sb3("p7_n2g", [128, 8], F32)
    n2b = sb3("p7_n2b", [128, 8], F32)
    P = LnScratch(es3, nc, "p7")
    tp2 = es3.enter_context(nc.psum_tensor("p7_tp2", [8, 128], BF16))
    yp2 = ps3("p7_yp2", [128, 1024])
    trp = ps3("p7_trp", [128, 512])

    sq.d("sync", lambda e: [
        e.dma_start(out=n2g[:], in_=dr["n2g"][:]),
        e.dma_start(out=n2b[:], in_=dr["n2b"][:]),
    ])
    for b in range(B):
        sq.d("sync", lambda e, b=b: [
            e.dma_start(out=xr32[:], in_=dr["xrT32"][b].rearrange("c p t -> p c t"))])
        for tt in range(NT):
            sq.d("sync", lambda e, b=b, tt=tt: [
                e.dma_start(out=selsb[:], in_=dr["selT"][b, tt])])

            def trm(e, b=b, tt=tt):
                return e.transpose(tp2[:], mbf_sb[:, b * NT + tt], identb[:])
            sq.c("tensor", trm)
            sq.c("vector", lambda e: e.tensor_copy(ohot[:], tp2[:]))

            def term(e):
                for half in range(2):
                    i = e.matmul(yp2[:, half * 512:(half + 1) * 512], ohot[:],
                                 b2rsc[:, half * 512:(half + 1) * 512],
                                 start=True, stop=True)
                return i
            sq.c("tensor", term)
            sq.c("vector", lambda e: e.tensor_tensor(selsb[:], selsb[:], yp2[:], ALU.add))

            for grp in range(2):
                def trsel(e, grp=grp):
                    for dc in range(4):
                        c = grp * 4 + dc
                        i = e.transpose(trp[:, dc * 128:(dc + 1) * 128],
                                        selsb[:, c * 128:(c + 1) * 128], ident[:])
                    return i
                sq.c("tensor", trsel)

                def evsel(e, grp=grp, tt=tt):
                    return e.tensor_copy(
                        selfm[:, grp * 4:(grp + 1) * 4, tt * 128:(tt + 1) * 128],
                        trp[:].rearrange("p (c t) -> p c t", c=4))
                sq.c("vector", evsel)

        def addxr(e):
            for c in range(8):
                i = e.scalar_tensor_tensor(out=selfm[:, c], in0=xr32[:, c],
                                           scalar=2.0, in1=selfm[:, c],
                                           op0=ALU.mult, op1=ALU.add)
            return i
        sq.c("vector", addxr)
        ln_fm(sq, P, consts, selfm, 8, T, ffout, None, gamma=n2g, beta=n2b)
        sq.d("sync", lambda e, b=b: [
            e.dma_start(out=dr["ffn0T"][b].rearrange("c p t -> p c t"), in_=ffout[:])])
    es3.close()



def p8(cx, sq, es):
    nc = cx.nc
    dr = cx.dram
    sb = lambda n, s, t: es.enter_context(nc.sbuf_tensor(n, s, t))

    x1 = sb("p8_x1", [128, 8, T], F32)
    t2 = sb("p8_t2", [128, 8, T], F32)
    xn1 = sb("p8_xn1", [128, 8, B, T], BF16)
    ones_col = sb("p8_onesc", [128, 1], F32)
    ones_row = sb("p8_onesr", [1, 128], F32)
    P = LnScratch(es, nc, "p8")
    consts = {"ones_col": ones_col, "ones_row": ones_row}
    xp = sb("p8_xp", [128, 4 * T], BF16)

    sq.d("sync", lambda e: [
        e.dma_start(out=ones_col[:], in_=dr["c_ones_col"][:]),
        e.dma_start(out=ones_row[:], in_=dr["c_ones_row"][:]),
    ])
    for b in range(B):
        sq.d("sync", lambda e, b=b: [
            e.dma_start(out=x1[:], in_=dr["xT32"][b].rearrange("c p t -> p c t")),
            e.dma_start(out=t2[:], in_=dr["attn0T"][b].rearrange("c p t -> p c t")),
        ])

        def add1(e):
            for c in range(8):
                i = e.tensor_tensor(x1[:, c], x1[:, c], t2[:, c], ALU.add)
            return i
        sq.c("vector", add1)
        sq.d("sync", lambda e, b=b: [
            e.dma_start(out=t2[:], in_=dr["ffn0T"][b].rearrange("c p t -> p c t"))])

        def add2(e):
            for c in range(8):
                i = e.tensor_tensor(x1[:, c], x1[:, c], t2[:, c], ALU.add)
            return i
        sq.c("vector", add2)
        sq.d("sync", lambda e, b=b: [
            e.dma_start(out=dr["x1T32"][b].rearrange("c p t -> p c t"), in_=x1[:])])
        ln_fm(sq, P, consts, x1, 8, T, None,
              xn1[:, :, b].rearrange("p c t -> p c t"))
        sq.d("sync", lambda e, b=b: [
            e.dma_start(out=dr["xn1bf"][b].rearrange("c p t -> p c t"),
                        in_=xn1[:, :, b])])
    for c in range(8):
        def pc(e, c=c):
            return e.tensor_copy(
                xp[:].rearrange("p (t i) -> p t i", i=4),
                xn1[:, c].rearrange("p i t -> p t i"))
        sq.c("vector", pc)
        sq.d("sync", lambda e, c=c: [e.dma_start(out=dr["xpbf"][c], in_=xp[:])])



def p9(cx, sq, es):
    nc = cx.nc
    dr = cx.dram
    sb = lambda n, s, t: es.enter_context(nc.sbuf_tensor(n, s, t))

    qk = sb("p9_qk", [128, 16, 4 * T], BF16)
    vsb = sb("p9_vsb", [128, 16, 16, 65], BF16)
    ones64 = sb("p9_o64", [1, 64], F32)
    l1m = sb("p9_l1m", [128, 128], BF16)
    bqk = sb("p9_bqk", [128, 16], F32)
    bqk8 = sb("p9_bqk8", [128, 16], F32)

    cast_load(sq, l1m[:], dr["c_l1mask"][:])
    sq.d("sync", lambda e: [
        e.dma_start(out=bqk[:], in_=dr["ipb_qk"][:]),
        e.dma_start(out=ones64[:], in_=dr["c_ones64"][:]),
    ])
    sq.c("scalar", lambda e: e.mul(bqk8[:, 0:8], bqk[:, 0:8], 0.125))
    sq.c("scalar", lambda e: e.copy(bqk8[:, 8:16], bqk[:, 8:16]))
    sq.c("vector", lambda e: e.memset(vsb[:, :, :, 64:65], 1.0))

    es1 = ExitStack()
    sb1 = lambda n, s, t: es1.enter_context(nc.sbuf_tensor(n, s, t))
    ps1 = lambda n, s: es1.enter_context(nc.psum_tensor(n, s, F32))
    ip = sb1("p9_ip", [128, 8, 3 * D], BF16)
    xpb = sb1("p9_xpb", [128, 8, 4 * T], BF16)
    bvr = sb1("p9_bvr", [1, D], BF16)
    onesr = sb1("p9_onesr", [1, 128], BF16)
    qp = ps1("p9_qp", [128, 512])
    vp = ps1("p9_vp", [128, 1024])

    cast_load(sq, ip[:], dr["ipT"][:].rearrange("(c p) f -> p c f", p=128))
    cast_load(sq, bvr[:], dr["ipb_v"][:])
    cast_load(sq, onesr[:], dr["c_ones_row"][:])
    sq.d("sync", lambda e: [
        e.dma_start(out=xpb[:], in_=dr["xpbf"][:].rearrange("c p t -> p c t"))])

    for fc in range(16):
        for g4 in range(4):
            def mm(e, fc=fc, g4=g4):
                for dc in range(8):
                    i = e.matmul(qp[:], ip[:, dc, fc * 128:(fc + 1) * 128],
                                 xpb[:, dc, g4 * 512:(g4 + 1) * 512],
                                 start=(dc == 0), stop=(dc == 7))
                return i
            sq.c("tensor", mm)

            def ev(e, fc=fc, g4=g4):
                sc = 0.125 if fc < 8 else 1.0
                return e.activation(out=qk[:, fc, g4 * 512:(g4 + 1) * 512], in_=qp[:],
                                    func=AF.Identity, scale=sc, bias=bqk8[:, fc:fc + 1])
            sq.c("scalar", ev)
    for tt in range(16):
        def mmv(e, tt=tt):
            for dc in range(8):
                for half in range(2):
                    e.matmul(vp[:, half * 512:(half + 1) * 512],
                             xpb[:, dc, tt * 128:(tt + 1) * 128],
                             ip[:, dc, 2048 + half * 512:2048 + (half + 1) * 512],
                             start=(dc == 0), stop=False)
            for half in range(2):
                i = e.matmul(vp[:, half * 512:(half + 1) * 512], onesr[:],
                             bvr[:, half * 512:(half + 1) * 512], start=False, stop=True)
            return i
        sq.c("tensor", mmv)

        def evv(e, tt=tt):
            return e.activation(out=vsb[:, tt, :, 0:64],
                                in_=vp[:].rearrange("p (h o) -> p h o", h=16),
                                func=AF.Copy)
        sq.c("scalar", evv)
    es1.close()

    es2 = ExitStack()
    sb2 = lambda n, s, t: es2.enter_context(nc.sbuf_tensor(n, s, t))
    ps2 = lambda n, s: es2.enter_context(nc.psum_tensor(n, s, F32))
    att = sb2("p9_att", [128, 8, 4 * T], BF16)
    pt = sb2("p9_pt", [128, 2, 128], BF16)
    rrow = sb2("p9_rrow", [1, 128], F32)
    repsb = sb2("p9_reps", [64, 128], F32)
    opw = sb2("p9_opw", [128, 8, 1024], BF16)
    opb = sb2("p9_opb", [128, 8], F32)
    obuf = sb2("p9_obuf", [128, 512], F32)
    sA = ps2("p9_sA", [128, 128])
    sB = ps2("p9_sB", [128, 128])
    av = [ps2(f"p9_av{i}", [65, 128]) for i in range(2)]
    repp = ps2("p9_repp", [64, 128])
    op = ps2("p9_op", [128, 512])

    cast_load(sq, opw[:], dr["opT"][:].rearrange("(c p) f -> p c f", p=128))
    sq.d("sync", lambda e: [e.dma_start(out=opb[:], in_=dr["opb"][:])])

    for tt in range(16):
        for hpp in range(8):
            def smm(e, tt=tt, hpp=hpp):
                e.matmul(sA[:], qk[0:64, 8 + hpp, tt * 128:(tt + 1) * 128],
                         qk[0:64, hpp, tt * 128:(tt + 1) * 128],
                         start=True, stop=True, tile_position=(0, 0))
                return e.matmul(sB[:], qk[64:128, 8 + hpp, tt * 128:(tt + 1) * 128],
                                qk[64:128, hpp, tt * 128:(tt + 1) * 128],
                                start=True, stop=True, tile_position=(64, 0))
            sq.c("tensor", smm)

            def ex(e):
                e.activation(out=pt[:, 0], in_=sA[:], func=AF.Exp)
                return e.activation(out=pt[:, 1], in_=sB[:], func=AF.Exp)
            sq.c("scalar", ex)

            def mk(e):
                e.tensor_tensor(pt[:, 0], pt[:, 0], l1m[:], ALU.mult)
                return e.tensor_tensor(pt[:, 1], pt[:, 1], l1m[:], ALU.mult)
            sq.c("gpsimd", mk)

            def avm(e, tt=tt, hpp=hpp):
                for j in range(2):
                    h = 2 * hpp + j
                    i = e.matmul(av[j][:], vsb[:, tt, h, :], pt[:, j],
                                 start=True, stop=True)
                return i
            sq.c("tensor", avm)
            for j in range(2):
                h = 2 * hpp + j
                sq.c("vector", lambda e, j=j: e.reciprocal(rrow[:], av[j][64:65, :]))
                sq.c("tensor", lambda e: e.matmul(repp[:], ones64[:], rrow[:],
                                                  start=True, stop=True))
                sq.c("scalar", lambda e: e.activation(out=repsb[:], in_=repp[:],
                                                      func=AF.Copy))

                def div(e, h=h, j=j, tt=tt):
                    c = h // 2
                    r0 = (h % 2) * 64
                    return e.tensor_tensor(att[r0:r0 + 64, c, tt * 128:(tt + 1) * 128],
                                           av[j][0:64, :], repsb[:], ALU.mult)
                sq.c("vector", div)

    for fc in range(8):
        for g4 in range(4):
            def mmo(e, fc=fc, g4=g4):
                for dc in range(8):
                    i = e.matmul(op[:], opw[:, dc, fc * 128:(fc + 1) * 128],
                                 att[:, dc, g4 * 512:(g4 + 1) * 512],
                                 start=(dc == 0), stop=(dc == 7))
                return i
            sq.c("tensor", mmo)
            sq.c("scalar", lambda e, fc=fc: e.activation(
                out=obuf[:], in_=op[:], func=AF.Identity, bias=opb[:, fc:fc + 1]))
            sq.d("sync", lambda e, fc=fc, g4=g4: [
                e.dma_start(out=dr["attn1p"][fc, :, g4 * 512:(g4 + 1) * 512],
                            in_=obuf[:])])
    es2.close()



def p10(cx, sq, es):
    nc = cx.nc
    dr = cx.dram
    sb = lambda n, s, t: es.enter_context(nc.sbuf_tensor(n, s, t))

    ident = sb("p10_ident", [128, 128], F32)
    ones_row = sb("p10_onesr", [1, 128], F32)
    sq.d("sync", lambda e: [
        e.dma_start(out=ident[:], in_=dr["c_ident"][:]),
        e.dma_start(out=ones_row[:], in_=dr["c_ones_row"][:]),
    ])

    es1 = ExitStack()
    sb1 = lambda n, s, t: es1.enter_context(nc.sbuf_tensor(n, s, t))
    ps1 = lambda n, s: es1.enter_context(nc.psum_tensor(n, s, F32))
    wrow = sb1("p10_wrow", [128, 4096], F32)
    scl = sb1("p10_scl", [128, 32], F32)
    sclT = sb1("p10_sclT", [32, 128], F32)
    screp = sb1("p10_screp", [128, 4096], F32)
    rcrep = sb1("p10_rcrep", [128, 4096], F32)
    scrow = sb1("p10_scrow", [1, 4096], F32)
    rcrow = sb1("p10_rcrow", [1, 4096], F32)
    wt = sb1("p10_wt", [128, 4096], F32)
    z = sb1("p10_z", [128, 4096], F32)
    gtl = sb1("p10_gtl", [128, 4096], F32)
    weffb = sb1("p10_weff", [128, 4096], BF16)
    tpp = ps1("p10_tpp", [32, 128])
    repP = ps1("p10_repP", [128, 512])

    def weff_prep(wname, wTname, nrows, scname, slabname):
        inner = WEIGHT_SHAPES[wname][1]
        ntile = nrows // 128
        for rt in range(ntile):
            sq.d("sync", lambda e, rt=rt, inner=inner: [
                e.dma_start(out=wrow[:, 0:inner],
                            in_=dr[wname][rt * 128:(rt + 1) * 128, :])])

            sq.c("vector", lambda e, rt=rt, inner=inner: e.tensor_reduce(
                scl[:, rt:rt + 1], wrow[:, 0:inner], AX.X, ALU.add,
                apply_absolute_value=True))
            sq.c("vector", lambda e, rt=rt, inner=inner: e.tensor_single_scalar(
                scl[:, rt:rt + 1], scl[:, rt:rt + 1], 1.0 / inner, ALU.mult))
            sq.c("vector", lambda e, rt=rt: e.tensor_single_scalar(
                scl[:, rt:rt + 1], scl[:, rt:rt + 1], 1e-5, ALU.max))

        def trs(e, ntile=ntile):
            return e.transpose(tpp[0:ntile, :], scl[:, 0:ntile], ident[:])
        sq.c("tensor", trs)
        sq.c("vector", lambda e, ntile=ntile: e.tensor_copy(sclT[0:ntile, :],
                                                            tpp[0:ntile, :]))
        sq.d("sync", lambda e, ntile=ntile, nrows=nrows: [
            e.dma_start(out=dr[scname][0:1, 0:nrows].rearrange(
                "one (a b) -> (one a) b", a=ntile), in_=sclT[0:ntile, :])])
        sq.d("sync", lambda e, nrows=nrows: [
            e.dma_start(out=scrow[:, 0:nrows], in_=dr[scname][:, 0:nrows])])
        sq.c("vector", lambda e, nrows=nrows: e.reciprocal(rcrow[:, 0:nrows],
                                                           scrow[:, 0:nrows]))
        for half in range(nrows // 512):
            def repmm(e, half=half):
                return e.matmul(repP[:], ones_row[:],
                                scrow[:, half * 512:(half + 1) * 512],
                                start=True, stop=True)
            sq.c("tensor", repmm)
            sq.c("vector", lambda e, half=half: e.tensor_copy(
                screp[:, half * 512:(half + 1) * 512], repP[:]))

            def repmm2(e, half=half):
                return e.matmul(repP[:], ones_row[:],
                                rcrow[:, half * 512:(half + 1) * 512],
                                start=True, stop=True)
            sq.c("tensor", repmm2)
            sq.c("vector", lambda e, half=half: e.tensor_copy(
                rcrep[:, half * 512:(half + 1) * 512], repP[:]))
        nchT = WEIGHT_SHAPES[wTname][0] // 128
        for dc in range(nchT):
            sq.d("sync", lambda e, dc=dc, nrows=nrows: [
                e.dma_start(out=wt[:, 0:nrows],
                            in_=dr[wTname][dc * 128:(dc + 1) * 128, :])])

            sq.c("vector", lambda e, nrows=nrows: e.tensor_tensor(
                z[:, 0:nrows], wt[:, 0:nrows], rcrep[:, 0:nrows], ALU.mult))
            sq.c("vector", lambda e, nrows=nrows: e.tensor_single_scalar(
                gtl[:, 0:nrows], z[:, 0:nrows], 0.5, ALU.is_gt))
            sq.c("vector", lambda e, nrows=nrows: e.tensor_single_scalar(
                z[:, 0:nrows], z[:, 0:nrows], -0.5, ALU.is_lt))
            sq.c("vector", lambda e, nrows=nrows: e.tensor_tensor(
                z[:, 0:nrows], gtl[:, 0:nrows], z[:, 0:nrows], ALU.subtract))
            sq.c("vector", lambda e, nrows=nrows: e.tensor_tensor(
                weffb[:, 0:nrows], z[:, 0:nrows], screp[:, 0:nrows], ALU.mult))
            sq.d("sync", lambda e, dc=dc, nrows=nrows, slabname=slabname: [
                e.dma_start(out=dr[slabname][dc, :, 0:nrows], in_=weffb[:, 0:nrows])])

    weff_prep("wfc", "wfcT", 4096, "scfc", "wfcE")
    weff_prep("wpr", "wprT", 1024, "scpr", "wprE")
    es1.close()

    es2 = ExitStack()
    sb2 = lambda n, s, t: es2.enter_context(nc.sbuf_tensor(n, s, t))
    ps2 = lambda n, s: es2.enter_context(nc.psum_tensor(n, s, F32))
    wfce = sb2("p10_wfce", [128, 8, 4096], BF16)
    wpre = sb2("p10_wpre", [128, 32, 1024], BF16)
    vT = sb2("p10_vT", [128, 8, 16], BF16)
    uT = sb2("p10_uT", [16, 4096], BF16)
    fbT = sb2("p10_fbT", [16, 16], BF16)
    gate = sb2("p10_gate", [1, 1], F32)
    gcol = sb2("p10_gcol", [16, 1], F32)
    xnb = sb2("p10_xnb", [128, 8, T], BF16)
    t1 = sb2("p10_t1", [16, 512], BF16)
    t2 = sb2("p10_t2", [16, 512], BF16)
    r2 = sb2("p10_r2", [128, 32, 512], BF16)
    rbuf = sb2("p10_rbuf", [128, 512], BF16)
    obuf = sb2("p10_obuf", [128, 512], F32)
    t1p = ps2("p10_t1p", [16, 512])
    t2p = ps2("p10_t2p", [16, 512])
    hpp = [ps2(f"p10_hp{i}", [128, 512]) for i in range(2)]
    op = ps2("p10_op", [128, 512])
    gp = ps2("p10_gp", [16, 1])

    sq.d("sync", lambda e: [
        e.dma_start(out=wfce[:], in_=dr["wfcE"][:].rearrange("c p f -> p c f")),
        e.dma_start(out=wpre[:], in_=dr["wprE"][:].rearrange("c p f -> p c f")),
        e.dma_start(out=gate[:], in_=dr["l1gate"][:]),
    ])
    cast_load(sq, vT[:], dr["l1vT"][:].rearrange("(c p) f -> p c f", p=128))
    cast_load(sq, uT[:], dr["l1uT"][:])
    cast_load(sq, fbT[:], dr["fbT"][:])

    def grep(e):
        return e.matmul(gp[:], ones_row[0:1, 0:16], gate[:], start=True, stop=True)
    sq.c("tensor", grep)
    sq.c("vector", lambda e: e.tensor_copy(gcol[:], gp[:]))

    for b in range(B):
        sq.d("sync", lambda e, b=b: [
            e.dma_start(out=xnb[:], in_=dr["xn1bf"][b].rearrange("c p t -> p c t"))])

        def mmt1(e):
            for dc in range(8):
                i = e.matmul(t1p[:], vT[:, dc], xnb[:, dc], start=(dc == 0),
                             stop=(dc == 7))
            return i
        sq.c("tensor", mmt1)
        sq.c("vector", lambda e: e.tensor_single_scalar(t1[:], t1p[:], gcol[:],
                                                        ALU.mult))
        sq.c("tensor", lambda e: e.matmul(t2p[:], fbT[:], t1[:], start=True, stop=True))
        sq.c("vector", lambda e: e.tensor_copy(t2[:], t2p[:]))
        for hc in range(32):
            def mmh(e, hc=hc):
                for dc in range(8):
                    e.matmul(hpp[hc % 2][:], wfce[:, dc, hc * 128:(hc + 1) * 128],
                             xnb[:, dc], start=(dc == 0), stop=False)
                return e.matmul(hpp[hc % 2][:], uT[:, hc * 128:(hc + 1) * 128], t2[:],
                                start=False, stop=True)
            sq.c("tensor", mmh)

            def evh(e, hc=hc):
                return e.tensor_single_scalar(rbuf[:], hpp[hc % 2][:], 0.0, ALU.max)
            sq.c("vector", evh)
            sq.c("scalar", lambda e, hc=hc: e.activation(out=r2[:, hc], in_=rbuf[:],
                                                         func=AF.Square))
        for dc in range(8):
            def mmo(e, dc=dc):
                for hc in range(32):
                    i = e.matmul(op[:], wpre[:, hc, dc * 128:(dc + 1) * 128],
                                 r2[:, hc], start=(hc == 0), stop=(hc == 31))
                return i
            sq.c("tensor", mmo)
            sq.c("vector", lambda e: e.tensor_copy(obuf[:], op[:]))
            sq.d("sync", lambda e, b=b, dc=dc: [
                e.dma_start(out=dr["ffn1T"][b, dc], in_=obuf[:])])
    es2.close()



def p11(cx, sq, es):
    nc = cx.nc
    dr = cx.dram
    sb = lambda n, s, t: es.enter_context(nc.sbuf_tensor(n, s, t))
    ps = lambda n, s: es.enter_context(nc.psum_tensor(n, s, F32))

    ident = sb("p11_ident", [128, 128], F32)
    x2 = sb("p11_x2", [128, 8, T], F32)
    tmp = sb("p11_tmp", [128, 8, T], F32)
    ap = sb("p11_ap", [128, 8, 4 * T], F32)
    obuf = sb("p11_obuf", [128, 1024], F32)
    tp_p = ps("p11_tp", [128, 1024])

    sq.d("sync", lambda e: [e.dma_start(out=ident[:], in_=dr["c_ident"][:])])
    sq.d("sync", lambda e: [
        e.dma_start(out=ap[:], in_=dr["attn1p"][:].rearrange("c p t -> p c t"))])
    for b in range(B):
        sq.d("sync", lambda e, b=b: [
            e.dma_start(out=x2[:], in_=dr["x1T32"][b].rearrange("c p t -> p c t")),
            e.dma_start(out=tmp[:], in_=dr["ffn1T"][b].rearrange("c p t -> p c t")),
        ])

        def add1(e):
            for c in range(8):
                i = e.tensor_tensor(x2[:, c], x2[:, c], tmp[:, c], ALU.add)
            return i
        sq.c("vector", add1)

        def add2(e, b=b):
            for c in range(8):
                i = e.tensor_tensor(
                    x2[:, c], x2[:, c],
                    ap[:, c].rearrange("p (t i) -> p t i", i=4)[:, :, b], ALU.add)
            return i
        sq.c("vector", add2)
        for tt in range(NT):
            def tr(e, tt=tt):
                for c in range(8):
                    i = e.transpose(tp_p[:, c * 128:(c + 1) * 128],
                                    x2[:, c, tt * 128:(tt + 1) * 128], ident[:])
                return i
            sq.c("tensor", tr)
            sq.c("vector", lambda e: e.tensor_copy(obuf[:], tp_p[:]))
            sq.d("sync", lambda e, b=b, tt=tt: [
                e.dma_start(out=dr["outp"][b, tt * 128:(tt + 1) * 128, :], in_=obuf[:])])


PHASES = {"p1": p1, "p2": p2, "p2w": p2w, "p3": p3, "p4a": p4a, "p4f": p4f,
          "p4af": p4af, "p4ff": p4ff, "p5": p5, "p7": p7, "p8": p8, "p9": p9,
          "p10": p10, "p11": p11}



def build_program(phases, ext_in_slabs=(), ext_out_slabs=()):
    nc = bass.Bass("TRN2", target_bir_lowering=False)
    cx = Ctx(nc)
    sq = Seq(cx)

    need = set()
    for p in phases:
        need.update(PHASE_WEIGHTS.get(p, []))
    if "p1" in phases:
        cx.dram_t("xsl", [B, T, D], F32, "ExternalInput")
    for w in sorted(need):
        cx.dram_t(w, list(WEIGHT_SHAPES[w]), F32, "ExternalInput")

    if "p2w" in phases:
        cx.dram_t("xfull", [B, S, D], F32, "ExternalInput")
    used = set()
    for p in phases:
        r, w = PHASE_SLABS[p]
        used.update(r)
        used.update(w)
    for name in sorted(used):
        shape, dt = SLABS.get(name) or GATHERED.get(name) or WIDE[name]
        if name in ext_in_slabs:
            kind = "ExternalInput"
        elif name in ext_out_slabs:
            kind = "ExternalOutput"
        else:
            kind = "Internal"
        cx.dram_t(name, shape, dt, kind)
    if "p11" in phases:
        cx.dram_t("outp", [B, T, D], F32, "ExternalOutput")

    with nc.Block() as blk:
        for p in phases:
            es = ExitStack()
            PHASES[p](cx, sq, es)
            sq.flush(blk)
            es.close()
        sq.finish(blk)
    cx.es.close()
    return nc



def assemble_gathered(resA):
    kattn = np.concatenate([r["kattn"] for r in resA], axis=3)
    kffn = np.concatenate([r["kffn"] for r in resA], axis=3)
    vattn = np.concatenate([r["vattn"] for r in resA], axis=2)
    vffn = np.concatenate([r["vffn"] for r in resA], axis=2)
    out = []
    for c in range(NC):
        ko, vo = reorder_causal(kattn, vattn, c)
        out.append({"kattn_r": ko, "vattn_r": vo, "kffn_g": kffn, "vffn_g": vffn})
    return out


PH_A = ["p1", "p2"]
PH_B = ["p3", "p4a", "p4f", "p5", "p7", "p8", "p9", "p10", "p11"]
PH_FUSED = ["p1", "p2", "p2w", "p3", "p4af", "p4ff", "p5", "p7", "p8", "p9",
            "p10", "p11"]
A_OUT = ["xT32", "xnbf", "xnf32", "kattn", "kffn", "vattn", "vffn"]
B_IN = ["xT32", "xnbf", "xnf32", "kattn_r", "kffn_g", "vattn_r", "vffn_g"]


def kernel_unfused(**inputs):
    g = stage_common(inputs)
    x = np.ascontiguousarray(np.asarray(inputs["L0_attn_x"], np.float32))

    ncA = build_program(PH_A, ext_out_slabs=A_OUT)
    in_mapsA = []
    for c in range(NC):
        m = {"xsl": np.ascontiguousarray(x[:, c * T:(c + 1) * T, :])}
        for w in sorted(set(PHASE_WEIGHTS["p1"]) | set(PHASE_WEIGHTS["p2"])):
            m[w] = g[w]
        in_mapsA.append(m)
    resA = run_bass_kernel_spmd(ncA, in_mapsA, core_ids=list(range(NC))).results

    percore_kv = assemble_gathered(resA)

    ncB = build_program(PH_B, ext_in_slabs=B_IN)
    wB = set()
    for p in PH_B:
        wB.update(PHASE_WEIGHTS.get(p, []))
    in_mapsB = []
    for c in range(NC):
        m = {}
        for w in sorted(wB):
            m[w] = g[w]
        for nm in ("xT32", "xnbf", "xnf32"):
            m[nm] = resA[c][nm]
        m.update(percore_kv[c])
        in_mapsB.append(m)
    resB = run_bass_kernel_spmd(ncB, in_mapsB, core_ids=list(range(NC))).results

    out = np.empty((B, S, D), np.float32)
    for c in range(NC):
        out[:, c * T:(c + 1) * T, :] = resB[c]["outp"]
    return out


def kernel_fused(**inputs):
    g = stage_common(inputs)
    x = np.ascontiguousarray(np.asarray(inputs["L0_attn_x"], np.float32))
    ncF = build_program(PH_FUSED)
    wF = set()
    for p in PH_FUSED:
        wF.update(PHASE_WEIGHTS.get(p, []))
    wF.discard("valid8")
    in_maps = []
    for c in range(NC):
        m = {"xsl": np.ascontiguousarray(x[:, c * T:(c + 1) * T, :]), "xfull": x}
        for w in sorted(wF):
            m[w] = g[w]
        valid = np.zeros((128, 8), np.float32)
        valid[:, :c] = 1.0
        m["valid8"] = valid
        in_maps.append(m)
    res = run_bass_kernel_spmd(ncF, in_maps, core_ids=list(range(NC))).results
    out = np.empty((B, S, D), np.float32)
    for c in range(NC):
        out[:, c * T:(c + 1) * T, :] = res[c]["outp"]
    return out


kernel = kernel_fused


# version 13
# speedup vs baseline: 1.0142x; 1.0142x over previous
import numpy as np
import ml_dtypes
import concourse.bass as bass
import concourse.mybir as mybir
from concourse.bass_utils import run_bass_kernel_spmd
from contextlib import ExitStack

F32 = mybir.dt.float32
BF16 = mybir.dt.bfloat16
AF = mybir.ActivationFunctionType
ALU = mybir.AluOpType
AX = mybir.AxisListType

NC = 8
B, S, D = 4, 4096, 1024
H, HD = 16, 64
FF, E = 2048, 8
T = 512
NT = T // 128
NKC = S // 128
EPS = 1e-5
BF = ml_dtypes.bfloat16



def _cb(v, n):
    return np.ascontiguousarray(np.asarray(v, np.float32).reshape(n, 128).T)


def stage_common(ins):
    g = {}
    g["wqT"] = ins["L0_attn_Wq"].T
    g["wkT"] = ins["L0_attn_Wk"].T
    g["wvT"] = ins["L0_attn_Wv"].T
    g["woT"] = ins["L0_attn_Wo"].T
    g["bq"] = _cb(ins["L0_attn_bq"], 8)
    g["bk"] = _cb(ins["L0_attn_bk"], 8)
    g["bv_row"] = np.asarray(ins["L0_attn_bv"]).reshape(1, D)
    g["bo"] = _cb(ins["L0_attn_bo"], 8)
    g["fqw"] = ins["L0_ffn_q_w"]
    g["fkw"] = ins["L0_ffn_k_w"]
    g["fvw"] = ins["L0_ffn_v_w"]
    g["fow"] = ins["L0_ffn_out_w"]
    g["fgw"] = ins["L0_ffn_gate_w"]
    g["fgb"] = _cb(ins["L0_ffn_gate_b"], 8)
    sw = np.asarray(ins["L0_ffn_scale_w"], np.float32)
    swbd = np.zeros((D, 2), np.float32)
    for h in range(H):
        swbd[h * HD:(h + 1) * HD, h % 2] = sw[h]
    g["swbd"] = swbd
    g["n1g"] = _cb(ins["L0_ffn_n1_g"], 8)
    g["n1b"] = _cb(ins["L0_ffn_n1_b"], 8)
    g["n2g"] = _cb(ins["L0_ffn_n2_g"], 8)
    g["n2b"] = _cb(ins["L0_ffn_n2_b"], 8)
    g["mgw"] = ins["L0_ffn_moe_gate_w"]
    g["mgb"] = np.asarray(ins["L0_ffn_moe_gate_b"], np.float32).reshape(8, 1)
    g["w1"] = ins["L0_ffn_w1"]
    g["b1c"] = np.asarray(ins["L0_ffn_b1"], np.float32).reshape(E, 16, 128).transpose(0, 2, 1)
    g["lngc"] = np.asarray(ins["L0_ffn_ln_g"], np.float32).reshape(E, 16, 128).transpose(0, 2, 1)
    g["lnbc"] = np.asarray(ins["L0_ffn_ln_b"], np.float32).reshape(E, 16, 128).transpose(0, 2, 1)
    g["w2"] = ins["L0_ffn_w2"]
    g["b2"] = ins["L0_ffn_b2"]
    g["rsc_row"] = np.asarray(ins["L0_ffn_res_scale"], np.float32).reshape(1, 8)
    g["rsc_col"] = np.asarray(ins["L0_ffn_res_scale"], np.float32).reshape(8, 1)
    g["ipT"] = ins["L1_attn_in_proj_weight"].T
    ipb = np.asarray(ins["L1_attn_in_proj_bias"], np.float32)
    g["ipb_qk"] = _cb(ipb[:2048], 16)
    g["ipb_v"] = ipb[2048:].reshape(1, D)
    g["opT"] = ins["L1_attn_out_proj_weight"].T
    g["opb"] = _cb(ins["L1_attn_out_proj_bias"], 8)
    g["wfc"] = ins["L1_ffn_w_fc"]
    g["wfcT"] = ins["L1_ffn_w_fc"].T
    g["wpr"] = ins["L1_ffn_w_proj"]
    g["wprT"] = ins["L1_ffn_w_proj"].T
    g["l1uT"] = ins["L1_ffn_u"].T
    g["l1vT"] = ins["L1_ffn_v"].T
    g["fbT"] = ins["L1_ffn_fast_b"].T
    g["l1gate"] = np.asarray(ins["L1_ffn_gate"], np.float32).reshape(1, 1)
    g["c_ident"] = np.eye(128, dtype=np.float32)
    g["c_ones_col"] = np.ones((128, 1), np.float32)
    g["c_ones_row"] = np.ones((1, 128), np.float32)
    g["c_ones64"] = np.ones((1, 64), np.float32)
    g["c_iota8"] = np.tile(np.arange(8, dtype=np.float32)[None, :], (128, 1))
    bp = np.zeros((2, 128), np.float32)
    bp[0, 0:64] = 1.0
    bp[1, 64:128] = 1.0
    g["c_bpair"] = bp
    g["c_ones16"] = np.ones((128, 16), np.float32)
    kk = np.arange(128)[:, None]
    pp = np.arange(512)[None, :]
    g["c_diagmask"] = np.stack([(kk <= pp - 128 * j) for j in range(4)]).astype(np.float32)
    jj = np.arange(128)[:, None]
    ii = np.arange(128)[None, :]
    g["c_l1mask"] = ((jj // 4) == (ii // 4)).astype(np.float32)
    return {k: np.ascontiguousarray(np.asarray(v, np.float32)) for k, v in g.items()}


def reorder_causal(kfull, vfull, core):
    qb = core * T
    nearly = qb // 128
    order = [qb // 128 + j for j in range(4)] + list(range(nearly))
    ko = np.zeros_like(kfull)
    vo = np.zeros_like(vfull)
    for pos, kc in enumerate(order):
        ko[:, :, :, pos * 128:(pos + 1) * 128] = kfull[:, :, :, kc * 128:(kc + 1) * 128]
        vo[:, :, pos] = vfull[:, :, kc]
    return np.ascontiguousarray(ko), np.ascontiguousarray(vo)



class Ctx:
    def __init__(self, nc):
        self.nc = nc
        self.es = ExitStack()
        self.dram = {}

    def dram_t(self, name, shape, dt, kind):
        if name in self.dram:
            return self.dram[name]
        t = self.nc.dram_tensor(name, list(shape), dt, kind=kind).ap()
        self.dram[name] = t
        return t


class _Mark:
    __slots__ = ("w",)

    def __init__(self):
        self.w = None


class Seq:

    CENGS = ("tensor", "scalar", "vector", "gpsimd")
    KEYS = ("t", "sc", "v", "gp", "s", "g", "sd", "gd")

    def __init__(self, cx):
        self.cx = cx
        nc = cx.nc
        self.sems = {
            "t": cx.es.enter_context(nc.semaphore("sq_t")),
            "sc": cx.es.enter_context(nc.semaphore("sq_sc")),
            "v": cx.es.enter_context(nc.semaphore("sq_v")),
            "gp": cx.es.enter_context(nc.semaphore("sq_gp")),
            "s": cx.es.enter_context(nc.semaphore("sq_s")),
            "g": cx.es.enter_context(nc.semaphore("sq_g")),
            "sd": cx.es.enter_context(nc.semaphore("sq_sd")),
            "gd": cx.es.enter_context(nc.semaphore("sq_gd")),
        }
        self.ckey = {"tensor": "t", "scalar": "sc", "vector": "v",
                     "gpsimd": "gp"}
        self.n = {k: 0 for k in self.KEYS}
        self.w_sd = 0
        self.w_gd = 0
        self.hw = {e: {} for e in ("tensor", "scalar", "vector", "gpsimd",
                                   "sync")}
        self.steps = []

    def c(self, eng, fn, at=None):
        self.steps.append((eng, "c", fn, at))

    def d(self, eng, fn, defer=False, at=None):
        self.steps.append((eng, "d", fn, (defer, at)))

    def barrier(self):
        self.steps.append((None, "bar", None, None))

    def mark(self):
        m = _Mark()
        self.steps.append((None, "mark", None, m))
        return m

    def _waits(self):
        w = dict(self.n)
        w["sd"] = self.w_sd
        w["gd"] = self.w_gd
        return w

    def _wait_list(self, eng, w):
        out = []
        hw = self.hw[eng]
        for k in self.KEYS:
            tgt = w.get(k, 0)
            if tgt <= 0 or tgt <= hw.get(k, 0):
                continue
            out.append((self.sems[k], tgt))
            hw[k] = tgt
        return out

    def _emit_crun(self, blk, eng, entries):
        def body(e, entries=entries):
            for fn, waits in entries:
                for sem, val in waits:
                    e.wait_ge(sem, val)
                r = fn(e)
                r = r[-1] if isinstance(r, (list, tuple)) else r
                r.then_inc(self.sems[self.ckey[eng]], 1)
        getattr(blk, eng)(body)

    def flush(self, blk):
        steps = self.steps
        self.steps = []
        runs = {}
        for eng, kind, fn, extra in steps:
            if kind == "bar":
                self.w_sd = self.n["sd"]
                self.w_gd = self.n["gd"]
                continue
            if kind == "mark":
                extra.w = self._waits()
                continue
            at = extra[1] if kind == "d" else extra
            if isinstance(at, _Mark):
                at = at.w
            w = at if at is not None else self._waits()
            if kind == "c":
                waits = self._wait_list(eng, w)
                runs.setdefault(eng, []).append((fn, waits))
                self.n[self.ckey[eng]] += 1
            else:
                defer = extra[0]
                assert eng in ("sync", "gpsimd")
                if eng in runs and runs[eng]:
                    self._emit_crun(blk, eng, runs.pop(eng))
                if defer:
                    key = "sd" if eng == "sync" else "gd"
                else:
                    key = "s" if eng == "sync" else "g"
                sem = self.sems[key]
                waits = self._wait_list(eng, w)
                holder = {}

                def body(e, fn=fn, waits=waits, sem=sem, holder=holder):
                    for s_, v_ in waits:
                        e.wait_ge(s_, v_)
                    r = fn(e)
                    rl = r if isinstance(r, (list, tuple)) else [r]
                    for i in rl:
                        i.then_inc(sem, 16)
                    holder["n"] = len(rl)
                getattr(blk, eng)(body)
                self.n[key] += 16 * holder["n"]
        for eng, entries in runs.items():
            if entries:
                self._emit_crun(blk, eng, entries)

    def finish(self, blk):
        w = dict(self.n)
        waits = self._wait_list("sync", w)

        def body(e):
            for sem, val in waits:
                e.wait_ge(sem, val)
        blk.sync(body)



class LnScratch:
    def __init__(self, es, nc, tag, nch=8, width=T, repm=None, repr_=None):
        sb = lambda n, s, t: es.enter_context(nc.sbuf_tensor(n, s, t))
        ps = lambda n, s: es.enter_context(nc.psum_tensor(n, s, F32))
        self.zsq = sb(f"{tag}_zsq", [128, nch, width], F32)
        self.rows = sb(f"{tag}_rows", [1, 4, width], F32)
        self.lnt = sb(f"{tag}_lnt", [128, nch, width], F32)
        self.s1p = ps(f"{tag}_s1", [1, width])
        self.s2p = ps(f"{tag}_s2", [1, width])
        self.repm = repm if repm is not None else ps(f"{tag}_repm", [128, width])
        self.repr = repr_ if repr_ is not None else ps(f"{tag}_repr", [128, width])


def ln_fm(sq, P, consts, z_sb, nch, width, out32, outbf, gamma=None, beta=None,
          n=None):
    n = n or (nch * 128)
    ones_col, ones_row = consts["ones_col"], consts["ones_row"]

    def sqs(e):
        for c in range(nch):
            i = e.activation(out=P.zsq[:, c], in_=z_sb[:, c], func=AF.Square)
        return i
    sq.c("scalar", sqs)

    def stats(e):
        for c in range(nch):
            e.matmul(P.s1p[:], ones_col[:], z_sb[:, c], start=(c == 0), stop=(c == nch - 1))
        for c in range(nch):
            i = e.matmul(P.s2p[:], ones_col[:], P.zsq[:, c], start=(c == 0),
                         stop=(c == nch - 1))
        return i
    sq.c("tensor", stats)

    sq.c("vector", lambda e: e.tensor_single_scalar(P.rows[:, 0], P.s1p[:],
                                                    1.0 / n, ALU.mult))
    sq.c("vector", lambda e: e.tensor_tensor(P.rows[:, 1], P.rows[:, 0],
                                             P.rows[:, 0], ALU.mult))
    sq.c("vector", lambda e: e.scalar_tensor_tensor(
        out=P.rows[:, 2], in0=P.s2p[:], scalar=1.0 / n, in1=P.rows[:, 1],
        op0=ALU.mult, op1=ALU.subtract))
    sq.c("vector", lambda e: e.tensor_single_scalar(P.rows[:, 2], P.rows[:, 2],
                                                    EPS, ALU.add))
    sq.c("scalar", lambda e: e.activation(out=P.rows[:, 1], in_=P.rows[:, 2],
                                          func=AF.Ln))
    sq.c("scalar", lambda e: e.activation(out=P.rows[:, 3], in_=P.rows[:, 1],
                                          func=AF.Exp, scale=-0.5))

    def reps(e):
        e.matmul(P.repm[:], ones_row[:], P.rows[:, 0], start=True, stop=True)
        return e.matmul(P.repr[:], ones_row[:], P.rows[:, 3], start=True, stop=True)
    sq.c("tensor", reps)

    def norm1(e):
        for c in range(nch):
            i = e.tensor_tensor(P.lnt[:, c], z_sb[:, c], P.repm[:], ALU.subtract)
        return i
    sq.c("vector", norm1)

    def norm2(e):
        for c in range(nch):
            i = e.tensor_tensor(P.lnt[:, c], P.lnt[:, c], P.repr[:], ALU.mult)
        return i
    sq.c("vector", norm2)

    if gamma is not None:
        tgt = out32 if out32 is not None else P.lnt

        def gb(e):
            for c in range(nch):
                i = e.activation(out=tgt[:, c], in_=P.lnt[:, c], func=AF.Identity,
                                 scale=gamma[:, c:c + 1], bias=beta[:, c:c + 1])
            return i
        sq.c("scalar", gb)
        if outbf is not None:
            def cb(e):
                for c in range(nch):
                    i = e.tensor_copy(outbf[:, c], tgt[:, c])
                return i
            sq.c("vector", cb)
    else:
        def outs(e):
            i = None
            for c in range(nch):
                if out32 is not None:
                    i = e.tensor_copy(out32[:, c], P.lnt[:, c])
                if outbf is not None:
                    i = e.tensor_copy(outbf[:, c], P.lnt[:, c])
            return i
        sq.c("vector", outs)


def cast_load(sq, dst_ap, src_ap):
    sq.d("gpsimd", lambda e: [e.dma_start(out=dst_ap, in_=src_ap)])



SLABS = {
    "xT32": ([B, 8, 128, T], F32),
    "xnbf": ([B, 8, 128, T], BF16),
    "xnf32": ([B, 8, 128, T], F32),
    "kattn": ([B, 8, 128, T], BF16),
    "kffn": ([B, 8, 128, T], BF16),
    "vattn": ([B, 4, NT, 128, 260], BF16),
    "vffn": ([B, 4, NT, 128, 260], BF16),
    "qa": ([B, 8, 128, T], BF16),
    "qf": ([B, 8, 128, T], BF16),
    "attn0T": ([B, 8, 128, T], F32),
    "fattn0T": ([B, 8, 128, T], F32),
    "xrT32": ([B, 8, 128, T], F32),
    "xrTbf": ([B, 8, 128, T], BF16),
    "selT": ([B, NT, 128, 1024], F32),
    "mrsc": ([B, NT, 128, 8], F32),
    "mbf": ([B, NT, 128, 8], BF16),
    "ffn0T": ([B, 8, 128, T], F32),
    "x1T32": ([B, 8, 128, T], F32),
    "xn1bf": ([B, 8, 128, T], BF16),
    "xpbf": ([8, 128, 4 * T], BF16),
    "attn1p": ([8, 128, 4 * T], F32),
    "ffn1T": ([B, 8, 128, T], F32),
    "wfcE": ([8, 128, 4096], BF16),
    "wprE": ([32, 128, 1024], BF16),
    "scfc": ([1, 4096], F32),
    "scpr": ([1, 1024], F32),
}

GATHERED = {
    "kattn_r": ([B, 8, 128, S], BF16),
    "kffn_g": ([B, 8, 128, S], BF16),
    "vattn_r": ([B, 4, NKC, 128, 260], BF16),
    "vffn_g": ([B, 4, NKC, 128, 260], BF16),
}

WIDE = {
    "kaw": ([B, 8, 128, S], BF16),
    "kfw": ([B, 8, 128, S], BF16),
    "vaw": ([B, 4, NKC, 128, 260], BF16),
    "vfw": ([B, 4, NKC, 128, 260], BF16),
}

WEIGHT_SHAPES = {
    "wqT": (D, D), "wkT": (D, D), "wvT": (D, D), "woT": (D, D),
    "bq": (128, 8), "bk": (128, 8), "bv_row": (1, D), "bo": (128, 8),
    "fqw": (D, D), "fkw": (D, D), "fvw": (D, D), "fow": (D, D),
    "fgw": (D, D), "fgb": (128, 8), "swbd": (D, 2),
    "n1g": (128, 8), "n1b": (128, 8), "n2g": (128, 8), "n2b": (128, 8),
    "mgw": (D, 8), "mgb": (8, 1),
    "w1": (E, D, FF), "b1c": (E, 128, 16), "lngc": (E, 128, 16),
    "lnbc": (E, 128, 16), "w2": (E, FF, D), "b2": (E, D),
    "rsc_row": (1, 8), "rsc_col": (8, 1),
    "ipT": (D, 3 * D), "ipb_qk": (128, 16), "ipb_v": (1, D),
    "opT": (D, D), "opb": (128, 8),
    "wfc": (4096, D), "wfcT": (D, 4096), "wpr": (D, 4096), "wprT": (4096, D),
    "l1uT": (16, 4096), "l1vT": (D, 16), "fbT": (16, 16), "l1gate": (1, 1),
    "c_ident": (128, 128), "c_ones_col": (128, 1), "c_ones_row": (1, 128),
    "c_ones64": (1, 64), "c_iota8": (128, 8),
    "c_diagmask": (4, 128, 512), "c_l1mask": (128, 128),
    "c_bpair": (2, 128), "valid8": (128, 8), "c_ones16": (128, 16),
}

PHASE_WEIGHTS = {
    "p1": ["c_ident", "c_ones_col", "c_ones_row"],
    "p2": ["wkT", "wvT", "fkw", "fvw", "bk", "bv_row", "c_ones_row"],
    "p2w": ["wkT", "wvT", "fkw", "fvw", "bk", "bv_row", "c_ones_row", "valid8",
            "c_ones16", "c_ident", "c_ones_col"],
    "p4af": ["woT", "bo", "c_diagmask", "c_ones64"],
    "p4ff": ["fgw", "fgb", "fow", "c_ones64"],
    "p3": ["wqT", "fqw", "bq", "swbd", "c_bpair"],
    "p4a": ["woT", "bo", "c_diagmask", "c_ones64"],
    "p4f": ["fgw", "fgb", "fow", "c_ones64"],
    "p5": ["n1g", "n1b", "c_ones_col", "c_ones_row"],
    "p7": ["mgw", "mgb", "w1", "b1c", "lngc", "lnbc", "w2", "b2", "rsc_row",
           "rsc_col", "c_iota8", "c_ident", "n2g", "n2b", "c_ones_col",
           "c_ones_row"],
    "p8": ["c_ones_col", "c_ones_row"],
    "p9": ["ipT", "ipb_qk", "ipb_v", "opT", "opb", "c_l1mask", "c_ones64",
           "c_ident"],
    "p10": ["wfc", "wfcT", "wpr", "wprT", "l1uT", "l1vT", "fbT", "l1gate",
            "c_ident", "c_ones_row"],
    "p11": ["c_ident"],
}

PHASE_SLABS = {
    "p1": ([], ["xT32", "xnbf", "xnf32"]),
    "p2": (["xnbf"], ["kattn", "kffn", "vattn", "vffn"]),
    "p2w": ([], ["kaw", "kfw", "vaw", "vfw"]),
    "p4af": (["qa", "kaw", "vaw", "kattn", "vattn"], ["attn0T"]),
    "p4ff": (["qf", "kfw", "vfw"], ["fattn0T"]),
    "p3": (["xnbf"], ["qa", "qf"]),
    "p4a": (["qa", "kattn_r", "vattn_r"], ["attn0T"]),
    "p4f": (["qf", "kffn_g", "vffn_g"], ["fattn0T"]),
    "p5": (["xnf32", "fattn0T"], ["xrT32", "xrTbf"]),
    "p7": (["xrTbf", "xrT32"], ["selT", "mrsc", "mbf", "ffn0T"]),
    "p8": (["xT32", "attn0T", "ffn0T"], ["x1T32", "xn1bf", "xpbf"]),
    "p9": (["xpbf"], ["attn1p"]),
    "p10": (["xn1bf"], ["wfcE", "wprE", "scfc", "scpr", "ffn1T"]),
    "p11": (["x1T32", "attn1p", "ffn1T"], []),
}



def p1(cx, sq, es):
    nc = cx.nc
    dr = cx.dram
    sb = lambda n, s, t: es.enter_context(nc.sbuf_tensor(n, s, t))
    ps = lambda n, s: es.enter_context(nc.psum_tensor(n, s, F32))

    xtile = sb("p1_xtile", [128, 1024], F32)
    ident = sb("p1_ident", [128, 128], F32)
    ones_col = sb("p1_onesc", [128, 1], F32)
    ones_row = sb("p1_onesr", [1, 128], F32)
    xT = sb("p1_xT", [128, 8, T], F32)
    xn32 = sb("p1_xn32", [128, 8, T], F32)
    xnbf = sb("p1_xnbf", [128, 8, T], BF16)
    P = LnScratch(es, nc, "p1")
    consts = {"ones_col": ones_col, "ones_row": ones_row}
    tp_p = ps("p1_tp", [128, 1024])

    sq.d("sync", lambda e: [
        e.dma_start(out=ident[:], in_=dr["c_ident"][:]),
        e.dma_start(out=ones_col[:], in_=dr["c_ones_col"][:]),
        e.dma_start(out=ones_row[:], in_=dr["c_ones_row"][:]),
    ])

    for b in range(B):
        for tt in range(NT):
            sq.d("sync", lambda e, b=b, tt=tt: [
                e.dma_start(out=xtile[:], in_=dr["xsl"][b, tt * 128:(tt + 1) * 128, :])])

            def tr(e):
                for c in range(8):
                    i = e.transpose(tp_p[:, c * 128:(c + 1) * 128],
                                    xtile[:, c * 128:(c + 1) * 128], ident[:])
                return i
            sq.c("tensor", tr)

            def ev(e, tt=tt):
                return e.tensor_copy(
                    xT[:, :, tt * 128:(tt + 1) * 128],
                    tp_p[:].rearrange("p (c t) -> p c t", c=8))
            sq.c("vector", ev)

        ln_fm(sq, P, consts, xT, 8, T, xn32, xnbf)

        def store(e, b=b):
            out = []
            for c in range(8):
                out.append(e.dma_start(out=dr["xT32"][b, c], in_=xT[:, c]))
                out.append(e.dma_start(out=dr["xnf32"][b, c], in_=xn32[:, c]))
                out.append(e.dma_start(out=dr["xnbf"][b, c], in_=xnbf[:, c]))
            return out
        sq.d("sync", store)



def p2(cx, sq, es):
    nc = cx.nc
    dr = cx.dram
    sb = lambda n, s, t: es.enter_context(nc.sbuf_tensor(n, s, t))
    ps = lambda n, s: es.enter_context(nc.psum_tensor(n, s, F32))

    wk = sb("p2_wk", [128, 8, 1024], BF16)
    wv = sb("p2_wv", [128, 8, 1024], BF16)
    fk = sb("p2_fk", [128, 8, 1024], BF16)
    fv = sb("p2_fv", [128, 8, 1024], BF16)
    bk = sb("p2_bk", [128, 8], F32)
    bvr = sb("p2_bvr", [1, 1024], BF16)
    onesr = sb("p2_onesr", [1, 128], BF16)
    xnb = sb("p2_xnb", [128, 8, T], BF16)
    kbuf = sb("p2_kbuf", [128, 512], BF16)
    vbuf = sb("p2_vbuf", [128, 16, 65], BF16)
    kp = ps("p2_kp", [128, 512])
    vp = ps("p2_vp", [128, 1024])

    for w, nm in [(wk, "wkT"), (wv, "wvT"), (fk, "fkw"), (fv, "fvw")]:
        cast_load(sq, w[:], dr[nm][:].rearrange("(c p) f -> p c f", p=128))
    cast_load(sq, bvr[:], dr["bv_row"][:])
    cast_load(sq, onesr[:], dr["c_ones_row"][:])
    sq.d("sync", lambda e: [e.dma_start(out=bk[:], in_=dr["bk"][:])])
    sq.c("vector", lambda e: e.memset(vbuf[:, :, 64:65], 1.0))

    for b in range(B):
        sq.d("sync", lambda e, b=b: [
            e.dma_start(out=xnb[:], in_=dr["xnbf"][b].rearrange("c p t -> p c t"))])
        for mod in ("a", "f"):
            wmat = wk if mod == "a" else fk
            kout = dr["kattn"] if mod == "a" else dr["kffn"]
            for fc in range(8):
                def mm(e, fc=fc, wmat=wmat):
                    for dc in range(8):
                        i = e.matmul(kp[:], wmat[:, dc, fc * 128:(fc + 1) * 128],
                                     xnb[:, dc], start=(dc == 0), stop=(dc == 7))
                    return i
                sq.c("tensor", mm)

                def ev(e, fc=fc, mod=mod):
                    if mod == "a":
                        return e.activation(out=kbuf[:], in_=kp[:], func=AF.Identity,
                                            bias=bk[:, fc:fc + 1])
                    return e.activation(out=kbuf[:], in_=kp[:], func=AF.Copy)
                sq.c("scalar", ev)
                sq.d("sync", lambda e, b=b, fc=fc, kout=kout: [
                    e.dma_start(out=kout[b, fc], in_=kbuf[:])])

            wv_ = wv if mod == "a" else fv
            vout = dr["vattn"] if mod == "a" else dr["vffn"]
            for tt in range(NT):
                def mmv(e, tt=tt, wv_=wv_, mod=mod):
                    for dc in range(8):
                        for half in range(2):
                            i = e.matmul(vp[:, half * 512:(half + 1) * 512],
                                         xnb[:, dc, tt * 128:(tt + 1) * 128],
                                         wv_[:, dc, half * 512:(half + 1) * 512],
                                         start=(dc == 0),
                                         stop=(dc == 7 and mod == "f"))
                    if mod == "a":
                        for half in range(2):
                            i = e.matmul(vp[:, half * 512:(half + 1) * 512],
                                         onesr[:], bvr[:, half * 512:(half + 1) * 512],
                                         start=False, stop=True)
                    return i
                sq.c("tensor", mmv)

                def evv(e):
                    return e.activation(out=vbuf[:, :, 0:64],
                                        in_=vp[:].rearrange("p (h o) -> p h o", h=16),
                                        func=AF.Copy)
                sq.c("scalar", evv)

                def stv(e, b=b, tt=tt, vout=vout):
                    return [e.dma_start(
                        out=vout[b, :, tt].rearrange("hg p o -> p hg o"),
                        in_=vbuf[:].rearrange("p (hg hl) o -> p hg (hl o)", hg=4))]
                sq.d("sync", stv)



def p2w(cx, sq, es):
    nc = cx.nc
    dr = cx.dram
    sb = lambda n, s, t: es.enter_context(nc.sbuf_tensor(n, s, t))
    ps = lambda n, s: es.enter_context(nc.psum_tensor(n, s, F32))

    wk = sb("p2w_wk", [128, 8, 1024], BF16)
    wv = sb("p2w_wv", [128, 8, 1024], BF16)
    fk = sb("p2w_fk", [128, 8, 1024], BF16)
    fv = sb("p2w_fv", [128, 8, 1024], BF16)
    bk = sb("p2w_bk", [128, 8], F32)
    bvr = sb("p2w_bvr", [1, 1024], BF16)
    onesr = sb("p2w_onesr", [1, 128], BF16)
    ident = sb("p2w_ident", [128, 128], F32)
    ones_col = sb("p2w_onesc", [128, 1], F32)
    ones_row = sb("p2w_onesrf", [1, 128], F32)
    ones16 = sb("p2w_ones16", [128, 16], F32)
    valid = sb("p2w_valid", [128, 8], F32)
    xtile = sb("p2w_xtile", [128, 1024], F32)
    xTg = sb("p2w_xT", [128, 8, T], F32)
    xnb = sb("p2w_xnb", [128, 8, T], BF16)
    kbuf = sb("p2w_kbuf", [128, 8, 512], BF16)
    kbuf2 = sb("p2w_kbuf2", [128, 8, 512], BF16)
    vbufA = sb("p2w_vbufA", [128, 2, 16, 65], BF16)
    vbufF = sb("p2w_vbufF", [128, 2, 16, 65], BF16)
    consts = {"ones_col": ones_col, "ones_row": ones_row}
    tp_p = ps("p2w_tp", [128, 512])
    kp = [ps(f"p2w_kp{i}", [128, 512]) for i in range(2)]
    vp = ps("p2w_vp", [128, 1024])
    P = LnScratch(es, nc, "p2w", repm=kp[0], repr_=kp[1])

    for w, nm in [(wk, "wkT"), (wv, "wvT"), (fk, "fkw"), (fv, "fvw")]:
        cast_load(sq, w[:], dr[nm][:].rearrange("(c p) f -> p c f", p=128))
    cast_load(sq, bvr[:], dr["bv_row"][:])
    cast_load(sq, onesr[:], dr["c_ones_row"][:])
    sq.d("sync", lambda e: [
        e.dma_start(out=bk[:], in_=dr["bk"][:]),
        e.dma_start(out=ident[:], in_=dr["c_ident"][:]),
        e.dma_start(out=ones_col[:], in_=dr["c_ones_col"][:]),
        e.dma_start(out=ones_row[:], in_=dr["c_ones_row"][:]),
        e.dma_start(out=ones16[:], in_=dr["c_ones16"][:]),
        e.dma_start(out=valid[:], in_=dr["valid8"][:]),
    ])
    sq.c("vector", lambda e: e.memset(vbufF[:, :, :, 64:65], 1.0))

    for b in range(B):
        for g in range(8):
            sq.barrier()
            m_ev = {}
            m_tr = {}
            for tt in range(NT):
                p0 = g * 512 + tt * 128
                sq.d("sync", lambda e, b=b, p0=p0: [
                    e.dma_start(out=xtile[:], in_=dr["xfull"][b, p0:p0 + 128, :])])
                m_ld = sq.mark()
                for r in range(2):
                    j = tt * 2 + r
                    tdst = tp_p if j % 2 == 0 else kp[0]

                    def tr(e, r=r, tdst=tdst):
                        for c in range(4):
                            i = e.transpose(tdst[:, c * 128:(c + 1) * 128],
                                            xtile[:, (r * 4 + c) * 128:(r * 4 + c + 1) * 128],
                                            ident[:])
                        return i
                    sq.c("tensor", tr, at=(m_ld if j >= 2 else None))
                    m_tr[j] = sq.mark()

                    def ev(e, tt=tt, r=r, tdst=tdst):
                        return e.tensor_copy(
                            xTg[:, r * 4:(r + 1) * 4, tt * 128:(tt + 1) * 128],
                            tdst[:].rearrange("p (c t) -> p c t", c=4))
                    sq.c("vector", ev, at=m_tr[j])
                    m_ev[j] = sq.mark()
            ln_fm(sq, P, consts, xTg, 8, T, None, xnb)
            m_evk = {}
            m_mm = {}
            for mod in ("a", "f"):
                wmat = wk if mod == "a" else fk
                kout = dr["kaw"] if mod == "a" else dr["kfw"]
                kb = kbuf if mod == "a" else kbuf2
                for fc in range(8):
                    j = (0 if mod == "a" else 8) + fc

                    def mm(e, fc=fc, wmat=wmat, j=j):
                        for dc in range(8):
                            i = e.matmul(kp[j % 2][:], wmat[:, dc, fc * 128:(fc + 1) * 128],
                                         xnb[:, dc], start=(dc == 0), stop=(dc == 7))
                        return i
                    sq.c("tensor", mm, at=(m_evk[j - 2] if j >= 2 else None))
                    m_mm[j] = sq.mark()

                    def evk(e, fc=fc, mod=mod, kb=kb, j=j):
                        if mod == "a":
                            return e.activation(out=kb[:, fc], in_=kp[j % 2][:],
                                                func=AF.Identity, bias=bk[:, fc:fc + 1])
                        return e.activation(out=kb[:, fc], in_=kp[j % 2][:], func=AF.Copy)
                    sq.c("scalar", evk, at=m_mm[j])
                    m_evk[j] = sq.mark()
                sq.d("sync", lambda e, b=b, g=g, kout=kout, kb=kb: [
                    e.dma_start(out=kout[b, :, :, g * 512:(g + 1) * 512]
                                .rearrange("c p t -> p c t"),
                                in_=kb[:])], defer=True, at=m_evk[j])
            m_evv = {}
            for mod in ("a", "f"):
                wv_ = wv if mod == "a" else fv
                vout = dr["vaw"] if mod == "a" else dr["vfw"]
                vb = vbufA if mod == "a" else vbufF
                for tt in range(NT):
                    j = (0 if mod == "a" else NT) + tt
                    par = j % 2

                    def mmv(e, tt=tt, wv_=wv_, mod=mod):
                        for dc in range(8):
                            for half in range(2):
                                i = e.matmul(vp[:, half * 512:(half + 1) * 512],
                                             xnb[:, dc, tt * 128:(tt + 1) * 128],
                                             wv_[:, dc, half * 512:(half + 1) * 512],
                                             start=(dc == 0),
                                             stop=(dc == 7 and mod == "f"))
                        if mod == "a":
                            for half in range(2):
                                i = e.matmul(vp[:, half * 512:(half + 1) * 512],
                                             onesr[:], bvr[:, half * 512:(half + 1) * 512],
                                             start=False, stop=True)
                        return i
                    sq.c("tensor", mmv, at=(m_evv[j - 1] if j >= 1 else None))
                    m_mmv = sq.mark()
                    if mod == "a":
                        sq.c("vector", lambda e, g=g, par=par: e.tensor_single_scalar(
                            vbufA[:, par, :, 64:65].rearrange("p h o -> p (h o)"),
                            ones16[:], valid[:, g:g + 1], ALU.mult), at=m_mmv)

                    def evv(e, vb=vb, g=g, mod=mod, par=par):
                        if mod == "a":
                            return e.activation(out=vb[:, par, :, 0:64],
                                                in_=vp[:].rearrange("p (h o) -> p h o", h=16),
                                                func=AF.Copy, scale=valid[:, g:g + 1])
                        return e.activation(out=vb[:, par, :, 0:64],
                                            in_=vp[:].rearrange("p (h o) -> p h o", h=16),
                                            func=AF.Copy)
                    sq.c("scalar", evv, at=m_mmv)
                    m_evv[j] = sq.mark()

                    def stv(e, b=b, g=g, tt=tt, vout=vout, vb=vb, par=par):
                        return [e.dma_start(
                            out=vout[b, :, g * NT + tt].rearrange("hg p o -> p hg o"),
                            in_=vb[:, par].rearrange("p (hg hl) o -> p hg (hl o)", hg=4))]
                    sq.d("sync", stv, defer=True, at=m_evv[j])
    sq.barrier()



def p3(cx, sq, es):
    nc = cx.nc
    dr = cx.dram
    sb = lambda n, s, t: es.enter_context(nc.sbuf_tensor(n, s, t))
    ps = lambda n, s: es.enter_context(nc.psum_tensor(n, s, F32))

    wq = sb("p3_wq", [128, 8, 1024], BF16)
    fq = sb("p3_fq", [128, 8, 1024], BF16)
    swb = sb("p3_swb", [128, 8, 2], BF16)
    bpair = sb("p3_bpair", [2, 128], F32)
    bq8 = sb("p3_bq8", [128, 8], F32)
    xnb = sb("p3_xnb", [128, 8, T], BF16)
    qfb = sb("p3_qfb", [128, 8, T], BF16)
    qbuf = sb("p3_qbuf", [128, 512], BF16)
    srow = sb("p3_srow", [2, 512], F32)
    qp = ps("p3_qp", [128, 512])
    sp = ps("p3_sp", [2, 512])
    rp = ps("p3_rp", [128, 512])

    cast_load(sq, wq[:], dr["wqT"][:].rearrange("(c p) f -> p c f", p=128))
    cast_load(sq, fq[:], dr["fqw"][:].rearrange("(c p) f -> p c f", p=128))
    cast_load(sq, swb[:], dr["swbd"][:].rearrange("(c p) f -> p c f", p=128))
    sq.d("sync", lambda e: [e.dma_start(out=bq8[:], in_=dr["bq"][:]),
                            e.dma_start(out=bpair[:], in_=dr["c_bpair"][:])])
    sq.c("scalar", lambda e: e.mul(bq8[:], bq8[:], 0.125))

    for b in range(B):
        sq.d("sync", lambda e, b=b: [
            e.dma_start(out=xnb[:], in_=dr["xnbf"][b].rearrange("c p t -> p c t"))])
        for fc in range(8):
            def mm(e, fc=fc):
                for dc in range(8):
                    i = e.matmul(qp[:], wq[:, dc, fc * 128:(fc + 1) * 128],
                                 xnb[:, dc], start=(dc == 0), stop=(dc == 7))
                return i
            sq.c("tensor", mm)
            sq.c("scalar", lambda e, fc=fc: e.activation(
                out=qbuf[:], in_=qp[:], func=AF.Identity, scale=0.125,
                bias=bq8[:, fc:fc + 1]))
            sq.d("sync", lambda e, b=b, fc=fc: [
                e.dma_start(out=dr["qa"][b, fc], in_=qbuf[:])])

            def mmf(e, fc=fc):
                for dc in range(8):
                    i = e.matmul(qp[:], fq[:, dc, fc * 128:(fc + 1) * 128],
                                 xnb[:, dc], start=(dc == 0), stop=(dc == 7))
                return i
            sq.c("tensor", mmf)
            sq.c("scalar", lambda e, fc=fc: e.activation(
                out=qfb[:, fc], in_=qp[:], func=AF.Copy))
        for fc in range(8):
            sq.c("tensor", lambda e, fc=fc: e.matmul(
                sp[:], swb[:, fc], qfb[:, fc], start=True, stop=True))
            sq.c("scalar", lambda e: e.activation(out=srow[:], in_=sp[:], func=AF.Sigmoid))
            sq.c("vector", lambda e: e.tensor_single_scalar(srow[:], srow[:], 0.25, ALU.mult))
            sq.c("tensor", lambda e: e.matmul(rp[:], bpair[:], srow[:], start=True, stop=True))
            sq.c("vector", lambda e, fc=fc: e.tensor_tensor(qbuf[:], qfb[:, fc], rp[:], ALU.mult))
            sq.d("sync", lambda e, b=b, fc=fc: [
                e.dma_start(out=dr["qf"][b, fc], in_=qbuf[:])])



def attention_l0(cx, sq, es, tag, qname, kname, vname, nmask, att_sb):
    raise NotImplementedError


def p4_common(cx, sq, es, tag, qname, kname, vname, nmask, post_cb, diag=None):
    nc = cx.nc
    dr = cx.dram
    sb = lambda n, s, t: es.enter_context(nc.sbuf_tensor(n, s, t))
    ps = lambda n, s: es.enter_context(nc.psum_tensor(n, s, F32))

    kline = sb(f"{tag}_kline", [128, 2, S], BF16)
    vline = sb(f"{tag}_vline", [128, NKC, 260], BF16)
    qpair = sb(f"{tag}_qpair", [128, 2, 512], BF16)
    pt = sb(f"{tag}_pt", [128, 2, 2, 512], BF16)
    masks = sb(f"{tag}_masks", [128, 4, 512], BF16)
    rrow = sb(f"{tag}_rrow", [1, 512], F32)
    repsb = sb(f"{tag}_reps", [64, 512], F32)
    ones64 = sb(f"{tag}_o64", [1, 64], F32)
    att_sb = sb(f"{tag}_att", [128, 8, T], BF16)
    sAb = [ps(f"{tag}_sA{i}", [128, 512]) for i in range(2)]
    sBb = [ps(f"{tag}_sB{i}", [128, 512]) for i in range(2)]
    av = [ps(f"{tag}_av{i}", [65, 512]) for i in range(4)]

    if diag is not None:
        klineD = sb(f"{tag}_klineD", [128, 2, 512], BF16)
        vlineD = sb(f"{tag}_vlineD", [128, NT, 260], BF16)
    sq.d("sync", lambda e: [e.dma_start(out=ones64[:], in_=dr["c_ones64"][:])])
    if nmask or diag is not None:
        sq.d("gpsimd", lambda e: [e.dma_start(
            out=masks[:], in_=dr["c_diagmask"][:].rearrange("j p t -> p j t"))])

    npos = NKC + (NT if diag is not None else 0)
    taus = [(pos, hp) for pos in range(npos) for hp in range(2)]
    ntau = len(taus)

    for b in range(B):
        for hg in range(4):
            def loads(e, b=b, hg=hg):
                out = [
                    e.dma_start(out=kline[:, 0], in_=dr[kname][b, 2 * hg]),
                    e.dma_start(out=kline[:, 1], in_=dr[kname][b, 2 * hg + 1]),
                    e.dma_start(out=vline[:],
                                in_=dr[vname][b, hg].rearrange("k p o -> p k o")),
                    e.dma_start(out=qpair[:, 0], in_=dr[qname][b, 2 * hg]),
                    e.dma_start(out=qpair[:, 1], in_=dr[qname][b, 2 * hg + 1]),
                ]
                if diag is not None:
                    out.append(e.dma_start(out=klineD[:, 0], in_=dr[diag[0]][b, 2 * hg]))
                    out.append(e.dma_start(out=klineD[:, 1], in_=dr[diag[0]][b, 2 * hg + 1]))
                    out.append(e.dma_start(
                        out=vlineD[:], in_=dr[diag[1]][b, hg].rearrange("k p o -> p k o")))
                return out
            sq.d("sync", loads)

            def mk_smm(tau):
                pos, hp = taus[tau]
                dpos = pos - NKC
                kl = kline if dpos < 0 else klineD
                kidx = pos if dpos < 0 else dpos
                par = tau % 2

                def smm(e, kl=kl, kidx=kidx, hp=hp, par=par):
                    e.matmul(sAb[par][:], kl[0:64, hp, kidx * 128:(kidx + 1) * 128],
                             qpair[0:64, hp], start=True, stop=True,
                             tile_position=(0, 0))
                    return e.matmul(sBb[par][:],
                                    kl[64:128, hp, kidx * 128:(kidx + 1) * 128],
                                    qpair[64:128, hp], start=True, stop=True,
                                    tile_position=(64, 0))
                return smm

            m_ex = {}
            m_smm = {}
            sq.c("tensor", mk_smm(0))
            m_smm[0] = sq.mark()
            for tau in range(ntau):
                pos, hp = taus[tau]
                dpos = pos - NKC
                vl = vline if dpos < 0 else vlineD
                kidx = pos if dpos < 0 else dpos
                masked = (pos < nmask) or (dpos >= 0)
                midx = pos if dpos < 0 else dpos
                par = tau % 2

                def ex(e, par=par):
                    e.activation(out=pt[:, par, 0], in_=sAb[par][:], func=AF.Exp)
                    return e.activation(out=pt[:, par, 1], in_=sBb[par][:],
                                        func=AF.Exp)
                sq.c("scalar", ex, at=(m_smm[tau] if tau >= 2 else None))
                m_ex[tau] = sq.mark()
                dep = m_ex[tau]
                if masked:
                    def mk(e, midx=midx, par=par):
                        e.tensor_tensor(pt[:, par, 0], pt[:, par, 0],
                                        masks[:, midx], ALU.mult)
                        return e.tensor_tensor(pt[:, par, 1], pt[:, par, 1],
                                               masks[:, midx], ALU.mult)
                    sq.c("gpsimd", mk, at=m_ex[tau])
                    dep = sq.mark()
                if tau + 1 < ntau:
                    sq.c("tensor", mk_smm(tau + 1),
                         at=(m_ex[tau - 1] if tau >= 1 else None))
                    m_smm[tau + 1] = sq.mark()

                def avm(e, vl=vl, kidx=kidx, pos=pos, hp=hp, par=par):
                    for j in range(2):
                        hl = 2 * hp + j
                        i = e.matmul(av[hl][:],
                                     vl[:, kidx, hl * 65:hl * 65 + 65],
                                     pt[:, par, j], start=(pos == 0),
                                     stop=(pos == npos - 1))
                    return i
                sq.c("tensor", avm, at=(dep if tau >= 2 else None))
            for hl in range(4):
                sq.c("vector", lambda e, hl=hl: e.reciprocal(rrow[:], av[hl][64:65, :]))
                sq.c("tensor", lambda e: e.matmul(sBb[1][0:64, :], ones64[:], rrow[:],
                                                  start=True, stop=True))
                sq.c("scalar", lambda e: e.activation(out=repsb[:], in_=sBb[1][0:64, :],
                                                      func=AF.Copy))

                def div(e, hl=hl, hg=hg):
                    c = 2 * hg + hl // 2
                    r0 = (hl % 2) * 64
                    return e.tensor_tensor(att_sb[r0:r0 + 64, c], av[hl][0:64, :],
                                           repsb[:], ALU.mult)
                sq.c("vector", div)
        post_cb(b, att_sb, sAb[0])


def p4a(cx, sq, es):
    nc = cx.nc
    dr = cx.dram
    sb = lambda n, s, t: es.enter_context(nc.sbuf_tensor(n, s, t))
    ps = lambda n, s: es.enter_context(nc.psum_tensor(n, s, F32))
    wo = sb("p4a_wo", [128, 8, 1024], BF16)
    bo = sb("p4a_bo", [128, 8], F32)
    obuf = sb("p4a_obuf", [128, 512], F32)
    cast_load(sq, wo[:], dr["woT"][:].rearrange("(c p) f -> p c f", p=128))
    sq.d("sync", lambda e: [e.dma_start(out=bo[:], in_=dr["bo"][:])])

    def post(b, att_sb, scr):
        for fc in range(8):
            def mm(e, fc=fc, att_sb=att_sb, scr=scr):
                for dc in range(8):
                    i = e.matmul(scr[:], wo[:, dc, fc * 128:(fc + 1) * 128],
                                 att_sb[:, dc], start=(dc == 0), stop=(dc == 7))
                return i
            sq.c("tensor", mm)
            sq.c("scalar", lambda e, fc=fc, scr=scr: e.activation(
                out=obuf[:], in_=scr[:], func=AF.Identity, bias=bo[:, fc:fc + 1]))
            sq.d("sync", lambda e, b=b, fc=fc: [
                e.dma_start(out=dr["attn0T"][b, fc], in_=obuf[:])])
    p4_common(cx, sq, es, "p4a", "qa", "kattn_r", "vattn_r", 4, post)


def p4af(cx, sq, es):
    nc = cx.nc
    dr = cx.dram
    sb = lambda n, s, t: es.enter_context(nc.sbuf_tensor(n, s, t))
    ps = lambda n, s: es.enter_context(nc.psum_tensor(n, s, F32))
    wo = sb("p4a_wo", [128, 8, 1024], BF16)
    bo = sb("p4a_bo", [128, 8], F32)
    obuf = sb("p4a_obuf", [128, 512], F32)
    cast_load(sq, wo[:], dr["woT"][:].rearrange("(c p) f -> p c f", p=128))
    sq.d("sync", lambda e: [e.dma_start(out=bo[:], in_=dr["bo"][:])])

    def post(b, att_sb, scr):
        for fc in range(8):
            def mm(e, fc=fc, att_sb=att_sb, scr=scr):
                for dc in range(8):
                    i = e.matmul(scr[:], wo[:, dc, fc * 128:(fc + 1) * 128],
                                 att_sb[:, dc], start=(dc == 0), stop=(dc == 7))
                return i
            sq.c("tensor", mm)
            sq.c("scalar", lambda e, fc=fc, scr=scr: e.activation(
                out=obuf[:], in_=scr[:], func=AF.Identity, bias=bo[:, fc:fc + 1]))
            sq.d("sync", lambda e, b=b, fc=fc: [
                e.dma_start(out=dr["attn0T"][b, fc], in_=obuf[:])])
    p4_common(cx, sq, es, "p4a", "qa", "kaw", "vaw", 0, post,
              diag=("kattn", "vattn"))


def p4f(cx, sq, es):
    nc = cx.nc
    dr = cx.dram
    sb = lambda n, s, t: es.enter_context(nc.sbuf_tensor(n, s, t))
    ps = lambda n, s: es.enter_context(nc.psum_tensor(n, s, F32))
    gw = sb("p4f_gw", [128, 8, 1024], BF16)
    ow = sb("p4f_ow", [128, 8, 1024], BF16)
    gb = sb("p4f_gb", [128, 8], F32)
    gt = sb("p4f_gt", [128, 8, T], BF16)
    go = sb("p4f_go", [128, 8, T], BF16)
    obuf = sb("p4f_obuf", [128, 512], F32)
    cast_load(sq, gw[:], dr["fgw"][:].rearrange("(c p) f -> p c f", p=128))
    cast_load(sq, ow[:], dr["fow"][:].rearrange("(c p) f -> p c f", p=128))
    sq.d("sync", lambda e: [e.dma_start(out=gb[:], in_=dr["fgb"][:])])

    def post(b, att_sb, scr):
        for fc in range(8):
            def mm(e, fc=fc, att_sb=att_sb, scr=scr):
                for dc in range(8):
                    i = e.matmul(scr[:], gw[:, dc, fc * 128:(fc + 1) * 128],
                                 att_sb[:, dc], start=(dc == 0), stop=(dc == 7))
                return i
            sq.c("tensor", mm)
            sq.c("scalar", lambda e, fc=fc, scr=scr: e.activation(
                out=gt[:, fc], in_=scr[:], func=AF.Sigmoid, bias=gb[:, fc:fc + 1]))

        def gmul(e, att_sb=att_sb):
            for fc in range(8):
                i = e.tensor_tensor(go[:, fc], att_sb[:, fc], gt[:, fc], ALU.mult)
            return i
        sq.c("vector", gmul)
        for fc in range(8):
            def mm2(e, fc=fc, scr=scr):
                for dc in range(8):
                    i = e.matmul(scr[:], ow[:, dc, fc * 128:(fc + 1) * 128],
                                 go[:, dc], start=(dc == 0), stop=(dc == 7))
                return i
            sq.c("tensor", mm2)
            sq.c("scalar", lambda e, fc=fc, scr=scr: e.activation(
                out=obuf[:], in_=scr[:], func=AF.Copy))
            sq.d("sync", lambda e, b=b, fc=fc: [
                e.dma_start(out=dr["fattn0T"][b, fc], in_=obuf[:])])
    p4_common(cx, sq, es, "p4f", "qf", "kffn_g", "vffn_g", 0, post)


def p4ff(cx, sq, es):
    nc = cx.nc
    dr = cx.dram
    sb = lambda n, s, t: es.enter_context(nc.sbuf_tensor(n, s, t))
    ps = lambda n, s: es.enter_context(nc.psum_tensor(n, s, F32))
    gw = sb("p4f_gw", [128, 8, 1024], BF16)
    ow = sb("p4f_ow", [128, 8, 1024], BF16)
    gb = sb("p4f_gb", [128, 8], F32)
    gt = sb("p4f_gt", [128, 8, T], BF16)
    go = sb("p4f_go", [128, 8, T], BF16)
    obuf = sb("p4f_obuf", [128, 512], F32)
    cast_load(sq, gw[:], dr["fgw"][:].rearrange("(c p) f -> p c f", p=128))
    cast_load(sq, ow[:], dr["fow"][:].rearrange("(c p) f -> p c f", p=128))
    sq.d("sync", lambda e: [e.dma_start(out=gb[:], in_=dr["fgb"][:])])

    def post(b, att_sb, scr):
        for fc in range(8):
            def mm(e, fc=fc, att_sb=att_sb, scr=scr):
                for dc in range(8):
                    i = e.matmul(scr[:], gw[:, dc, fc * 128:(fc + 1) * 128],
                                 att_sb[:, dc], start=(dc == 0), stop=(dc == 7))
                return i
            sq.c("tensor", mm)
            sq.c("scalar", lambda e, fc=fc, scr=scr: e.activation(
                out=gt[:, fc], in_=scr[:], func=AF.Sigmoid, bias=gb[:, fc:fc + 1]))

        def gmul(e, att_sb=att_sb):
            for fc in range(8):
                i = e.tensor_tensor(go[:, fc], att_sb[:, fc], gt[:, fc], ALU.mult)
            return i
        sq.c("vector", gmul)
        for fc in range(8):
            def mm2(e, fc=fc, scr=scr):
                for dc in range(8):
                    i = e.matmul(scr[:], ow[:, dc, fc * 128:(fc + 1) * 128],
                                 go[:, dc], start=(dc == 0), stop=(dc == 7))
                return i
            sq.c("tensor", mm2)
            sq.c("scalar", lambda e, fc=fc, scr=scr: e.activation(
                out=obuf[:], in_=scr[:], func=AF.Copy))
            sq.d("sync", lambda e, b=b, fc=fc: [
                e.dma_start(out=dr["fattn0T"][b, fc], in_=obuf[:])])
    p4_common(cx, sq, es, "p4f", "qf", "kfw", "vfw", 0, post)



def p5(cx, sq, es):
    nc = cx.nc
    dr = cx.dram
    sb = lambda n, s, t: es.enter_context(nc.sbuf_tensor(n, s, t))

    z = sb("p5_z", [128, 8, T], F32)
    fat = sb("p5_fat", [128, 8, T], F32)
    xr32 = sb("p5_xr32", [128, 8, T], F32)
    xrbf = sb("p5_xrbf", [128, 8, T], BF16)
    g = sb("p5_g", [128, 8], F32)
    bb = sb("p5_b", [128, 8], F32)
    ones_col = sb("p5_onesc", [128, 1], F32)
    ones_row = sb("p5_onesr", [1, 128], F32)
    P = LnScratch(es, nc, "p5")
    consts = {"ones_col": ones_col, "ones_row": ones_row}

    sq.d("sync", lambda e: [
        e.dma_start(out=g[:], in_=dr["n1g"][:]),
        e.dma_start(out=bb[:], in_=dr["n1b"][:]),
        e.dma_start(out=ones_col[:], in_=dr["c_ones_col"][:]),
        e.dma_start(out=ones_row[:], in_=dr["c_ones_row"][:]),
    ])
    for b in range(B):
        sq.d("sync", lambda e, b=b: [
            e.dma_start(out=z[:], in_=dr["xnf32"][b].rearrange("c p t -> p c t")),
            e.dma_start(out=fat[:], in_=dr["fattn0T"][b].rearrange("c p t -> p c t")),
        ])

        def add(e):
            for c in range(8):
                i = e.tensor_tensor(z[:, c], z[:, c], fat[:, c], ALU.add)
            return i
        sq.c("vector", add)
        ln_fm(sq, P, consts, z, 8, T, xr32, xrbf, gamma=g, beta=bb)
        sq.d("sync", lambda e, b=b: [
            e.dma_start(out=dr["xrT32"][b].rearrange("c p t -> p c t"), in_=xr32[:]),
            e.dma_start(out=dr["xrTbf"][b].rearrange("c p t -> p c t"), in_=xrbf[:]),
        ])



def p7(cx, sq, es):
    nc = cx.nc
    dr = cx.dram
    sb = lambda n, s, t: es.enter_context(nc.sbuf_tensor(n, s, t))

    iota = sb("p7_iota", [128, 8], F32)
    ident = sb("p7_ident", [128, 128], F32)
    identb = sb("p7_identb", [128, 128], BF16)
    ones_col = sb("p7_onesc", [128, 1], F32)
    onescb = sb("p7_onescb", [128, 1], BF16)
    ones_row = sb("p7_onesr", [1, 128], F32)
    rsc_row = sb("p7_rscrow", [1, 8], F32)
    rsc_col = sb("p7_rsccol", [8, 1], F32)
    b2sb = sb("p7_b2", [8, 1024], F32)
    b2rsc = sb("p7_b2rsc", [8, 1024], BF16)
    mrsc_sb = sb("p7_mrsc", [128, B * NT, 8], F32)
    mbf_sb = sb("p7_mbf", [128, B * NT, 8], BF16)
    rscrep = sb("p7_rscrep", [128, 8], F32)
    consts = {"ones_col": ones_col, "ones_row": ones_row}

    cast_load(sq, identb[:], dr["c_ident"][:])
    cast_load(sq, onescb[:], dr["c_ones_col"][:])
    sq.d("sync", lambda e: [
        e.dma_start(out=iota[:], in_=dr["c_iota8"][:]),
        e.dma_start(out=ident[:], in_=dr["c_ident"][:]),
        e.dma_start(out=ones_col[:], in_=dr["c_ones_col"][:]),
        e.dma_start(out=ones_row[:], in_=dr["c_ones_row"][:]),
        e.dma_start(out=rsc_row[:], in_=dr["rsc_row"][:]),
        e.dma_start(out=rsc_col[:], in_=dr["rsc_col"][:]),
        e.dma_start(out=b2sb[:], in_=dr["b2"][:]),
    ])
    sq.c("vector", lambda e: e.tensor_single_scalar(b2rsc[:], b2sb[:], rsc_col[:],
                                                    ALU.mult))

    es1 = ExitStack()
    sb1 = lambda n, s, t: es1.enter_context(nc.sbuf_tensor(n, s, t))
    ps1 = lambda n, s: es1.enter_context(nc.psum_tensor(n, s, F32))
    mgw = sb1("p7_mgw", [128, 8, 8], F32)
    xr32r = sb1("p7_xr32r", [128, 8, T], F32)
    mgb = sb1("p7_mgb", [8, 1], F32)
    lgs = sb1("p7_lgs", [8, 512], F32)
    ltm = sb1("p7_ltm", [128, 8], F32)
    mx8 = sb1("p7_mx8", [128, 8], F32)
    ge2 = sb1("p7_ge2", [128, 8], F32)
    chosen = sb1("p7_chosen", [128, 1], F32)
    lgp = ps1("p7_lgp", [8, 512])
    ltp = ps1("p7_ltp", [128, 32])

    sq.d("sync", lambda e: [
        e.dma_start(out=mgw[:], in_=dr["mgw"][:].rearrange("(c p) f -> p c f", p=128)),
        e.dma_start(out=mgb[:], in_=dr["mgb"][:])])
    sq.c("tensor", lambda e: e.matmul(ltp[:, 0:8], ones_row[:], rsc_row[:],
                                      start=True, stop=True))
    sq.c("vector", lambda e: e.tensor_copy(rscrep[:], ltp[:, 0:8]))

    for b in range(B):
        sq.d("sync", lambda e, b=b: [
            e.dma_start(out=xr32r[:], in_=dr["xrT32"][b].rearrange("c p t -> p c t"))])

        def lgmm(e):
            for dc in range(8):
                i = e.matmul(lgp[:], mgw[:, dc], xr32r[:, dc], start=(dc == 0),
                             stop=(dc == 7))
            return i
        sq.c("tensor", lgmm)
        sq.c("scalar", lambda e: e.activation(out=lgs[:], in_=lgp[:], func=AF.Identity,
                                              bias=mgb[:]))

        def ltr(e):
            for tt in range(NT):
                i = e.transpose(ltp[:, tt * 8:(tt + 1) * 8],
                                lgs[:, tt * 128:(tt + 1) * 128], ident[0:8, 0:8])
            return i
        sq.c("tensor", ltr)
        for tt in range(NT):
            sq.c("vector", lambda e, tt=tt: e.tensor_copy(
                ltm[:], ltp[:, tt * 8:(tt + 1) * 8]))
            sq.c("vector", lambda e: e.max(out=mx8[:], in_=ltm[:]))
            sq.c("vector", lambda e: e.tensor_single_scalar(
                ge2[:], ltm[:], mx8[:, 1:2], ALU.is_ge))
            sq.c("vector", lambda e: e.tensor_tensor(ge2[:], ge2[:], iota[:],
                                                     ALU.mult))
            sq.c("vector", lambda e: e.reduce_max(chosen[:], ge2[:], axis=AX.X))
            sq.c("vector", lambda e, b=b, tt=tt: e.tensor_single_scalar(
                mbf_sb[:, b * NT + tt], iota[:], chosen[:], ALU.is_equal))
            sq.c("vector", lambda e, b=b, tt=tt: e.tensor_single_scalar(
                mrsc_sb[:, b * NT + tt], iota[:], chosen[:], ALU.is_equal))
            sq.c("vector", lambda e, b=b, tt=tt: e.tensor_tensor(
                mrsc_sb[:, b * NT + tt], mrsc_sb[:, b * NT + tt], rscrep[:],
                ALU.mult))
        sq.d("sync", lambda e, b=b: [
            e.dma_start(out=dr["mrsc"][b].rearrange("n p f -> p n f"),
                        in_=mrsc_sb[:, b * NT:(b + 1) * NT]),
            e.dma_start(out=dr["mbf"][b].rearrange("n p f -> p n f"),
                        in_=mbf_sb[:, b * NT:(b + 1) * NT]),
        ])
    es1.close()

    es2 = ExitStack()
    sb2 = lambda n, s, t: es2.enter_context(nc.sbuf_tensor(n, s, t))
    ps2 = lambda n, s: es2.enter_context(nc.psum_tensor(n, s, F32))
    w1sb = sb2("p7_w1", [128, 8, FF], BF16)
    w2sb = sb2("p7_w2", [128, 16, 1024], BF16)
    b1 = sb2("p7_b1", [128, 16], F32)
    lng = sb2("p7_lng", [128, 16], F32)
    lnb = sb2("p7_lnb", [128, 16], F32)
    hsb = sb2("p7_h", [128, 16, 512], BF16)
    hsq = sb2("p7_hsq", [128, 16, 512], BF16)
    gsb = sb2("p7_g", [128, 16, 512], BF16)
    rows = sb2("p7_erows", [1, 4, 512], F32)
    msb = sb2("p7_m", [128, 512], F32)
    rsb = sb2("p7_r", [128, 512], F32)
    ysel = sb2("p7_ysel", [128, 2, 1024], F32)
    s1p = ps2("p7_es1", [1, 512])
    s2p = ps2("p7_es2", [1, 512])
    hp = [ps2(f"p7_hp{i}", [128, 512]) for i in range(2)]
    yp = [ps2(f"p7_yp{i}", [128, 1024]) for i in range(2)]
    repm = hp[0]
    repr_ = hp[1]

    xrb4 = sb2("p7_xrb4", [128, B, 8, T], BF16)
    for b in range(B):
        sq.d("sync", lambda e, b=b: [
            e.dma_start(out=xrb4[:, b], in_=dr["xrTbf"][b].rearrange("c p t -> p c t"))])

    m_bnd = None
    for e_i in range(E):
        sq.d("gpsimd", lambda e, e_i=e_i: [
            e.dma_start(out=w1sb[:], in_=dr["w1"][e_i].rearrange("(c p) f -> p c f", p=128)),
            e.dma_start(out=w2sb[:], in_=dr["w2"][e_i].rearrange("(c p) f -> p c f", p=128)),
        ])
        sq.d("sync", lambda e, e_i=e_i: [
            e.dma_start(out=b1[:], in_=dr["b1c"][e_i]),
            e.dma_start(out=lng[:], in_=dr["lngc"][e_i]),
            e.dma_start(out=lnb[:], in_=dr["lnbc"][e_i]),
        ])
        m_bnd = None
        for b in range(B):
            m_evh = {}
            m_mmh = {}
            for fc in range(16):
                def mmh(e, fc=fc, b=b):
                    for dc in range(8):
                        i = e.matmul(hp[fc % 2][:], w1sb[:, dc, fc * 128:(fc + 1) * 128],
                                     xrb4[:, b, dc], start=(dc == 0), stop=(dc == 7))
                    return i
                sq.c("tensor", mmh,
                     at=(m_evh[fc - 2] if fc >= 2 else m_bnd))
                m_mmh[fc] = sq.mark()

                def evh(e, fc=fc):
                    e.activation(out=hsb[:, fc], in_=hp[fc % 2][:], func=AF.Identity,
                                 bias=b1[:, fc:fc + 1])
                    return e.activation(out=hsq[:, fc], in_=hp[fc % 2][:], func=AF.Square,
                                        bias=b1[:, fc:fc + 1])
                sq.c("scalar", evh, at=m_mmh[fc])
                m_evh[fc] = sq.mark()

            def stats(e):
                for fc in range(16):
                    e.matmul(s1p[:], onescb[:], hsb[:, fc], start=(fc == 0), stop=(fc == 15))
                for fc in range(16):
                    i = e.matmul(s2p[:], onescb[:], hsq[:, fc], start=(fc == 0),
                                 stop=(fc == 15))
                return i
            sq.c("tensor", stats)

            sq.c("vector", lambda e: e.tensor_single_scalar(rows[:, 0], s1p[:],
                                                            1.0 / FF, ALU.mult))
            sq.c("vector", lambda e: e.tensor_tensor(rows[:, 1], rows[:, 0],
                                                     rows[:, 0], ALU.mult))
            sq.c("vector", lambda e: e.scalar_tensor_tensor(
                out=rows[:, 2], in0=s2p[:], scalar=1.0 / FF, in1=rows[:, 1],
                op0=ALU.mult, op1=ALU.subtract))
            sq.c("vector", lambda e: e.tensor_single_scalar(rows[:, 2], rows[:, 2],
                                                            EPS, ALU.add))
            sq.c("scalar", lambda e: e.activation(out=rows[:, 1], in_=rows[:, 2],
                                                  func=AF.Ln))
            sq.c("scalar", lambda e: e.activation(out=rows[:, 3], in_=rows[:, 1],
                                                  func=AF.Exp, scale=-0.5))

            def reps(e):
                e.matmul(repm[:], ones_row[:], rows[:, 0], start=True, stop=True)
                return e.matmul(repr_[:], ones_row[:], rows[:, 3], start=True, stop=True)
            sq.c("tensor", reps)

            def evreps(e):
                e.tensor_copy(msb[:], repm[:])
                return e.tensor_copy(rsb[:], repr_[:])
            sq.c("vector", evreps)
            m_bnd = sq.mark()

            def normh1(e):
                for fc in range(16):
                    i = e.tensor_tensor(hsb[:, fc], hsb[:, fc], msb[:], ALU.subtract)
                return i
            sq.c("vector", normh1)

            def normh2(e):
                for fc in range(16):
                    i = e.tensor_tensor(hsb[:, fc], hsb[:, fc], rsb[:], ALU.mult)
                return i
            sq.c("vector", normh2)

            def gelu(e):
                for fc in range(16):
                    i = e.activation(out=gsb[:, fc], in_=hsb[:, fc], func=AF.Gelu,
                                     scale=lng[:, fc:fc + 1], bias=lnb[:, fc:fc + 1])
                return i
            sq.c("scalar", gelu)
            m_gelu = sq.mark()

            m_sel = {}
            for tt in range(NT):
                def mmy(e, tt=tt):
                    for fc in range(16):
                        for half in range(2):
                            i = e.matmul(yp[tt % 2][:, half * 512:(half + 1) * 512],
                                         gsb[:, fc, tt * 128:(tt + 1) * 128],
                                         w2sb[:, fc, half * 512:(half + 1) * 512],
                                         start=(fc == 0), stop=(fc == 15))
                    return i
                sq.c("tensor", mmy,
                     at=(m_sel[tt - 2] if tt >= 2 else m_gelu))
                m_mmy = sq.mark()

                def sel(e, b=b, tt=tt, e_i=e_i):
                    return e.tensor_single_scalar(
                        ysel[:, tt % 2], yp[tt % 2][:],
                        mrsc_sb[:, b * NT + tt, e_i:e_i + 1], ALU.mult)
                sq.c("vector", sel, at=m_mmy)
                m_sel[tt] = sq.mark()
                sq.d("gpsimd", lambda e, b=b, tt=tt, e_i=e_i: [
                    e.dma_start(out=dr["selT"][b, tt], in_=ysel[:, tt % 2],
                                accum_op=(ALU.bypass if e_i == 0 else ALU.add))],
                     at=m_sel[tt])
    es2.close()

    es3 = ExitStack()
    sb3 = lambda n, s, t: es3.enter_context(nc.sbuf_tensor(n, s, t))
    ps3 = lambda n, s: es3.enter_context(nc.psum_tensor(n, s, F32))
    xr32 = sb3("p7_xr32", [128, 8, T], F32)
    selsb = sb3("p7_sel", [128, 1024], F32)
    selfm = sb3("p7_selfm", [128, 8, T], F32)
    ffout = sb3("p7_ffout", [128, 8, T], F32)
    ohot = sb3("p7_ohot", [8, 128], BF16)
    n2g = sb3("p7_n2g", [128, 8], F32)
    n2b = sb3("p7_n2b", [128, 8], F32)
    P = LnScratch(es3, nc, "p7")
    tp2 = es3.enter_context(nc.psum_tensor("p7_tp2", [8, 128], BF16))
    yp2 = ps3("p7_yp2", [128, 1024])
    trp = ps3("p7_trp", [128, 512])

    sq.d("sync", lambda e: [
        e.dma_start(out=n2g[:], in_=dr["n2g"][:]),
        e.dma_start(out=n2b[:], in_=dr["n2b"][:]),
    ])
    for b in range(B):
        sq.d("sync", lambda e, b=b: [
            e.dma_start(out=xr32[:], in_=dr["xrT32"][b].rearrange("c p t -> p c t"))])
        for tt in range(NT):
            sq.d("sync", lambda e, b=b, tt=tt: [
                e.dma_start(out=selsb[:], in_=dr["selT"][b, tt])])

            def trm(e, b=b, tt=tt):
                return e.transpose(tp2[:], mbf_sb[:, b * NT + tt], identb[:])
            sq.c("tensor", trm)
            sq.c("vector", lambda e: e.tensor_copy(ohot[:], tp2[:]))

            def term(e):
                for half in range(2):
                    i = e.matmul(yp2[:, half * 512:(half + 1) * 512], ohot[:],
                                 b2rsc[:, half * 512:(half + 1) * 512],
                                 start=True, stop=True)
                return i
            sq.c("tensor", term)
            sq.c("vector", lambda e: e.tensor_tensor(selsb[:], selsb[:], yp2[:], ALU.add))

            for grp in range(2):
                def trsel(e, grp=grp):
                    for dc in range(4):
                        c = grp * 4 + dc
                        i = e.transpose(trp[:, dc * 128:(dc + 1) * 128],
                                        selsb[:, c * 128:(c + 1) * 128], ident[:])
                    return i
                sq.c("tensor", trsel)

                def evsel(e, grp=grp, tt=tt):
                    return e.tensor_copy(
                        selfm[:, grp * 4:(grp + 1) * 4, tt * 128:(tt + 1) * 128],
                        trp[:].rearrange("p (c t) -> p c t", c=4))
                sq.c("vector", evsel)

        def addxr(e):
            for c in range(8):
                i = e.scalar_tensor_tensor(out=selfm[:, c], in0=xr32[:, c],
                                           scalar=2.0, in1=selfm[:, c],
                                           op0=ALU.mult, op1=ALU.add)
            return i
        sq.c("vector", addxr)
        ln_fm(sq, P, consts, selfm, 8, T, ffout, None, gamma=n2g, beta=n2b)
        sq.d("sync", lambda e, b=b: [
            e.dma_start(out=dr["ffn0T"][b].rearrange("c p t -> p c t"), in_=ffout[:])])
    es3.close()



def p8(cx, sq, es):
    nc = cx.nc
    dr = cx.dram
    sb = lambda n, s, t: es.enter_context(nc.sbuf_tensor(n, s, t))

    x1 = sb("p8_x1", [128, 8, T], F32)
    t2 = sb("p8_t2", [128, 8, T], F32)
    xn1 = sb("p8_xn1", [128, 8, B, T], BF16)
    ones_col = sb("p8_onesc", [128, 1], F32)
    ones_row = sb("p8_onesr", [1, 128], F32)
    P = LnScratch(es, nc, "p8")
    consts = {"ones_col": ones_col, "ones_row": ones_row}
    xp = sb("p8_xp", [128, 4 * T], BF16)

    sq.d("sync", lambda e: [
        e.dma_start(out=ones_col[:], in_=dr["c_ones_col"][:]),
        e.dma_start(out=ones_row[:], in_=dr["c_ones_row"][:]),
    ])
    for b in range(B):
        sq.d("sync", lambda e, b=b: [
            e.dma_start(out=x1[:], in_=dr["xT32"][b].rearrange("c p t -> p c t")),
            e.dma_start(out=t2[:], in_=dr["attn0T"][b].rearrange("c p t -> p c t")),
        ])

        def add1(e):
            for c in range(8):
                i = e.tensor_tensor(x1[:, c], x1[:, c], t2[:, c], ALU.add)
            return i
        sq.c("vector", add1)
        sq.d("sync", lambda e, b=b: [
            e.dma_start(out=t2[:], in_=dr["ffn0T"][b].rearrange("c p t -> p c t"))])

        def add2(e):
            for c in range(8):
                i = e.tensor_tensor(x1[:, c], x1[:, c], t2[:, c], ALU.add)
            return i
        sq.c("vector", add2)
        sq.d("sync", lambda e, b=b: [
            e.dma_start(out=dr["x1T32"][b].rearrange("c p t -> p c t"), in_=x1[:])])
        ln_fm(sq, P, consts, x1, 8, T, None,
              xn1[:, :, b].rearrange("p c t -> p c t"))
        sq.d("sync", lambda e, b=b: [
            e.dma_start(out=dr["xn1bf"][b].rearrange("c p t -> p c t"),
                        in_=xn1[:, :, b])])
    for c in range(8):
        def pc(e, c=c):
            return e.tensor_copy(
                xp[:].rearrange("p (t i) -> p t i", i=4),
                xn1[:, c].rearrange("p i t -> p t i"))
        sq.c("vector", pc)
        sq.d("sync", lambda e, c=c: [e.dma_start(out=dr["xpbf"][c], in_=xp[:])])



def p9(cx, sq, es):
    nc = cx.nc
    dr = cx.dram
    sb = lambda n, s, t: es.enter_context(nc.sbuf_tensor(n, s, t))

    qk = sb("p9_qk", [128, 16, 4 * T], BF16)
    vsb = sb("p9_vsb", [128, 16, 16, 65], BF16)
    ones64 = sb("p9_o64", [1, 64], F32)
    l1m2 = sb("p9_l1m2", [128, 2, 128], BF16)
    bqk = sb("p9_bqk", [128, 16], F32)
    bqk8 = sb("p9_bqk8", [128, 16], F32)

    cast_load(sq, l1m2[:, 0], dr["c_l1mask"][:])
    cast_load(sq, l1m2[:, 1], dr["c_l1mask"][:])
    sq.d("sync", lambda e: [
        e.dma_start(out=bqk[:], in_=dr["ipb_qk"][:]),
        e.dma_start(out=ones64[:], in_=dr["c_ones64"][:]),
    ])
    sq.c("scalar", lambda e: e.mul(bqk8[:, 0:8], bqk[:, 0:8], 0.125))
    sq.c("scalar", lambda e: e.copy(bqk8[:, 8:16], bqk[:, 8:16]))
    sq.c("vector", lambda e: e.memset(vsb[:, :, :, 64:65], 1.0))

    es1 = ExitStack()
    sb1 = lambda n, s, t: es1.enter_context(nc.sbuf_tensor(n, s, t))
    ps1 = lambda n, s: es1.enter_context(nc.psum_tensor(n, s, F32))
    ip = sb1("p9_ip", [128, 8, 3 * D], BF16)
    xpb = sb1("p9_xpb", [128, 8, 4 * T], BF16)
    bvr = sb1("p9_bvr", [1, D], BF16)
    onesr = sb1("p9_onesr", [1, 128], BF16)
    qp = [ps1(f"p9_qp{i}", [128, 512]) for i in range(2)]
    vp = [ps1(f"p9_vp{i}", [128, 1024]) for i in range(2)]

    cast_load(sq, ip[:], dr["ipT"][:].rearrange("(c p) f -> p c f", p=128))
    cast_load(sq, bvr[:], dr["ipb_v"][:])
    cast_load(sq, onesr[:], dr["c_ones_row"][:])
    sq.d("sync", lambda e: [
        e.dma_start(out=xpb[:], in_=dr["xpbf"][:].rearrange("c p t -> p c t"))])

    m_mm = {}
    m_ev = {}
    for k in range(64):
        fc, g4 = k // 4, k % 4

        def mm(e, fc=fc, g4=g4, k=k):
            for dc in range(8):
                i = e.matmul(qp[k % 2][:], ip[:, dc, fc * 128:(fc + 1) * 128],
                             xpb[:, dc, g4 * 512:(g4 + 1) * 512],
                             start=(dc == 0), stop=(dc == 7))
            return i
        sq.c("tensor", mm, at=(m_ev[k - 2] if k >= 2 else None))
        m_mm[k] = sq.mark()

        def ev(e, fc=fc, g4=g4, k=k):
            sc = 0.125 if fc < 8 else 1.0
            return e.activation(out=qk[:, fc, g4 * 512:(g4 + 1) * 512],
                                in_=qp[k % 2][:],
                                func=AF.Identity, scale=sc, bias=bqk8[:, fc:fc + 1])
        sq.c("scalar", ev, at=m_mm[k])
        m_ev[k] = sq.mark()
    m_evv = {}
    for tt in range(16):
        def mmv(e, tt=tt):
            for dc in range(8):
                for half in range(2):
                    e.matmul(vp[tt % 2][:, half * 512:(half + 1) * 512],
                             xpb[:, dc, tt * 128:(tt + 1) * 128],
                             ip[:, dc, 2048 + half * 512:2048 + (half + 1) * 512],
                             start=(dc == 0), stop=False)
            for half in range(2):
                i = e.matmul(vp[tt % 2][:, half * 512:(half + 1) * 512], onesr[:],
                             bvr[:, half * 512:(half + 1) * 512], start=False, stop=True)
            return i
        sq.c("tensor", mmv, at=(m_evv[tt - 2] if tt >= 2 else None))
        m_mmv = sq.mark()

        def evv(e, tt=tt):
            return e.activation(out=vsb[:, tt, :, 0:64],
                                in_=vp[tt % 2][:].rearrange("p (h o) -> p h o", h=16),
                                func=AF.Copy)
        sq.c("scalar", evv, at=m_mmv)
        m_evv[tt] = sq.mark()
    es1.close()

    es2 = ExitStack()
    sb2 = lambda n, s, t: es2.enter_context(nc.sbuf_tensor(n, s, t))
    ps2 = lambda n, s: es2.enter_context(nc.psum_tensor(n, s, F32))
    att = sb2("p9_att", [128, 8, 4 * T], BF16)
    pt = sb2("p9_pt", [128, 2, 2, 128], BF16)
    rrow = sb2("p9_rrow", [1, 2, 128], F32)
    repsb = sb2("p9_reps", [64, 2, 128], F32)
    opw = sb2("p9_opw", [128, 8, 1024], BF16)
    opb = sb2("p9_opb", [128, 8], F32)
    obuf = sb2("p9_obuf", [128, 2, 512], F32)
    s2 = ps2("p9_s", [128, 2, 512])
    av2 = [ps2(f"p9_av{i}", [65, 2, 512]) for i in range(2)]
    repp = ps2("p9_repp", [64, 256])
    op = ps2("p9_op", [128, 512])

    cast_load(sq, opw[:], dr["opT"][:].rearrange("(c p) f -> p c f", p=128))
    sq.d("sync", lambda e: [e.dma_start(out=opb[:], in_=dr["opb"][:])])

    def mk_smm(tau):
        tt, hpp = tau // 8, tau % 8

        def smm(e, tt=tt, hpp=hpp):
            e.matmul(s2[:, 0, 0:128], qk[0:64, 8 + hpp, tt * 128:(tt + 1) * 128],
                     qk[0:64, hpp, tt * 128:(tt + 1) * 128],
                     start=True, stop=True, tile_position=(0, 0))
            return e.matmul(s2[:, 1, 0:128], qk[64:128, 8 + hpp, tt * 128:(tt + 1) * 128],
                            qk[64:128, hpp, tt * 128:(tt + 1) * 128],
                            start=True, stop=True, tile_position=(64, 0))
        return smm

    m_ex = {}
    m_smm = {}
    sq.c("tensor", mk_smm(0))
    m_smm[0] = sq.mark()
    for tau in range(128):
        tt, hpp = tau // 8, tau % 8
        par = tau % 2

        def ex(e, par=par):
            return e.activation(out=pt[:, par], in_=s2[:, :, 0:128], func=AF.Exp)
        sq.c("scalar", ex, at=(m_smm[tau] if tau >= 2 else None))
        m_ex[tau] = sq.mark()

        def mk(e, par=par):
            return e.tensor_tensor(pt[:, par].rearrange("p j t -> p (j t)"),
                                   pt[:, par].rearrange("p j t -> p (j t)"),
                                   l1m2[:].rearrange("p j t -> p (j t)"), ALU.mult)
        sq.c("gpsimd", mk, at=(m_ex[tau] if tau >= 2 else None))
        m_mk = sq.mark()
        if tau + 1 < 128:
            sq.c("tensor", mk_smm(tau + 1),
                 at=(m_ex[tau] if tau >= 1 else None))
            m_smm[tau + 1] = sq.mark()

        def avm(e, tt=tt, hpp=hpp, par=par):
            for j in range(2):
                h = 2 * hpp + j
                i = e.matmul(av2[par][:, j, 0:128], vsb[:, tt, h, :], pt[:, par, j],
                             start=True, stop=True)
            return i
        sq.c("tensor", avm, at=(m_mk if tau >= 2 else None))
        m_avm = sq.mark()

        sq.c("vector", lambda e, par=par: e.reciprocal(
            rrow[:], av2[par][64:65, :, 0:128]),
             at=(m_avm if tau >= 2 else None))
        m_rc = sq.mark()
        sq.c("tensor", lambda e: e.matmul(
            repp[:], ones64[:], rrow[:].rearrange("o j t -> o (j t)"),
            start=True, stop=True),
             at=(m_rc if tau >= 2 else None))
        m_rp = sq.mark()
        sq.c("scalar", lambda e: e.activation(
            out=repsb[:].rearrange("p j t -> p (j t)"),
            in_=repp[:], func=AF.Copy),
             at=(m_rp if tau >= 2 else None))
        m_cp = sq.mark()
        for j in range(2):
            def div(e, hpp=hpp, j=j, tt=tt, par=par):
                r0 = j * 64
                return e.tensor_tensor(att[r0:r0 + 64, hpp, tt * 128:(tt + 1) * 128],
                                       av2[par][0:64, j, 0:128], repsb[:, j], ALU.mult)
            sq.c("vector", div, at=(m_cp if tau >= 2 else None))

    m_mmo = {}
    m_evo = {}
    for k in range(32):
        fc, g4 = k // 4, k % 4

        def mmo(e, fc=fc, g4=g4):
            for dc in range(8):
                i = e.matmul(op[:], opw[:, dc, fc * 128:(fc + 1) * 128],
                             att[:, dc, g4 * 512:(g4 + 1) * 512],
                             start=(dc == 0), stop=(dc == 7))
            return i
        sq.c("tensor", mmo, at=(m_evo[k - 1] if k >= 1 else None))
        m_mmo[k] = sq.mark()
        sq.c("scalar", lambda e, fc=fc, k=k: e.activation(
            out=obuf[:, k % 2], in_=op[:], func=AF.Identity,
            bias=opb[:, fc:fc + 1]), at=m_mmo[k])
        m_evo[k] = sq.mark()
        sq.d("sync", lambda e, fc=fc, g4=g4, k=k: [
            e.dma_start(out=dr["attn1p"][fc, :, g4 * 512:(g4 + 1) * 512],
                        in_=obuf[:, k % 2])], at=m_evo[k])
    es2.close()



def p10(cx, sq, es):
    nc = cx.nc
    dr = cx.dram
    sb = lambda n, s, t: es.enter_context(nc.sbuf_tensor(n, s, t))

    ident = sb("p10_ident", [128, 128], F32)
    ones_row = sb("p10_onesr", [1, 128], F32)
    sq.d("sync", lambda e: [
        e.dma_start(out=ident[:], in_=dr["c_ident"][:]),
        e.dma_start(out=ones_row[:], in_=dr["c_ones_row"][:]),
    ])

    es1 = ExitStack()
    sb1 = lambda n, s, t: es1.enter_context(nc.sbuf_tensor(n, s, t))
    ps1 = lambda n, s: es1.enter_context(nc.psum_tensor(n, s, F32))
    wrow = sb1("p10_wrow", [128, 4096], F32)
    scl = sb1("p10_scl", [128, 32], F32)
    sclT = sb1("p10_sclT", [32, 128], F32)
    screp = sb1("p10_screp", [128, 4096], F32)
    rcrep = sb1("p10_rcrep", [128, 4096], F32)
    scrow = sb1("p10_scrow", [1, 4096], F32)
    rcrow = sb1("p10_rcrow", [1, 4096], F32)
    wt = sb1("p10_wt", [128, 4096], F32)
    z = sb1("p10_z", [128, 4096], F32)
    gtl = sb1("p10_gtl", [128, 4096], F32)
    weffb = sb1("p10_weff", [128, 4096], BF16)
    tpp = ps1("p10_tpp", [32, 128])
    repP = ps1("p10_repP", [128, 512])

    def weff_prep(wname, wTname, nrows, scname, slabname):
        inner = WEIGHT_SHAPES[wname][1]
        ntile = nrows // 128
        for rt in range(ntile):
            sq.d("sync", lambda e, rt=rt, inner=inner: [
                e.dma_start(out=wrow[:, 0:inner],
                            in_=dr[wname][rt * 128:(rt + 1) * 128, :])])

            sq.c("vector", lambda e, rt=rt, inner=inner: e.tensor_reduce(
                scl[:, rt:rt + 1], wrow[:, 0:inner], AX.X, ALU.add,
                apply_absolute_value=True))
            sq.c("vector", lambda e, rt=rt, inner=inner: e.tensor_single_scalar(
                scl[:, rt:rt + 1], scl[:, rt:rt + 1], 1.0 / inner, ALU.mult))
            sq.c("vector", lambda e, rt=rt: e.tensor_single_scalar(
                scl[:, rt:rt + 1], scl[:, rt:rt + 1], 1e-5, ALU.max))

        def trs(e, ntile=ntile):
            return e.transpose(tpp[0:ntile, :], scl[:, 0:ntile], ident[:])
        sq.c("tensor", trs)
        sq.c("vector", lambda e, ntile=ntile: e.tensor_copy(sclT[0:ntile, :],
                                                            tpp[0:ntile, :]))
        sq.d("sync", lambda e, ntile=ntile, nrows=nrows: [
            e.dma_start(out=dr[scname][0:1, 0:nrows].rearrange(
                "one (a b) -> (one a) b", a=ntile), in_=sclT[0:ntile, :])])
        sq.d("sync", lambda e, nrows=nrows: [
            e.dma_start(out=scrow[:, 0:nrows], in_=dr[scname][:, 0:nrows])])
        sq.c("vector", lambda e, nrows=nrows: e.reciprocal(rcrow[:, 0:nrows],
                                                           scrow[:, 0:nrows]))
        for half in range(nrows // 512):
            def repmm(e, half=half):
                return e.matmul(repP[:], ones_row[:],
                                scrow[:, half * 512:(half + 1) * 512],
                                start=True, stop=True)
            sq.c("tensor", repmm)
            sq.c("vector", lambda e, half=half: e.tensor_copy(
                screp[:, half * 512:(half + 1) * 512], repP[:]))

            def repmm2(e, half=half):
                return e.matmul(repP[:], ones_row[:],
                                rcrow[:, half * 512:(half + 1) * 512],
                                start=True, stop=True)
            sq.c("tensor", repmm2)
            sq.c("vector", lambda e, half=half: e.tensor_copy(
                rcrep[:, half * 512:(half + 1) * 512], repP[:]))
        nchT = WEIGHT_SHAPES[wTname][0] // 128
        for dc in range(nchT):
            sq.d("sync", lambda e, dc=dc, nrows=nrows: [
                e.dma_start(out=wt[:, 0:nrows],
                            in_=dr[wTname][dc * 128:(dc + 1) * 128, :])])

            sq.c("vector", lambda e, nrows=nrows: e.tensor_tensor(
                z[:, 0:nrows], wt[:, 0:nrows], rcrep[:, 0:nrows], ALU.mult))
            sq.c("vector", lambda e, nrows=nrows: e.tensor_single_scalar(
                gtl[:, 0:nrows], z[:, 0:nrows], 0.5, ALU.is_gt))
            sq.c("vector", lambda e, nrows=nrows: e.tensor_single_scalar(
                z[:, 0:nrows], z[:, 0:nrows], -0.5, ALU.is_lt))
            sq.c("vector", lambda e, nrows=nrows: e.tensor_tensor(
                z[:, 0:nrows], gtl[:, 0:nrows], z[:, 0:nrows], ALU.subtract))
            sq.c("vector", lambda e, nrows=nrows: e.tensor_tensor(
                weffb[:, 0:nrows], z[:, 0:nrows], screp[:, 0:nrows], ALU.mult))
            sq.d("sync", lambda e, dc=dc, nrows=nrows, slabname=slabname: [
                e.dma_start(out=dr[slabname][dc, :, 0:nrows], in_=weffb[:, 0:nrows])])

    weff_prep("wfc", "wfcT", 4096, "scfc", "wfcE")
    weff_prep("wpr", "wprT", 1024, "scpr", "wprE")
    es1.close()

    es2 = ExitStack()
    sb2 = lambda n, s, t: es2.enter_context(nc.sbuf_tensor(n, s, t))
    ps2 = lambda n, s: es2.enter_context(nc.psum_tensor(n, s, F32))
    wfce = sb2("p10_wfce", [128, 8, 4096], BF16)
    wpre = sb2("p10_wpre", [128, 32, 1024], BF16)
    vT = sb2("p10_vT", [128, 8, 16], BF16)
    uT = sb2("p10_uT", [16, 4096], BF16)
    fbT = sb2("p10_fbT", [16, 16], BF16)
    gate = sb2("p10_gate", [1, 1], F32)
    gcol = sb2("p10_gcol", [16, 1], F32)
    xnb = sb2("p10_xnb", [128, 8, T], BF16)
    t1 = sb2("p10_t1", [16, 512], BF16)
    t2 = sb2("p10_t2", [16, 512], BF16)
    r2 = sb2("p10_r2", [128, 32, 512], BF16)
    rbuf = sb2("p10_rbuf", [128, 512], BF16)
    obuf = sb2("p10_obuf", [128, 512], F32)
    t1p = ps2("p10_t1p", [16, 512])
    t2p = ps2("p10_t2p", [16, 512])
    hpp = [ps2(f"p10_hp{i}", [128, 512]) for i in range(2)]
    op = ps2("p10_op", [128, 512])
    gp = ps2("p10_gp", [16, 1])

    sq.d("sync", lambda e: [
        e.dma_start(out=wfce[:], in_=dr["wfcE"][:].rearrange("c p f -> p c f")),
        e.dma_start(out=wpre[:], in_=dr["wprE"][:].rearrange("c p f -> p c f")),
        e.dma_start(out=gate[:], in_=dr["l1gate"][:]),
    ])
    cast_load(sq, vT[:], dr["l1vT"][:].rearrange("(c p) f -> p c f", p=128))
    cast_load(sq, uT[:], dr["l1uT"][:])
    cast_load(sq, fbT[:], dr["fbT"][:])

    def grep(e):
        return e.matmul(gp[:], ones_row[0:1, 0:16], gate[:], start=True, stop=True)
    sq.c("tensor", grep)
    sq.c("vector", lambda e: e.tensor_copy(gcol[:], gp[:]))

    for b in range(B):
        sq.d("sync", lambda e, b=b: [
            e.dma_start(out=xnb[:], in_=dr["xn1bf"][b].rearrange("c p t -> p c t"))])

        def mmt1(e):
            for dc in range(8):
                i = e.matmul(t1p[:], vT[:, dc], xnb[:, dc], start=(dc == 0),
                             stop=(dc == 7))
            return i
        sq.c("tensor", mmt1)
        sq.c("vector", lambda e: e.tensor_single_scalar(t1[:], t1p[:], gcol[:],
                                                        ALU.mult))
        sq.c("tensor", lambda e: e.matmul(t2p[:], fbT[:], t1[:], start=True, stop=True))
        sq.c("vector", lambda e: e.tensor_copy(t2[:], t2p[:]))
        for hc in range(32):
            def mmh(e, hc=hc):
                for dc in range(8):
                    e.matmul(hpp[hc % 2][:], wfce[:, dc, hc * 128:(hc + 1) * 128],
                             xnb[:, dc], start=(dc == 0), stop=False)
                return e.matmul(hpp[hc % 2][:], uT[:, hc * 128:(hc + 1) * 128], t2[:],
                                start=False, stop=True)
            sq.c("tensor", mmh)

            def evh(e, hc=hc):
                return e.tensor_single_scalar(rbuf[:], hpp[hc % 2][:], 0.0, ALU.max)
            sq.c("vector", evh)
            sq.c("scalar", lambda e, hc=hc: e.activation(out=r2[:, hc], in_=rbuf[:],
                                                         func=AF.Square))
        for dc in range(8):
            def mmo(e, dc=dc):
                for hc in range(32):
                    i = e.matmul(op[:], wpre[:, hc, dc * 128:(dc + 1) * 128],
                                 r2[:, hc], start=(hc == 0), stop=(hc == 31))
                return i
            sq.c("tensor", mmo)
            sq.c("vector", lambda e: e.tensor_copy(obuf[:], op[:]))
            sq.d("sync", lambda e, b=b, dc=dc: [
                e.dma_start(out=dr["ffn1T"][b, dc], in_=obuf[:])])
    es2.close()



def p11(cx, sq, es):
    nc = cx.nc
    dr = cx.dram
    sb = lambda n, s, t: es.enter_context(nc.sbuf_tensor(n, s, t))
    ps = lambda n, s: es.enter_context(nc.psum_tensor(n, s, F32))

    ident = sb("p11_ident", [128, 128], F32)
    x2 = sb("p11_x2", [128, 8, T], F32)
    tmp = sb("p11_tmp", [128, 8, T], F32)
    ap = sb("p11_ap", [128, 8, 4 * T], F32)
    obuf = sb("p11_obuf", [128, 1024], F32)
    tp_p = ps("p11_tp", [128, 1024])

    sq.d("sync", lambda e: [e.dma_start(out=ident[:], in_=dr["c_ident"][:])])
    sq.d("sync", lambda e: [
        e.dma_start(out=ap[:], in_=dr["attn1p"][:].rearrange("c p t -> p c t"))])
    for b in range(B):
        sq.d("sync", lambda e, b=b: [
            e.dma_start(out=x2[:], in_=dr["x1T32"][b].rearrange("c p t -> p c t")),
            e.dma_start(out=tmp[:], in_=dr["ffn1T"][b].rearrange("c p t -> p c t")),
        ])

        def add1(e):
            for c in range(8):
                i = e.tensor_tensor(x2[:, c], x2[:, c], tmp[:, c], ALU.add)
            return i
        sq.c("vector", add1)

        def add2(e, b=b):
            for c in range(8):
                i = e.tensor_tensor(
                    x2[:, c], x2[:, c],
                    ap[:, c].rearrange("p (t i) -> p t i", i=4)[:, :, b], ALU.add)
            return i
        sq.c("vector", add2)
        for tt in range(NT):
            def tr(e, tt=tt):
                for c in range(8):
                    i = e.transpose(tp_p[:, c * 128:(c + 1) * 128],
                                    x2[:, c, tt * 128:(tt + 1) * 128], ident[:])
                return i
            sq.c("tensor", tr)
            sq.c("vector", lambda e: e.tensor_copy(obuf[:], tp_p[:]))
            sq.d("sync", lambda e, b=b, tt=tt: [
                e.dma_start(out=dr["outp"][b, tt * 128:(tt + 1) * 128, :], in_=obuf[:])])


PHASES = {"p1": p1, "p2": p2, "p2w": p2w, "p3": p3, "p4a": p4a, "p4f": p4f,
          "p4af": p4af, "p4ff": p4ff, "p5": p5, "p7": p7, "p8": p8, "p9": p9,
          "p10": p10, "p11": p11}



def build_program(phases, ext_in_slabs=(), ext_out_slabs=()):
    nc = bass.Bass("TRN2", target_bir_lowering=False)
    cx = Ctx(nc)
    sq = Seq(cx)

    need = set()
    for p in phases:
        need.update(PHASE_WEIGHTS.get(p, []))
    if "p1" in phases:
        cx.dram_t("xsl", [B, T, D], F32, "ExternalInput")
    for w in sorted(need):
        cx.dram_t(w, list(WEIGHT_SHAPES[w]), F32, "ExternalInput")

    if "p2w" in phases:
        cx.dram_t("xfull", [B, S, D], F32, "ExternalInput")
    used = set()
    for p in phases:
        r, w = PHASE_SLABS[p]
        used.update(r)
        used.update(w)
    for name in sorted(used):
        shape, dt = SLABS.get(name) or GATHERED.get(name) or WIDE[name]
        if name in ext_in_slabs:
            kind = "ExternalInput"
        elif name in ext_out_slabs:
            kind = "ExternalOutput"
        else:
            kind = "Internal"
        cx.dram_t(name, shape, dt, kind)
    if "p11" in phases:
        cx.dram_t("outp", [B, T, D], F32, "ExternalOutput")

    with nc.Block() as blk:
        for p in phases:
            es = ExitStack()
            PHASES[p](cx, sq, es)
            sq.flush(blk)
            es.close()
        sq.finish(blk)
    cx.es.close()
    return nc



def assemble_gathered(resA):
    kattn = np.concatenate([r["kattn"] for r in resA], axis=3)
    kffn = np.concatenate([r["kffn"] for r in resA], axis=3)
    vattn = np.concatenate([r["vattn"] for r in resA], axis=2)
    vffn = np.concatenate([r["vffn"] for r in resA], axis=2)
    out = []
    for c in range(NC):
        ko, vo = reorder_causal(kattn, vattn, c)
        out.append({"kattn_r": ko, "vattn_r": vo, "kffn_g": kffn, "vffn_g": vffn})
    return out


PH_A = ["p1", "p2"]
PH_B = ["p3", "p4a", "p4f", "p5", "p7", "p8", "p9", "p10", "p11"]
PH_FUSED = ["p1", "p2", "p2w", "p3", "p4af", "p4ff", "p5", "p7", "p8", "p9",
            "p10", "p11"]
A_OUT = ["xT32", "xnbf", "xnf32", "kattn", "kffn", "vattn", "vffn"]
B_IN = ["xT32", "xnbf", "xnf32", "kattn_r", "kffn_g", "vattn_r", "vffn_g"]


def kernel_unfused(**inputs):
    g = stage_common(inputs)
    x = np.ascontiguousarray(np.asarray(inputs["L0_attn_x"], np.float32))

    ncA = build_program(PH_A, ext_out_slabs=A_OUT)
    in_mapsA = []
    for c in range(NC):
        m = {"xsl": np.ascontiguousarray(x[:, c * T:(c + 1) * T, :])}
        for w in sorted(set(PHASE_WEIGHTS["p1"]) | set(PHASE_WEIGHTS["p2"])):
            m[w] = g[w]
        in_mapsA.append(m)
    resA = run_bass_kernel_spmd(ncA, in_mapsA, core_ids=list(range(NC))).results

    percore_kv = assemble_gathered(resA)

    ncB = build_program(PH_B, ext_in_slabs=B_IN)
    wB = set()
    for p in PH_B:
        wB.update(PHASE_WEIGHTS.get(p, []))
    in_mapsB = []
    for c in range(NC):
        m = {}
        for w in sorted(wB):
            m[w] = g[w]
        for nm in ("xT32", "xnbf", "xnf32"):
            m[nm] = resA[c][nm]
        m.update(percore_kv[c])
        in_mapsB.append(m)
    resB = run_bass_kernel_spmd(ncB, in_mapsB, core_ids=list(range(NC))).results

    out = np.empty((B, S, D), np.float32)
    for c in range(NC):
        out[:, c * T:(c + 1) * T, :] = resB[c]["outp"]
    return out


def kernel_fused(**inputs):
    g = stage_common(inputs)
    x = np.ascontiguousarray(np.asarray(inputs["L0_attn_x"], np.float32))
    ncF = build_program(PH_FUSED)
    wF = set()
    for p in PH_FUSED:
        wF.update(PHASE_WEIGHTS.get(p, []))
    wF.discard("valid8")
    in_maps = []
    for c in range(NC):
        m = {"xsl": np.ascontiguousarray(x[:, c * T:(c + 1) * T, :]), "xfull": x}
        for w in sorted(wF):
            m[w] = g[w]
        valid = np.zeros((128, 8), np.float32)
        valid[:, :c] = 1.0
        m["valid8"] = valid
        in_maps.append(m)
    res = run_bass_kernel_spmd(ncF, in_maps, core_ids=list(range(NC))).results
    out = np.empty((B, S, D), np.float32)
    for c in range(NC):
        out[:, c * T:(c + 1) * T, :] = res[c]["outp"]
    return out


kernel = kernel_fused


# version 15 (final)
# speedup vs baseline: 1.0209x; 1.0066x over previous
import numpy as np
import ml_dtypes
import concourse.bass as bass
import concourse.mybir as mybir
from concourse.bass_utils import run_bass_kernel_spmd
from contextlib import ExitStack

F32 = mybir.dt.float32
BF16 = mybir.dt.bfloat16
AF = mybir.ActivationFunctionType
ALU = mybir.AluOpType
AX = mybir.AxisListType

NC = 8
B, S, D = 4, 4096, 1024
H, HD = 16, 64
FF, E = 2048, 8
T = 512
NT = T // 128
NKC = S // 128
EPS = 1e-5
BF = ml_dtypes.bfloat16



def _cb(v, n):
    return np.ascontiguousarray(np.asarray(v, np.float32).reshape(n, 128).T)


def stage_common(ins):
    g = {}
    g["wqT"] = ins["L0_attn_Wq"].T
    g["wkT"] = ins["L0_attn_Wk"].T
    g["wvT"] = ins["L0_attn_Wv"].T
    g["woT"] = ins["L0_attn_Wo"].T
    g["bq"] = _cb(ins["L0_attn_bq"], 8)
    g["bk"] = _cb(ins["L0_attn_bk"], 8)
    g["bv_row"] = np.asarray(ins["L0_attn_bv"]).reshape(1, D)
    g["bo"] = _cb(ins["L0_attn_bo"], 8)
    g["fqw"] = ins["L0_ffn_q_w"]
    g["fkw"] = ins["L0_ffn_k_w"]
    g["fvw"] = ins["L0_ffn_v_w"]
    g["fow"] = ins["L0_ffn_out_w"]
    g["fgw"] = ins["L0_ffn_gate_w"]
    g["fgb"] = _cb(ins["L0_ffn_gate_b"], 8)
    sw = np.asarray(ins["L0_ffn_scale_w"], np.float32)
    swbd = np.zeros((D, 2), np.float32)
    for h in range(H):
        swbd[h * HD:(h + 1) * HD, h % 2] = sw[h]
    g["swbd"] = swbd
    g["n1g"] = _cb(ins["L0_ffn_n1_g"], 8)
    g["n1b"] = _cb(ins["L0_ffn_n1_b"], 8)
    g["n2g"] = _cb(ins["L0_ffn_n2_g"], 8)
    g["n2b"] = _cb(ins["L0_ffn_n2_b"], 8)
    g["mgw"] = ins["L0_ffn_moe_gate_w"]
    g["mgb"] = np.asarray(ins["L0_ffn_moe_gate_b"], np.float32).reshape(8, 1)
    g["w1"] = ins["L0_ffn_w1"]
    g["b1c"] = np.asarray(ins["L0_ffn_b1"], np.float32).reshape(E, 16, 128).transpose(0, 2, 1)
    g["lngc"] = np.asarray(ins["L0_ffn_ln_g"], np.float32).reshape(E, 16, 128).transpose(0, 2, 1)
    g["lnbc"] = np.asarray(ins["L0_ffn_ln_b"], np.float32).reshape(E, 16, 128).transpose(0, 2, 1)
    g["w2"] = ins["L0_ffn_w2"]
    g["b2"] = ins["L0_ffn_b2"]
    g["rsc_row"] = np.asarray(ins["L0_ffn_res_scale"], np.float32).reshape(1, 8)
    g["rsc_col"] = np.asarray(ins["L0_ffn_res_scale"], np.float32).reshape(8, 1)
    g["ipT"] = ins["L1_attn_in_proj_weight"].T
    ipb = np.asarray(ins["L1_attn_in_proj_bias"], np.float32)
    g["ipb_qk"] = _cb(ipb[:2048], 16)
    g["ipb_v"] = ipb[2048:].reshape(1, D)
    g["opT"] = ins["L1_attn_out_proj_weight"].T
    g["opb"] = _cb(ins["L1_attn_out_proj_bias"], 8)
    g["wfc"] = ins["L1_ffn_w_fc"]
    g["wfcT"] = ins["L1_ffn_w_fc"].T
    g["wpr"] = ins["L1_ffn_w_proj"]
    g["wprT"] = ins["L1_ffn_w_proj"].T
    g["l1uT"] = ins["L1_ffn_u"].T
    g["l1vT"] = ins["L1_ffn_v"].T
    g["fbT"] = ins["L1_ffn_fast_b"].T
    g["l1gate"] = np.asarray(ins["L1_ffn_gate"], np.float32).reshape(1, 1)
    g["c_ident"] = np.eye(128, dtype=np.float32)
    g["c_ones_col"] = np.ones((128, 1), np.float32)
    g["c_ones_row"] = np.ones((1, 128), np.float32)
    g["c_ones64"] = np.ones((1, 64), np.float32)
    g["c_iota8"] = np.tile(np.arange(8, dtype=np.float32)[None, :], (128, 1))
    bp = np.zeros((2, 128), np.float32)
    bp[0, 0:64] = 1.0
    bp[1, 64:128] = 1.0
    g["c_bpair"] = bp
    g["c_ones16"] = np.ones((128, 16), np.float32)
    kk = np.arange(128)[:, None]
    pp = np.arange(512)[None, :]
    g["c_diagmask"] = np.stack([(kk <= pp - 128 * j) for j in range(4)]).astype(np.float32)
    jj = np.arange(128)[:, None]
    ii = np.arange(128)[None, :]
    g["c_l1mask"] = ((jj // 4) == (ii // 4)).astype(np.float32)
    return {k: np.ascontiguousarray(np.asarray(v, np.float32)) for k, v in g.items()}


def reorder_causal(kfull, vfull, core):
    qb = core * T
    nearly = qb // 128
    order = [qb // 128 + j for j in range(4)] + list(range(nearly))
    ko = np.zeros_like(kfull)
    vo = np.zeros_like(vfull)
    for pos, kc in enumerate(order):
        ko[:, :, :, pos * 128:(pos + 1) * 128] = kfull[:, :, :, kc * 128:(kc + 1) * 128]
        vo[:, :, pos] = vfull[:, :, kc]
    return np.ascontiguousarray(ko), np.ascontiguousarray(vo)



class Ctx:
    def __init__(self, nc):
        self.nc = nc
        self.es = ExitStack()
        self.dram = {}

    def dram_t(self, name, shape, dt, kind):
        if name in self.dram:
            return self.dram[name]
        t = self.nc.dram_tensor(name, list(shape), dt, kind=kind).ap()
        self.dram[name] = t
        return t


class _Mark:
    __slots__ = ("w",)

    def __init__(self):
        self.w = None


class Seq:

    CENGS = ("tensor", "scalar", "vector", "gpsimd")
    KEYS = ("t", "sc", "v", "gp", "s", "g", "sd", "gd")

    def __init__(self, cx):
        self.cx = cx
        nc = cx.nc
        self.sems = {
            "t": cx.es.enter_context(nc.semaphore("sq_t")),
            "sc": cx.es.enter_context(nc.semaphore("sq_sc")),
            "v": cx.es.enter_context(nc.semaphore("sq_v")),
            "gp": cx.es.enter_context(nc.semaphore("sq_gp")),
            "s": cx.es.enter_context(nc.semaphore("sq_s")),
            "g": cx.es.enter_context(nc.semaphore("sq_g")),
            "sd": cx.es.enter_context(nc.semaphore("sq_sd")),
            "gd": cx.es.enter_context(nc.semaphore("sq_gd")),
        }
        self.ckey = {"tensor": "t", "scalar": "sc", "vector": "v",
                     "gpsimd": "gp"}
        self.n = {k: 0 for k in self.KEYS}
        self.w_sd = 0
        self.w_gd = 0
        self.hw = {e: {} for e in ("tensor", "scalar", "vector", "gpsimd",
                                   "sync")}
        self.steps = []

    def c(self, eng, fn, at=None):
        self.steps.append((eng, "c", fn, at))

    def d(self, eng, fn, defer=False, at=None):
        self.steps.append((eng, "d", fn, (defer, at)))

    def barrier(self):
        self.steps.append((None, "bar", None, None))

    def mark(self):
        m = _Mark()
        self.steps.append((None, "mark", None, m))
        return m

    def _waits(self):
        w = dict(self.n)
        w["sd"] = self.w_sd
        w["gd"] = self.w_gd
        return w

    def _wait_list(self, eng, w):
        out = []
        hw = self.hw[eng]
        for k in self.KEYS:
            tgt = w.get(k, 0)
            if tgt <= 0 or tgt <= hw.get(k, 0):
                continue
            out.append((self.sems[k], tgt))
            hw[k] = tgt
        return out

    def _emit_crun(self, blk, eng, entries):
        def body(e, entries=entries):
            for fn, waits in entries:
                for sem, val in waits:
                    e.wait_ge(sem, val)
                r = fn(e)
                r = r[-1] if isinstance(r, (list, tuple)) else r
                r.then_inc(self.sems[self.ckey[eng]], 1)
        getattr(blk, eng)(body)

    def flush(self, blk):
        steps = self.steps
        self.steps = []
        runs = {}
        for eng, kind, fn, extra in steps:
            if kind == "bar":
                self.w_sd = self.n["sd"]
                self.w_gd = self.n["gd"]
                continue
            if kind == "mark":
                extra.w = self._waits()
                continue
            at = extra[1] if kind == "d" else extra
            if isinstance(at, _Mark):
                at = at.w
            w = at if at is not None else self._waits()
            if kind == "c":
                waits = self._wait_list(eng, w)
                runs.setdefault(eng, []).append((fn, waits))
                self.n[self.ckey[eng]] += 1
            else:
                defer = extra[0]
                assert eng in ("sync", "gpsimd")
                if eng in runs and runs[eng]:
                    self._emit_crun(blk, eng, runs.pop(eng))
                if defer:
                    key = "sd" if eng == "sync" else "gd"
                else:
                    key = "s" if eng == "sync" else "g"
                sem = self.sems[key]
                waits = self._wait_list(eng, w)
                holder = {}

                def body(e, fn=fn, waits=waits, sem=sem, holder=holder):
                    for s_, v_ in waits:
                        e.wait_ge(s_, v_)
                    r = fn(e)
                    rl = r if isinstance(r, (list, tuple)) else [r]
                    for i in rl:
                        i.then_inc(sem, 16)
                    holder["n"] = len(rl)
                getattr(blk, eng)(body)
                self.n[key] += 16 * holder["n"]
        for eng, entries in runs.items():
            if entries:
                self._emit_crun(blk, eng, entries)

    def finish(self, blk):
        w = dict(self.n)
        waits = self._wait_list("sync", w)

        def body(e):
            for sem, val in waits:
                e.wait_ge(sem, val)
        blk.sync(body)



class LnScratch:
    def __init__(self, es, nc, tag, nch=8, width=T, repm=None, repr_=None):
        sb = lambda n, s, t: es.enter_context(nc.sbuf_tensor(n, s, t))
        ps = lambda n, s: es.enter_context(nc.psum_tensor(n, s, F32))
        self.zsq = sb(f"{tag}_zsq", [128, nch, width], F32)
        self.rows = sb(f"{tag}_rows", [1, 4, width], F32)
        self.lnt = sb(f"{tag}_lnt", [128, nch, width], F32)
        self.s1p = ps(f"{tag}_s1", [1, width])
        self.s2p = ps(f"{tag}_s2", [1, width])
        self.repm = repm if repm is not None else ps(f"{tag}_repm", [128, width])
        self.repr = repr_ if repr_ is not None else ps(f"{tag}_repr", [128, width])


def ln_fm(sq, P, consts, z_sb, nch, width, out32, outbf, gamma=None, beta=None,
          n=None):
    n = n or (nch * 128)
    ones_col, ones_row = consts["ones_col"], consts["ones_row"]

    def sqs(e):
        for c in range(nch):
            i = e.activation(out=P.zsq[:, c], in_=z_sb[:, c], func=AF.Square)
        return i
    sq.c("scalar", sqs)

    def stats(e):
        for c in range(nch):
            e.matmul(P.s1p[:], ones_col[:], z_sb[:, c], start=(c == 0), stop=(c == nch - 1))
        for c in range(nch):
            i = e.matmul(P.s2p[:], ones_col[:], P.zsq[:, c], start=(c == 0),
                         stop=(c == nch - 1))
        return i
    sq.c("tensor", stats)

    sq.c("vector", lambda e: e.tensor_single_scalar(P.rows[:, 0], P.s1p[:],
                                                    1.0 / n, ALU.mult))
    sq.c("vector", lambda e: e.tensor_tensor(P.rows[:, 1], P.rows[:, 0],
                                             P.rows[:, 0], ALU.mult))
    sq.c("vector", lambda e: e.scalar_tensor_tensor(
        out=P.rows[:, 2], in0=P.s2p[:], scalar=1.0 / n, in1=P.rows[:, 1],
        op0=ALU.mult, op1=ALU.subtract))
    sq.c("vector", lambda e: e.tensor_single_scalar(P.rows[:, 2], P.rows[:, 2],
                                                    EPS, ALU.add))
    sq.c("scalar", lambda e: e.activation(out=P.rows[:, 1], in_=P.rows[:, 2],
                                          func=AF.Ln))
    sq.c("scalar", lambda e: e.activation(out=P.rows[:, 3], in_=P.rows[:, 1],
                                          func=AF.Exp, scale=-0.5))

    def reps(e):
        e.matmul(P.repm[:], ones_row[:], P.rows[:, 0], start=True, stop=True)
        return e.matmul(P.repr[:], ones_row[:], P.rows[:, 3], start=True, stop=True)
    sq.c("tensor", reps)

    def norm1(e):
        for c in range(nch):
            i = e.tensor_tensor(P.lnt[:, c], z_sb[:, c], P.repm[:], ALU.subtract)
        return i
    sq.c("vector", norm1)

    def norm2(e):
        for c in range(nch):
            i = e.tensor_tensor(P.lnt[:, c], P.lnt[:, c], P.repr[:], ALU.mult)
        return i
    sq.c("vector", norm2)

    if gamma is not None:
        tgt = out32 if out32 is not None else P.lnt

        def gb(e):
            for c in range(nch):
                i = e.activation(out=tgt[:, c], in_=P.lnt[:, c], func=AF.Identity,
                                 scale=gamma[:, c:c + 1], bias=beta[:, c:c + 1])
            return i
        sq.c("scalar", gb)
        if outbf is not None:
            def cb(e):
                for c in range(nch):
                    i = e.tensor_copy(outbf[:, c], tgt[:, c])
                return i
            sq.c("vector", cb)
    else:
        def outs(e):
            i = None
            for c in range(nch):
                if out32 is not None:
                    i = e.tensor_copy(out32[:, c], P.lnt[:, c])
                if outbf is not None:
                    i = e.tensor_copy(outbf[:, c], P.lnt[:, c])
            return i
        sq.c("vector", outs)


def cast_load(sq, dst_ap, src_ap):
    sq.d("gpsimd", lambda e: [e.dma_start(out=dst_ap, in_=src_ap)])



SLABS = {
    "xT32": ([B, 8, 128, T], F32),
    "xnbf": ([B, 8, 128, T], BF16),
    "xnf32": ([B, 8, 128, T], F32),
    "kattn": ([B, 8, 128, T], BF16),
    "kffn": ([B, 8, 128, T], BF16),
    "vattn": ([B, 4, NT, 128, 260], BF16),
    "vffn": ([B, 4, NT, 128, 260], BF16),
    "qa": ([B, 8, 128, T], BF16),
    "qf": ([B, 8, 128, T], BF16),
    "attn0T": ([B, 8, 128, T], F32),
    "fattn0T": ([B, 8, 128, T], F32),
    "xrT32": ([B, 8, 128, T], F32),
    "xrTbf": ([B, 8, 128, T], BF16),
    "selT": ([B, NT, 128, 1024], F32),
    "mrsc": ([B, NT, 128, 8], F32),
    "mbf": ([B, NT, 128, 8], BF16),
    "ffn0T": ([B, 8, 128, T], F32),
    "x1T32": ([B, 8, 128, T], F32),
    "xn1bf": ([B, 8, 128, T], BF16),
    "xpbf": ([8, 128, 4 * T], BF16),
    "attn1p": ([8, 128, 4 * T], F32),
    "ffn1T": ([B, 8, 128, T], F32),
    "wfcE": ([8, 128, 4096], BF16),
    "wprE": ([32, 128, 1024], BF16),
    "scfc": ([1, 4096], F32),
    "scpr": ([1, 1024], F32),
}

GATHERED = {
    "kattn_r": ([B, 8, 128, S], BF16),
    "kffn_g": ([B, 8, 128, S], BF16),
    "vattn_r": ([B, 4, NKC, 128, 260], BF16),
    "vffn_g": ([B, 4, NKC, 128, 260], BF16),
}

WIDE = {
    "kaw": ([B, 8, 128, S], BF16),
    "kfw": ([B, 8, 128, S], BF16),
    "vaw": ([B, 4, NKC, 128, 260], BF16),
    "vfw": ([B, 4, NKC, 128, 260], BF16),
}

WEIGHT_SHAPES = {
    "wqT": (D, D), "wkT": (D, D), "wvT": (D, D), "woT": (D, D),
    "bq": (128, 8), "bk": (128, 8), "bv_row": (1, D), "bo": (128, 8),
    "fqw": (D, D), "fkw": (D, D), "fvw": (D, D), "fow": (D, D),
    "fgw": (D, D), "fgb": (128, 8), "swbd": (D, 2),
    "n1g": (128, 8), "n1b": (128, 8), "n2g": (128, 8), "n2b": (128, 8),
    "mgw": (D, 8), "mgb": (8, 1),
    "w1": (E, D, FF), "b1c": (E, 128, 16), "lngc": (E, 128, 16),
    "lnbc": (E, 128, 16), "w2": (E, FF, D), "b2": (E, D),
    "rsc_row": (1, 8), "rsc_col": (8, 1),
    "ipT": (D, 3 * D), "ipb_qk": (128, 16), "ipb_v": (1, D),
    "opT": (D, D), "opb": (128, 8),
    "wfc": (4096, D), "wfcT": (D, 4096), "wpr": (D, 4096), "wprT": (4096, D),
    "l1uT": (16, 4096), "l1vT": (D, 16), "fbT": (16, 16), "l1gate": (1, 1),
    "c_ident": (128, 128), "c_ones_col": (128, 1), "c_ones_row": (1, 128),
    "c_ones64": (1, 64), "c_iota8": (128, 8),
    "c_diagmask": (4, 128, 512), "c_l1mask": (128, 128),
    "c_bpair": (2, 128), "valid8": (128, 8), "c_ones16": (128, 16),
}

PHASE_WEIGHTS = {
    "p1": ["c_ident", "c_ones_col", "c_ones_row"],
    "p2": ["wkT", "wvT", "fkw", "fvw", "bk", "bv_row", "c_ones_row"],
    "p2w": ["wkT", "wvT", "fkw", "fvw", "bk", "bv_row", "c_ones_row", "valid8",
            "c_ones16", "c_ident", "c_ones_col"],
    "p4af": ["woT", "bo", "c_diagmask", "c_ones64"],
    "p4ff": ["fgw", "fgb", "fow", "c_ones64"],
    "p3": ["wqT", "fqw", "bq", "swbd", "c_bpair"],
    "p4a": ["woT", "bo", "c_diagmask", "c_ones64"],
    "p4f": ["fgw", "fgb", "fow", "c_ones64"],
    "p5": ["n1g", "n1b", "c_ones_col", "c_ones_row"],
    "p7": ["mgw", "mgb", "w1", "b1c", "lngc", "lnbc", "w2", "b2", "rsc_row",
           "rsc_col", "c_iota8", "c_ident", "n2g", "n2b", "c_ones_col",
           "c_ones_row"],
    "p8": ["c_ones_col", "c_ones_row"],
    "p9": ["ipT", "ipb_qk", "ipb_v", "opT", "opb", "c_l1mask", "c_ones64",
           "c_ident"],
    "p10": ["wfc", "wfcT", "wpr", "wprT", "l1uT", "l1vT", "fbT", "l1gate",
            "c_ident", "c_ones_row"],
    "p11": ["c_ident"],
}

PHASE_SLABS = {
    "p1": ([], ["xT32", "xnbf", "xnf32"]),
    "p2": (["xnbf"], ["kattn", "kffn", "vattn", "vffn"]),
    "p2w": ([], ["kaw", "kfw", "vaw", "vfw"]),
    "p4af": (["qa", "kaw", "vaw", "kattn", "vattn"], ["attn0T"]),
    "p4ff": (["qf", "kfw", "vfw"], ["fattn0T"]),
    "p3": (["xnbf"], ["qa", "qf"]),
    "p4a": (["qa", "kattn_r", "vattn_r"], ["attn0T"]),
    "p4f": (["qf", "kffn_g", "vffn_g"], ["fattn0T"]),
    "p5": (["xnf32", "fattn0T"], ["xrT32", "xrTbf"]),
    "p7": (["xrTbf", "xrT32"], ["selT", "mrsc", "mbf", "ffn0T"]),
    "p8": (["xT32", "attn0T", "ffn0T"], ["x1T32", "xn1bf", "xpbf"]),
    "p9": (["xpbf"], ["attn1p"]),
    "p10": (["xn1bf"], ["wfcE", "wprE", "scfc", "scpr", "ffn1T"]),
    "p11": (["x1T32", "attn1p", "ffn1T"], []),
}



def p1(cx, sq, es):
    nc = cx.nc
    dr = cx.dram
    sb = lambda n, s, t: es.enter_context(nc.sbuf_tensor(n, s, t))
    ps = lambda n, s: es.enter_context(nc.psum_tensor(n, s, F32))

    xtile = sb("p1_xtile", [128, 1024], F32)
    ident = sb("p1_ident", [128, 128], F32)
    ones_col = sb("p1_onesc", [128, 1], F32)
    ones_row = sb("p1_onesr", [1, 128], F32)
    xT = sb("p1_xT", [128, 8, T], F32)
    xn32 = sb("p1_xn32", [128, 8, T], F32)
    xnbf = sb("p1_xnbf", [128, 8, T], BF16)
    P = LnScratch(es, nc, "p1")
    consts = {"ones_col": ones_col, "ones_row": ones_row}
    tp_p = ps("p1_tp", [128, 1024])

    sq.d("sync", lambda e: [
        e.dma_start(out=ident[:], in_=dr["c_ident"][:]),
        e.dma_start(out=ones_col[:], in_=dr["c_ones_col"][:]),
        e.dma_start(out=ones_row[:], in_=dr["c_ones_row"][:]),
    ])

    for b in range(B):
        for tt in range(NT):
            sq.d("sync", lambda e, b=b, tt=tt: [
                e.dma_start(out=xtile[:], in_=dr["xsl"][b, tt * 128:(tt + 1) * 128, :])])

            def tr(e):
                for c in range(8):
                    i = e.transpose(tp_p[:, c * 128:(c + 1) * 128],
                                    xtile[:, c * 128:(c + 1) * 128], ident[:])
                return i
            sq.c("tensor", tr)

            def ev(e, tt=tt):
                return e.tensor_copy(
                    xT[:, :, tt * 128:(tt + 1) * 128],
                    tp_p[:].rearrange("p (c t) -> p c t", c=8))
            sq.c("vector", ev)

        ln_fm(sq, P, consts, xT, 8, T, xn32, xnbf)

        def store(e, b=b):
            out = []
            for c in range(8):
                out.append(e.dma_start(out=dr["xT32"][b, c], in_=xT[:, c]))
                out.append(e.dma_start(out=dr["xnf32"][b, c], in_=xn32[:, c]))
                out.append(e.dma_start(out=dr["xnbf"][b, c], in_=xnbf[:, c]))
            return out
        sq.d("sync", store)



def p2(cx, sq, es):
    nc = cx.nc
    dr = cx.dram
    sb = lambda n, s, t: es.enter_context(nc.sbuf_tensor(n, s, t))
    ps = lambda n, s: es.enter_context(nc.psum_tensor(n, s, F32))

    wk = sb("p2_wk", [128, 8, 1024], BF16)
    wv = sb("p2_wv", [128, 8, 1024], BF16)
    fk = sb("p2_fk", [128, 8, 1024], BF16)
    fv = sb("p2_fv", [128, 8, 1024], BF16)
    bk = sb("p2_bk", [128, 8], F32)
    bvr = sb("p2_bvr", [1, 1024], BF16)
    onesr = sb("p2_onesr", [1, 128], BF16)
    xnb = sb("p2_xnb", [128, 8, T], BF16)
    kbuf = sb("p2_kbuf", [128, 512], BF16)
    vbuf = sb("p2_vbuf", [128, 16, 65], BF16)
    kp = ps("p2_kp", [128, 512])
    vp = ps("p2_vp", [128, 1024])

    for w, nm in [(wk, "wkT"), (wv, "wvT"), (fk, "fkw"), (fv, "fvw")]:
        cast_load(sq, w[:], dr[nm][:].rearrange("(c p) f -> p c f", p=128))
    cast_load(sq, bvr[:], dr["bv_row"][:])
    cast_load(sq, onesr[:], dr["c_ones_row"][:])
    sq.d("sync", lambda e: [e.dma_start(out=bk[:], in_=dr["bk"][:])])
    sq.c("vector", lambda e: e.memset(vbuf[:, :, 64:65], 1.0))

    for b in range(B):
        sq.d("sync", lambda e, b=b: [
            e.dma_start(out=xnb[:], in_=dr["xnbf"][b].rearrange("c p t -> p c t"))])
        for mod in ("a", "f"):
            wmat = wk if mod == "a" else fk
            kout = dr["kattn"] if mod == "a" else dr["kffn"]
            for fc in range(8):
                def mm(e, fc=fc, wmat=wmat):
                    for dc in range(8):
                        i = e.matmul(kp[:], wmat[:, dc, fc * 128:(fc + 1) * 128],
                                     xnb[:, dc], start=(dc == 0), stop=(dc == 7))
                    return i
                sq.c("tensor", mm)

                def ev(e, fc=fc, mod=mod):
                    if mod == "a":
                        return e.activation(out=kbuf[:], in_=kp[:], func=AF.Identity,
                                            bias=bk[:, fc:fc + 1])
                    return e.activation(out=kbuf[:], in_=kp[:], func=AF.Copy)
                sq.c("scalar", ev)
                sq.d("sync", lambda e, b=b, fc=fc, kout=kout: [
                    e.dma_start(out=kout[b, fc], in_=kbuf[:])])

            wv_ = wv if mod == "a" else fv
            vout = dr["vattn"] if mod == "a" else dr["vffn"]
            for tt in range(NT):
                def mmv(e, tt=tt, wv_=wv_, mod=mod):
                    for dc in range(8):
                        for half in range(2):
                            i = e.matmul(vp[:, half * 512:(half + 1) * 512],
                                         xnb[:, dc, tt * 128:(tt + 1) * 128],
                                         wv_[:, dc, half * 512:(half + 1) * 512],
                                         start=(dc == 0),
                                         stop=(dc == 7 and mod == "f"))
                    if mod == "a":
                        for half in range(2):
                            i = e.matmul(vp[:, half * 512:(half + 1) * 512],
                                         onesr[:], bvr[:, half * 512:(half + 1) * 512],
                                         start=False, stop=True)
                    return i
                sq.c("tensor", mmv)

                def evv(e):
                    return e.activation(out=vbuf[:, :, 0:64],
                                        in_=vp[:].rearrange("p (h o) -> p h o", h=16),
                                        func=AF.Copy)
                sq.c("scalar", evv)

                def stv(e, b=b, tt=tt, vout=vout):
                    return [e.dma_start(
                        out=vout[b, :, tt].rearrange("hg p o -> p hg o"),
                        in_=vbuf[:].rearrange("p (hg hl) o -> p hg (hl o)", hg=4))]
                sq.d("sync", stv)



def p2w(cx, sq, es):
    nc = cx.nc
    dr = cx.dram
    sb = lambda n, s, t: es.enter_context(nc.sbuf_tensor(n, s, t))
    ps = lambda n, s: es.enter_context(nc.psum_tensor(n, s, F32))

    wk = sb("p2w_wk", [128, 8, 1024], BF16)
    wv = sb("p2w_wv", [128, 8, 1024], BF16)
    fk = sb("p2w_fk", [128, 8, 1024], BF16)
    fv = sb("p2w_fv", [128, 8, 1024], BF16)
    bk = sb("p2w_bk", [128, 8], F32)
    bvr = sb("p2w_bvr", [1, 1024], BF16)
    onesr = sb("p2w_onesr", [1, 128], BF16)
    ident = sb("p2w_ident", [128, 128], F32)
    ones_col = sb("p2w_onesc", [128, 1], F32)
    ones_row = sb("p2w_onesrf", [1, 128], F32)
    ones16 = sb("p2w_ones16", [128, 16], F32)
    valid = sb("p2w_valid", [128, 8], F32)
    xtile = sb("p2w_xtile", [128, 1024], F32)
    xTg = sb("p2w_xT", [128, 8, T], F32)
    xnb = sb("p2w_xnb", [128, 8, T], BF16)
    kbuf = sb("p2w_kbuf", [128, 8, 512], BF16)
    kbuf2 = sb("p2w_kbuf2", [128, 8, 512], BF16)
    vbufA = sb("p2w_vbufA", [128, 2, 16, 65], BF16)
    vbufF = sb("p2w_vbufF", [128, 2, 16, 65], BF16)
    consts = {"ones_col": ones_col, "ones_row": ones_row}
    tp_p = ps("p2w_tp", [128, 512])
    kp = [ps(f"p2w_kp{i}", [128, 512]) for i in range(2)]
    vp = ps("p2w_vp", [128, 1024])
    P = LnScratch(es, nc, "p2w", repm=kp[0], repr_=kp[1])

    for w, nm in [(wk, "wkT"), (wv, "wvT"), (fk, "fkw"), (fv, "fvw")]:
        cast_load(sq, w[:], dr[nm][:].rearrange("(c p) f -> p c f", p=128))
    cast_load(sq, bvr[:], dr["bv_row"][:])
    cast_load(sq, onesr[:], dr["c_ones_row"][:])
    sq.d("sync", lambda e: [
        e.dma_start(out=bk[:], in_=dr["bk"][:]),
        e.dma_start(out=ident[:], in_=dr["c_ident"][:]),
        e.dma_start(out=ones_col[:], in_=dr["c_ones_col"][:]),
        e.dma_start(out=ones_row[:], in_=dr["c_ones_row"][:]),
        e.dma_start(out=ones16[:], in_=dr["c_ones16"][:]),
        e.dma_start(out=valid[:], in_=dr["valid8"][:]),
    ])
    sq.c("vector", lambda e: e.memset(vbufF[:, :, :, 64:65], 1.0))

    for b in range(B):
        for g in range(8):
            sq.barrier()
            m_ev = {}
            m_tr = {}
            for tt in range(NT):
                p0 = g * 512 + tt * 128
                sq.d("sync", lambda e, b=b, p0=p0: [
                    e.dma_start(out=xtile[:], in_=dr["xfull"][b, p0:p0 + 128, :])])
                m_ld = sq.mark()
                for r in range(2):
                    j = tt * 2 + r
                    tdst = tp_p if j % 2 == 0 else kp[0]

                    def tr(e, r=r, tdst=tdst):
                        for c in range(4):
                            i = e.transpose(tdst[:, c * 128:(c + 1) * 128],
                                            xtile[:, (r * 4 + c) * 128:(r * 4 + c + 1) * 128],
                                            ident[:])
                        return i
                    sq.c("tensor", tr, at=(m_ld if j >= 2 else None))
                    m_tr[j] = sq.mark()

                    def ev(e, tt=tt, r=r, tdst=tdst):
                        return e.tensor_copy(
                            xTg[:, r * 4:(r + 1) * 4, tt * 128:(tt + 1) * 128],
                            tdst[:].rearrange("p (c t) -> p c t", c=4))
                    sq.c("vector", ev, at=m_tr[j])
                    m_ev[j] = sq.mark()
            ln_fm(sq, P, consts, xTg, 8, T, None, xnb)
            m_evk = {}
            m_mm = {}
            for mod in ("a", "f"):
                wmat = wk if mod == "a" else fk
                kout = dr["kaw"] if mod == "a" else dr["kfw"]
                kb = kbuf if mod == "a" else kbuf2
                for fc in range(8):
                    j = (0 if mod == "a" else 8) + fc

                    def mm(e, fc=fc, wmat=wmat, j=j):
                        for dc in range(8):
                            i = e.matmul(kp[j % 2][:], wmat[:, dc, fc * 128:(fc + 1) * 128],
                                         xnb[:, dc], start=(dc == 0), stop=(dc == 7))
                        return i
                    sq.c("tensor", mm, at=(m_evk[j - 2] if j >= 2 else None))
                    m_mm[j] = sq.mark()

                    def evk(e, fc=fc, mod=mod, kb=kb, j=j):
                        if mod == "a":
                            return e.activation(out=kb[:, fc], in_=kp[j % 2][:],
                                                func=AF.Identity, bias=bk[:, fc:fc + 1])
                        return e.activation(out=kb[:, fc], in_=kp[j % 2][:], func=AF.Copy)
                    sq.c("scalar", evk, at=m_mm[j])
                    m_evk[j] = sq.mark()
                sq.d("sync", lambda e, b=b, g=g, kout=kout, kb=kb: [
                    e.dma_start(out=kout[b, :, :, g * 512:(g + 1) * 512]
                                .rearrange("c p t -> p c t"),
                                in_=kb[:])], defer=True, at=m_evk[j])
            m_evv = {}
            for mod in ("a", "f"):
                wv_ = wv if mod == "a" else fv
                vout = dr["vaw"] if mod == "a" else dr["vfw"]
                vb = vbufA if mod == "a" else vbufF
                for tt in range(NT):
                    j = (0 if mod == "a" else NT) + tt
                    par = j % 2

                    def mmv(e, tt=tt, wv_=wv_, mod=mod):
                        for dc in range(8):
                            for half in range(2):
                                i = e.matmul(vp[:, half * 512:(half + 1) * 512],
                                             xnb[:, dc, tt * 128:(tt + 1) * 128],
                                             wv_[:, dc, half * 512:(half + 1) * 512],
                                             start=(dc == 0),
                                             stop=(dc == 7 and mod == "f"))
                        if mod == "a":
                            for half in range(2):
                                i = e.matmul(vp[:, half * 512:(half + 1) * 512],
                                             onesr[:], bvr[:, half * 512:(half + 1) * 512],
                                             start=False, stop=True)
                        return i
                    sq.c("tensor", mmv, at=(m_evv[j - 1] if j >= 1 else None))
                    m_mmv = sq.mark()
                    if mod == "a":
                        sq.c("vector", lambda e, g=g, par=par: e.tensor_single_scalar(
                            vbufA[:, par, :, 64:65].rearrange("p h o -> p (h o)"),
                            ones16[:], valid[:, g:g + 1], ALU.mult), at=m_mmv)

                    def evv(e, vb=vb, g=g, mod=mod, par=par):
                        if mod == "a":
                            return e.activation(out=vb[:, par, :, 0:64],
                                                in_=vp[:].rearrange("p (h o) -> p h o", h=16),
                                                func=AF.Copy, scale=valid[:, g:g + 1])
                        return e.activation(out=vb[:, par, :, 0:64],
                                            in_=vp[:].rearrange("p (h o) -> p h o", h=16),
                                            func=AF.Copy)
                    sq.c("scalar", evv, at=m_mmv)
                    m_evv[j] = sq.mark()

                    def stv(e, b=b, g=g, tt=tt, vout=vout, vb=vb, par=par):
                        return [e.dma_start(
                            out=vout[b, :, g * NT + tt].rearrange("hg p o -> p hg o"),
                            in_=vb[:, par].rearrange("p (hg hl) o -> p hg (hl o)", hg=4))]
                    sq.d("sync", stv, at=m_evv[j])
    sq.barrier()



def p3(cx, sq, es):
    nc = cx.nc
    dr = cx.dram
    sb = lambda n, s, t: es.enter_context(nc.sbuf_tensor(n, s, t))
    ps = lambda n, s: es.enter_context(nc.psum_tensor(n, s, F32))

    wq = sb("p3_wq", [128, 8, 1024], BF16)
    fq = sb("p3_fq", [128, 8, 1024], BF16)
    swb = sb("p3_swb", [128, 8, 2], BF16)
    bpair = sb("p3_bpair", [2, 128], F32)
    bq8 = sb("p3_bq8", [128, 8], F32)
    xnb = sb("p3_xnb", [128, 8, T], BF16)
    qfb = sb("p3_qfb", [128, 8, T], BF16)
    qbuf = sb("p3_qbuf", [128, 512], BF16)
    srow = sb("p3_srow", [2, 512], F32)
    qp = ps("p3_qp", [128, 512])
    sp = ps("p3_sp", [2, 512])
    rp = ps("p3_rp", [128, 512])

    cast_load(sq, wq[:], dr["wqT"][:].rearrange("(c p) f -> p c f", p=128))
    cast_load(sq, fq[:], dr["fqw"][:].rearrange("(c p) f -> p c f", p=128))
    cast_load(sq, swb[:], dr["swbd"][:].rearrange("(c p) f -> p c f", p=128))
    sq.d("sync", lambda e: [e.dma_start(out=bq8[:], in_=dr["bq"][:]),
                            e.dma_start(out=bpair[:], in_=dr["c_bpair"][:])])
    sq.c("scalar", lambda e: e.mul(bq8[:], bq8[:], 0.125))

    for b in range(B):
        sq.d("sync", lambda e, b=b: [
            e.dma_start(out=xnb[:], in_=dr["xnbf"][b].rearrange("c p t -> p c t"))])
        for fc in range(8):
            def mm(e, fc=fc):
                for dc in range(8):
                    i = e.matmul(qp[:], wq[:, dc, fc * 128:(fc + 1) * 128],
                                 xnb[:, dc], start=(dc == 0), stop=(dc == 7))
                return i
            sq.c("tensor", mm)
            sq.c("scalar", lambda e, fc=fc: e.activation(
                out=qbuf[:], in_=qp[:], func=AF.Identity, scale=0.125,
                bias=bq8[:, fc:fc + 1]))
            sq.d("sync", lambda e, b=b, fc=fc: [
                e.dma_start(out=dr["qa"][b, fc], in_=qbuf[:])])

            def mmf(e, fc=fc):
                for dc in range(8):
                    i = e.matmul(qp[:], fq[:, dc, fc * 128:(fc + 1) * 128],
                                 xnb[:, dc], start=(dc == 0), stop=(dc == 7))
                return i
            sq.c("tensor", mmf)
            sq.c("scalar", lambda e, fc=fc: e.activation(
                out=qfb[:, fc], in_=qp[:], func=AF.Copy))
        for fc in range(8):
            sq.c("tensor", lambda e, fc=fc: e.matmul(
                sp[:], swb[:, fc], qfb[:, fc], start=True, stop=True))
            sq.c("scalar", lambda e: e.activation(out=srow[:], in_=sp[:], func=AF.Sigmoid))
            sq.c("vector", lambda e: e.tensor_single_scalar(srow[:], srow[:], 0.25, ALU.mult))
            sq.c("tensor", lambda e: e.matmul(rp[:], bpair[:], srow[:], start=True, stop=True))
            sq.c("vector", lambda e, fc=fc: e.tensor_tensor(qbuf[:], qfb[:, fc], rp[:], ALU.mult))
            sq.d("sync", lambda e, b=b, fc=fc: [
                e.dma_start(out=dr["qf"][b, fc], in_=qbuf[:])])



def attention_l0(cx, sq, es, tag, qname, kname, vname, nmask, att_sb):
    raise NotImplementedError


def p4_common(cx, sq, es, tag, qname, kname, vname, nmask, post_cb, diag=None):
    nc = cx.nc
    dr = cx.dram
    sb = lambda n, s, t: es.enter_context(nc.sbuf_tensor(n, s, t))
    ps = lambda n, s: es.enter_context(nc.psum_tensor(n, s, F32))

    kline = sb(f"{tag}_kline", [128, 2, S], BF16)
    vline = sb(f"{tag}_vline", [128, NKC, 260], BF16)
    qpair = sb(f"{tag}_qpair", [128, 2, 512], BF16)
    pt = sb(f"{tag}_pt", [128, 2, 2, 512], BF16)
    masks = sb(f"{tag}_masks", [128, 4, 512], BF16)
    rrow = sb(f"{tag}_rrow", [1, 512], F32)
    repsb = sb(f"{tag}_reps", [64, 512], F32)
    ones64 = sb(f"{tag}_o64", [1, 64], F32)
    att_sb = sb(f"{tag}_att", [128, 8, T], BF16)
    sAb = [ps(f"{tag}_sA{i}", [128, 512]) for i in range(2)]
    sBb = [ps(f"{tag}_sB{i}", [128, 512]) for i in range(2)]
    av = [ps(f"{tag}_av{i}", [65, 512]) for i in range(4)]

    if diag is not None:
        klineD = sb(f"{tag}_klineD", [128, 2, 512], BF16)
        vlineD = sb(f"{tag}_vlineD", [128, NT, 260], BF16)
    sq.d("sync", lambda e: [e.dma_start(out=ones64[:], in_=dr["c_ones64"][:])])
    if nmask or diag is not None:
        sq.d("gpsimd", lambda e: [e.dma_start(
            out=masks[:], in_=dr["c_diagmask"][:].rearrange("j p t -> p j t"))])

    npos = NKC + (NT if diag is not None else 0)
    taus = [(pos, hp) for pos in range(npos) for hp in range(2)]
    ntau = len(taus)

    for b in range(B):
        for hg in range(4):
            def loads(e, b=b, hg=hg):
                out = [
                    e.dma_start(out=kline[:, 0], in_=dr[kname][b, 2 * hg]),
                    e.dma_start(out=kline[:, 1], in_=dr[kname][b, 2 * hg + 1]),
                    e.dma_start(out=vline[:],
                                in_=dr[vname][b, hg].rearrange("k p o -> p k o")),
                    e.dma_start(out=qpair[:, 0], in_=dr[qname][b, 2 * hg]),
                    e.dma_start(out=qpair[:, 1], in_=dr[qname][b, 2 * hg + 1]),
                ]
                if diag is not None:
                    out.append(e.dma_start(out=klineD[:, 0], in_=dr[diag[0]][b, 2 * hg]))
                    out.append(e.dma_start(out=klineD[:, 1], in_=dr[diag[0]][b, 2 * hg + 1]))
                    out.append(e.dma_start(
                        out=vlineD[:], in_=dr[diag[1]][b, hg].rearrange("k p o -> p k o")))
                return out
            sq.d("sync", loads)

            def mk_smm(tau):
                pos, hp = taus[tau]
                dpos = pos - NKC
                kl = kline if dpos < 0 else klineD
                kidx = pos if dpos < 0 else dpos
                par = tau % 2

                def smm(e, kl=kl, kidx=kidx, hp=hp, par=par):
                    e.matmul(sAb[par][:], kl[0:64, hp, kidx * 128:(kidx + 1) * 128],
                             qpair[0:64, hp], start=True, stop=True,
                             tile_position=(0, 0))
                    return e.matmul(sBb[par][:],
                                    kl[64:128, hp, kidx * 128:(kidx + 1) * 128],
                                    qpair[64:128, hp], start=True, stop=True,
                                    tile_position=(64, 0))
                return smm

            m_ex = {}
            m_smm = {}
            sq.c("tensor", mk_smm(0))
            m_smm[0] = sq.mark()
            for tau in range(ntau):
                pos, hp = taus[tau]
                dpos = pos - NKC
                vl = vline if dpos < 0 else vlineD
                kidx = pos if dpos < 0 else dpos
                masked = (pos < nmask) or (dpos >= 0)
                midx = pos if dpos < 0 else dpos
                par = tau % 2

                def ex(e, par=par):
                    e.activation(out=pt[:, par, 0], in_=sAb[par][:], func=AF.Exp)
                    return e.activation(out=pt[:, par, 1], in_=sBb[par][:],
                                        func=AF.Exp)
                sq.c("scalar", ex, at=(m_smm[tau] if tau >= 2 else None))
                m_ex[tau] = sq.mark()
                dep = m_ex[tau]
                if masked:
                    def mk(e, midx=midx, par=par):
                        e.tensor_tensor(pt[:, par, 0], pt[:, par, 0],
                                        masks[:, midx], ALU.mult)
                        return e.tensor_tensor(pt[:, par, 1], pt[:, par, 1],
                                               masks[:, midx], ALU.mult)
                    sq.c("gpsimd", mk, at=m_ex[tau])
                    dep = sq.mark()
                if tau + 1 < ntau:
                    sq.c("tensor", mk_smm(tau + 1),
                         at=(m_ex[tau - 1] if tau >= 1 else None))
                    m_smm[tau + 1] = sq.mark()

                def avm(e, vl=vl, kidx=kidx, pos=pos, hp=hp, par=par):
                    for j in range(2):
                        hl = 2 * hp + j
                        i = e.matmul(av[hl][:],
                                     vl[:, kidx, hl * 65:hl * 65 + 65],
                                     pt[:, par, j], start=(pos == 0),
                                     stop=(pos == npos - 1))
                    return i
                sq.c("tensor", avm, at=(dep if tau >= 2 else None))
            for hl in range(4):
                sq.c("vector", lambda e, hl=hl: e.reciprocal(rrow[:], av[hl][64:65, :]))
                sq.c("tensor", lambda e: e.matmul(sBb[1][0:64, :], ones64[:], rrow[:],
                                                  start=True, stop=True))
                sq.c("scalar", lambda e: e.activation(out=repsb[:], in_=sBb[1][0:64, :],
                                                      func=AF.Copy))

                def div(e, hl=hl, hg=hg):
                    c = 2 * hg + hl // 2
                    r0 = (hl % 2) * 64
                    return e.tensor_tensor(att_sb[r0:r0 + 64, c], av[hl][0:64, :],
                                           repsb[:], ALU.mult)
                sq.c("vector", div)
        post_cb(b, att_sb, sAb[0])


def p4a(cx, sq, es):
    nc = cx.nc
    dr = cx.dram
    sb = lambda n, s, t: es.enter_context(nc.sbuf_tensor(n, s, t))
    ps = lambda n, s: es.enter_context(nc.psum_tensor(n, s, F32))
    wo = sb("p4a_wo", [128, 8, 1024], BF16)
    bo = sb("p4a_bo", [128, 8], F32)
    obuf = sb("p4a_obuf", [128, 512], F32)
    cast_load(sq, wo[:], dr["woT"][:].rearrange("(c p) f -> p c f", p=128))
    sq.d("sync", lambda e: [e.dma_start(out=bo[:], in_=dr["bo"][:])])

    def post(b, att_sb, scr):
        for fc in range(8):
            def mm(e, fc=fc, att_sb=att_sb, scr=scr):
                for dc in range(8):
                    i = e.matmul(scr[:], wo[:, dc, fc * 128:(fc + 1) * 128],
                                 att_sb[:, dc], start=(dc == 0), stop=(dc == 7))
                return i
            sq.c("tensor", mm)
            sq.c("scalar", lambda e, fc=fc, scr=scr: e.activation(
                out=obuf[:], in_=scr[:], func=AF.Identity, bias=bo[:, fc:fc + 1]))
            sq.d("sync", lambda e, b=b, fc=fc: [
                e.dma_start(out=dr["attn0T"][b, fc], in_=obuf[:])])
    p4_common(cx, sq, es, "p4a", "qa", "kattn_r", "vattn_r", 4, post)


def p4af(cx, sq, es):
    nc = cx.nc
    dr = cx.dram
    sb = lambda n, s, t: es.enter_context(nc.sbuf_tensor(n, s, t))
    ps = lambda n, s: es.enter_context(nc.psum_tensor(n, s, F32))
    wo = sb("p4a_wo", [128, 8, 1024], BF16)
    bo = sb("p4a_bo", [128, 8], F32)
    obuf = sb("p4a_obuf", [128, 512], F32)
    cast_load(sq, wo[:], dr["woT"][:].rearrange("(c p) f -> p c f", p=128))
    sq.d("sync", lambda e: [e.dma_start(out=bo[:], in_=dr["bo"][:])])

    def post(b, att_sb, scr):
        for fc in range(8):
            def mm(e, fc=fc, att_sb=att_sb, scr=scr):
                for dc in range(8):
                    i = e.matmul(scr[:], wo[:, dc, fc * 128:(fc + 1) * 128],
                                 att_sb[:, dc], start=(dc == 0), stop=(dc == 7))
                return i
            sq.c("tensor", mm)
            sq.c("scalar", lambda e, fc=fc, scr=scr: e.activation(
                out=obuf[:], in_=scr[:], func=AF.Identity, bias=bo[:, fc:fc + 1]))
            sq.d("sync", lambda e, b=b, fc=fc: [
                e.dma_start(out=dr["attn0T"][b, fc], in_=obuf[:])])
    p4_common(cx, sq, es, "p4a", "qa", "kaw", "vaw", 0, post,
              diag=("kattn", "vattn"))


def p4f(cx, sq, es):
    nc = cx.nc
    dr = cx.dram
    sb = lambda n, s, t: es.enter_context(nc.sbuf_tensor(n, s, t))
    ps = lambda n, s: es.enter_context(nc.psum_tensor(n, s, F32))
    gw = sb("p4f_gw", [128, 8, 1024], BF16)
    ow = sb("p4f_ow", [128, 8, 1024], BF16)
    gb = sb("p4f_gb", [128, 8], F32)
    gt = sb("p4f_gt", [128, 8, T], BF16)
    go = sb("p4f_go", [128, 8, T], BF16)
    obuf = sb("p4f_obuf", [128, 512], F32)
    cast_load(sq, gw[:], dr["fgw"][:].rearrange("(c p) f -> p c f", p=128))
    cast_load(sq, ow[:], dr["fow"][:].rearrange("(c p) f -> p c f", p=128))
    sq.d("sync", lambda e: [e.dma_start(out=gb[:], in_=dr["fgb"][:])])

    def post(b, att_sb, scr):
        for fc in range(8):
            def mm(e, fc=fc, att_sb=att_sb, scr=scr):
                for dc in range(8):
                    i = e.matmul(scr[:], gw[:, dc, fc * 128:(fc + 1) * 128],
                                 att_sb[:, dc], start=(dc == 0), stop=(dc == 7))
                return i
            sq.c("tensor", mm)
            sq.c("scalar", lambda e, fc=fc, scr=scr: e.activation(
                out=gt[:, fc], in_=scr[:], func=AF.Sigmoid, bias=gb[:, fc:fc + 1]))

        def gmul(e, att_sb=att_sb):
            for fc in range(8):
                i = e.tensor_tensor(go[:, fc], att_sb[:, fc], gt[:, fc], ALU.mult)
            return i
        sq.c("vector", gmul)
        for fc in range(8):
            def mm2(e, fc=fc, scr=scr):
                for dc in range(8):
                    i = e.matmul(scr[:], ow[:, dc, fc * 128:(fc + 1) * 128],
                                 go[:, dc], start=(dc == 0), stop=(dc == 7))
                return i
            sq.c("tensor", mm2)
            sq.c("scalar", lambda e, fc=fc, scr=scr: e.activation(
                out=obuf[:], in_=scr[:], func=AF.Copy))
            sq.d("sync", lambda e, b=b, fc=fc: [
                e.dma_start(out=dr["fattn0T"][b, fc], in_=obuf[:])])
    p4_common(cx, sq, es, "p4f", "qf", "kffn_g", "vffn_g", 0, post)


def p4ff(cx, sq, es):
    nc = cx.nc
    dr = cx.dram
    sb = lambda n, s, t: es.enter_context(nc.sbuf_tensor(n, s, t))
    ps = lambda n, s: es.enter_context(nc.psum_tensor(n, s, F32))
    gw = sb("p4f_gw", [128, 8, 1024], BF16)
    ow = sb("p4f_ow", [128, 8, 1024], BF16)
    gb = sb("p4f_gb", [128, 8], F32)
    gt = sb("p4f_gt", [128, 8, T], BF16)
    go = sb("p4f_go", [128, 8, T], BF16)
    obuf = sb("p4f_obuf", [128, 512], F32)
    cast_load(sq, gw[:], dr["fgw"][:].rearrange("(c p) f -> p c f", p=128))
    cast_load(sq, ow[:], dr["fow"][:].rearrange("(c p) f -> p c f", p=128))
    sq.d("sync", lambda e: [e.dma_start(out=gb[:], in_=dr["fgb"][:])])

    def post(b, att_sb, scr):
        for fc in range(8):
            def mm(e, fc=fc, att_sb=att_sb, scr=scr):
                for dc in range(8):
                    i = e.matmul(scr[:], gw[:, dc, fc * 128:(fc + 1) * 128],
                                 att_sb[:, dc], start=(dc == 0), stop=(dc == 7))
                return i
            sq.c("tensor", mm)
            sq.c("scalar", lambda e, fc=fc, scr=scr: e.activation(
                out=gt[:, fc], in_=scr[:], func=AF.Sigmoid, bias=gb[:, fc:fc + 1]))

        def gmul(e, att_sb=att_sb):
            for fc in range(8):
                i = e.tensor_tensor(go[:, fc], att_sb[:, fc], gt[:, fc], ALU.mult)
            return i
        sq.c("vector", gmul)
        for fc in range(8):
            def mm2(e, fc=fc, scr=scr):
                for dc in range(8):
                    i = e.matmul(scr[:], ow[:, dc, fc * 128:(fc + 1) * 128],
                                 go[:, dc], start=(dc == 0), stop=(dc == 7))
                return i
            sq.c("tensor", mm2)
            sq.c("scalar", lambda e, fc=fc, scr=scr: e.activation(
                out=obuf[:], in_=scr[:], func=AF.Copy))
            sq.d("sync", lambda e, b=b, fc=fc: [
                e.dma_start(out=dr["fattn0T"][b, fc], in_=obuf[:])])
    p4_common(cx, sq, es, "p4f", "qf", "kfw", "vfw", 0, post)



def p5(cx, sq, es):
    nc = cx.nc
    dr = cx.dram
    sb = lambda n, s, t: es.enter_context(nc.sbuf_tensor(n, s, t))

    z = sb("p5_z", [128, 8, T], F32)
    fat = sb("p5_fat", [128, 8, T], F32)
    xr32 = sb("p5_xr32", [128, 8, T], F32)
    xrbf = sb("p5_xrbf", [128, 8, T], BF16)
    g = sb("p5_g", [128, 8], F32)
    bb = sb("p5_b", [128, 8], F32)
    ones_col = sb("p5_onesc", [128, 1], F32)
    ones_row = sb("p5_onesr", [1, 128], F32)
    P = LnScratch(es, nc, "p5")
    consts = {"ones_col": ones_col, "ones_row": ones_row}

    sq.d("sync", lambda e: [
        e.dma_start(out=g[:], in_=dr["n1g"][:]),
        e.dma_start(out=bb[:], in_=dr["n1b"][:]),
        e.dma_start(out=ones_col[:], in_=dr["c_ones_col"][:]),
        e.dma_start(out=ones_row[:], in_=dr["c_ones_row"][:]),
    ])
    for b in range(B):
        sq.d("sync", lambda e, b=b: [
            e.dma_start(out=z[:], in_=dr["xnf32"][b].rearrange("c p t -> p c t")),
            e.dma_start(out=fat[:], in_=dr["fattn0T"][b].rearrange("c p t -> p c t")),
        ])

        def add(e):
            for c in range(8):
                i = e.tensor_tensor(z[:, c], z[:, c], fat[:, c], ALU.add)
            return i
        sq.c("vector", add)
        ln_fm(sq, P, consts, z, 8, T, xr32, xrbf, gamma=g, beta=bb)
        sq.d("sync", lambda e, b=b: [
            e.dma_start(out=dr["xrT32"][b].rearrange("c p t -> p c t"), in_=xr32[:]),
            e.dma_start(out=dr["xrTbf"][b].rearrange("c p t -> p c t"), in_=xrbf[:]),
        ])



def p7(cx, sq, es):
    nc = cx.nc
    dr = cx.dram
    sb = lambda n, s, t: es.enter_context(nc.sbuf_tensor(n, s, t))

    iota = sb("p7_iota", [128, 8], F32)
    ident = sb("p7_ident", [128, 128], F32)
    identb = sb("p7_identb", [128, 128], BF16)
    ones_col = sb("p7_onesc", [128, 1], F32)
    onescb = sb("p7_onescb", [128, 1], BF16)
    ones_row = sb("p7_onesr", [1, 128], F32)
    rsc_row = sb("p7_rscrow", [1, 8], F32)
    rsc_col = sb("p7_rsccol", [8, 1], F32)
    b2sb = sb("p7_b2", [8, 1024], F32)
    b2rsc = sb("p7_b2rsc", [8, 1024], BF16)
    mrsc_sb = sb("p7_mrsc", [128, B * NT, 8], F32)
    mbf_sb = sb("p7_mbf", [128, B * NT, 8], BF16)
    rscrep = sb("p7_rscrep", [128, 8], F32)
    consts = {"ones_col": ones_col, "ones_row": ones_row}

    cast_load(sq, identb[:], dr["c_ident"][:])
    cast_load(sq, onescb[:], dr["c_ones_col"][:])
    sq.d("sync", lambda e: [
        e.dma_start(out=iota[:], in_=dr["c_iota8"][:]),
        e.dma_start(out=ident[:], in_=dr["c_ident"][:]),
        e.dma_start(out=ones_col[:], in_=dr["c_ones_col"][:]),
        e.dma_start(out=ones_row[:], in_=dr["c_ones_row"][:]),
        e.dma_start(out=rsc_row[:], in_=dr["rsc_row"][:]),
        e.dma_start(out=rsc_col[:], in_=dr["rsc_col"][:]),
        e.dma_start(out=b2sb[:], in_=dr["b2"][:]),
    ])
    sq.c("vector", lambda e: e.tensor_single_scalar(b2rsc[:], b2sb[:], rsc_col[:],
                                                    ALU.mult))

    es1 = ExitStack()
    sb1 = lambda n, s, t: es1.enter_context(nc.sbuf_tensor(n, s, t))
    ps1 = lambda n, s: es1.enter_context(nc.psum_tensor(n, s, F32))
    mgw = sb1("p7_mgw", [128, 8, 8], F32)
    xr32r = sb1("p7_xr32r", [128, 8, T], F32)
    mgb = sb1("p7_mgb", [8, 1], F32)
    lgs = sb1("p7_lgs", [8, 512], F32)
    ltm = sb1("p7_ltm", [128, 8], F32)
    mx8 = sb1("p7_mx8", [128, 8], F32)
    ge2 = sb1("p7_ge2", [128, 8], F32)
    chosen = sb1("p7_chosen", [128, 1], F32)
    lgp = ps1("p7_lgp", [8, 512])
    ltp = ps1("p7_ltp", [128, 32])

    sq.d("sync", lambda e: [
        e.dma_start(out=mgw[:], in_=dr["mgw"][:].rearrange("(c p) f -> p c f", p=128)),
        e.dma_start(out=mgb[:], in_=dr["mgb"][:])])
    sq.c("tensor", lambda e: e.matmul(ltp[:, 0:8], ones_row[:], rsc_row[:],
                                      start=True, stop=True))
    sq.c("vector", lambda e: e.tensor_copy(rscrep[:], ltp[:, 0:8]))

    for b in range(B):
        sq.d("sync", lambda e, b=b: [
            e.dma_start(out=xr32r[:], in_=dr["xrT32"][b].rearrange("c p t -> p c t"))])

        def lgmm(e):
            for dc in range(8):
                i = e.matmul(lgp[:], mgw[:, dc], xr32r[:, dc], start=(dc == 0),
                             stop=(dc == 7))
            return i
        sq.c("tensor", lgmm)
        sq.c("scalar", lambda e: e.activation(out=lgs[:], in_=lgp[:], func=AF.Identity,
                                              bias=mgb[:]))

        def ltr(e):
            for tt in range(NT):
                i = e.transpose(ltp[:, tt * 8:(tt + 1) * 8],
                                lgs[:, tt * 128:(tt + 1) * 128], ident[0:8, 0:8])
            return i
        sq.c("tensor", ltr)
        for tt in range(NT):
            sq.c("vector", lambda e, tt=tt: e.tensor_copy(
                ltm[:], ltp[:, tt * 8:(tt + 1) * 8]))
            sq.c("vector", lambda e: e.max(out=mx8[:], in_=ltm[:]))
            sq.c("vector", lambda e: e.tensor_single_scalar(
                ge2[:], ltm[:], mx8[:, 1:2], ALU.is_ge))
            sq.c("vector", lambda e: e.tensor_tensor(ge2[:], ge2[:], iota[:],
                                                     ALU.mult))
            sq.c("vector", lambda e: e.reduce_max(chosen[:], ge2[:], axis=AX.X))
            sq.c("vector", lambda e, b=b, tt=tt: e.tensor_single_scalar(
                mbf_sb[:, b * NT + tt], iota[:], chosen[:], ALU.is_equal))
            sq.c("vector", lambda e, b=b, tt=tt: e.tensor_single_scalar(
                mrsc_sb[:, b * NT + tt], iota[:], chosen[:], ALU.is_equal))
            sq.c("vector", lambda e, b=b, tt=tt: e.tensor_tensor(
                mrsc_sb[:, b * NT + tt], mrsc_sb[:, b * NT + tt], rscrep[:],
                ALU.mult))
        sq.d("sync", lambda e, b=b: [
            e.dma_start(out=dr["mrsc"][b].rearrange("n p f -> p n f"),
                        in_=mrsc_sb[:, b * NT:(b + 1) * NT]),
            e.dma_start(out=dr["mbf"][b].rearrange("n p f -> p n f"),
                        in_=mbf_sb[:, b * NT:(b + 1) * NT]),
        ])
    es1.close()

    es2 = ExitStack()
    sb2 = lambda n, s, t: es2.enter_context(nc.sbuf_tensor(n, s, t))
    ps2 = lambda n, s: es2.enter_context(nc.psum_tensor(n, s, F32))
    w1sb = sb2("p7_w1", [128, 8, FF], BF16)
    w2sb = sb2("p7_w2", [128, 16, 1024], BF16)
    b1 = sb2("p7_b1", [128, 16], F32)
    lng = sb2("p7_lng", [128, 16], F32)
    lnb = sb2("p7_lnb", [128, 16], F32)
    hsb = sb2("p7_h", [128, 16, 512], BF16)
    hsq = sb2("p7_hsq", [128, 16, 512], BF16)
    gsb = sb2("p7_g", [128, 16, 512], BF16)
    rows = sb2("p7_erows", [1, 4, 512], F32)
    msb = sb2("p7_m", [128, 512], F32)
    rsb = sb2("p7_r", [128, 512], F32)
    ysel = sb2("p7_ysel", [128, 2, 1024], F32)
    s1p = ps2("p7_es1", [1, 512])
    s2p = ps2("p7_es2", [1, 512])
    hp = [ps2(f"p7_hp{i}", [128, 512]) for i in range(2)]
    yp = [ps2(f"p7_yp{i}", [128, 1024]) for i in range(2)]
    repm = hp[0]
    repr_ = hp[1]

    xrb4 = sb2("p7_xrb4", [128, B, 8, T], BF16)
    for b in range(B):
        sq.d("sync", lambda e, b=b: [
            e.dma_start(out=xrb4[:, b], in_=dr["xrTbf"][b].rearrange("c p t -> p c t"))])

    m_bnd = None
    pending = None
    for e_i in range(E):
        if pending is not None:
            pending()
            pending = None
        sq.d("gpsimd", lambda e, e_i=e_i: [
            e.dma_start(out=w1sb[:], in_=dr["w1"][e_i].rearrange("(c p) f -> p c f", p=128)),
            e.dma_start(out=w2sb[:], in_=dr["w2"][e_i].rearrange("(c p) f -> p c f", p=128)),
        ])
        sq.d("sync", lambda e, e_i=e_i: [
            e.dma_start(out=b1[:], in_=dr["b1c"][e_i]),
            e.dma_start(out=lng[:], in_=dr["lngc"][e_i]),
            e.dma_start(out=lnb[:], in_=dr["lnbc"][e_i]),
        ])
        m_bnd = None
        for b in range(B):
            m_evh = {}
            m_mmh = {}
            for fc in range(16):
                def mmh(e, fc=fc, b=b):
                    for dc in range(8):
                        i = e.matmul(hp[fc % 2][:], w1sb[:, dc, fc * 128:(fc + 1) * 128],
                                     xrb4[:, b, dc], start=(dc == 0), stop=(dc == 7))
                    return i
                sq.c("tensor", mmh,
                     at=(m_evh[fc - 2] if fc >= 2 else m_bnd))
                m_mmh[fc] = sq.mark()

                def evh(e, fc=fc):
                    e.activation(out=hsb[:, fc], in_=hp[fc % 2][:], func=AF.Identity,
                                 bias=b1[:, fc:fc + 1])
                    return e.activation(out=hsq[:, fc], in_=hp[fc % 2][:], func=AF.Square,
                                        bias=b1[:, fc:fc + 1])
                sq.c("scalar", evh, at=m_mmh[fc])
                m_evh[fc] = sq.mark()
            if pending is not None:
                pending()
                pending = None

            def stats(e):
                for fc in range(16):
                    e.matmul(s1p[:], onescb[:], hsb[:, fc], start=(fc == 0), stop=(fc == 15))
                for fc in range(16):
                    i = e.matmul(s2p[:], onescb[:], hsq[:, fc], start=(fc == 0),
                                 stop=(fc == 15))
                return i
            sq.c("tensor", stats)

            sq.c("vector", lambda e: e.tensor_single_scalar(rows[:, 0], s1p[:],
                                                            1.0 / FF, ALU.mult))
            sq.c("vector", lambda e: e.tensor_tensor(rows[:, 1], rows[:, 0],
                                                     rows[:, 0], ALU.mult))
            sq.c("vector", lambda e: e.scalar_tensor_tensor(
                out=rows[:, 2], in0=s2p[:], scalar=1.0 / FF, in1=rows[:, 1],
                op0=ALU.mult, op1=ALU.subtract))
            sq.c("vector", lambda e: e.tensor_single_scalar(rows[:, 2], rows[:, 2],
                                                            EPS, ALU.add))
            sq.c("scalar", lambda e: e.activation(out=rows[:, 1], in_=rows[:, 2],
                                                  func=AF.Ln))
            sq.c("scalar", lambda e: e.activation(out=rows[:, 3], in_=rows[:, 1],
                                                  func=AF.Exp, scale=-0.5))

            def reps(e):
                e.matmul(repm[:], ones_row[:], rows[:, 0], start=True, stop=True)
                return e.matmul(repr_[:], ones_row[:], rows[:, 3], start=True, stop=True)
            sq.c("tensor", reps)

            def evreps(e):
                e.tensor_copy(msb[:], repm[:])
                return e.tensor_copy(rsb[:], repr_[:])
            sq.c("vector", evreps)
            m_bnd = sq.mark()

            def normh1(e):
                for fc in range(16):
                    i = e.tensor_tensor(hsb[:, fc], hsb[:, fc], msb[:], ALU.subtract)
                return i
            sq.c("vector", normh1)

            def normh2(e):
                for fc in range(16):
                    i = e.tensor_tensor(hsb[:, fc], hsb[:, fc], rsb[:], ALU.mult)
                return i
            sq.c("vector", normh2)

            def gelu(e):
                for fc in range(16):
                    i = e.activation(out=gsb[:, fc], in_=hsb[:, fc], func=AF.Gelu,
                                     scale=lng[:, fc:fc + 1], bias=lnb[:, fc:fc + 1])
                return i
            sq.c("scalar", gelu)
            m_gelu = sq.mark()

            def make_pending(b=b, e_i=e_i, m_gelu=m_gelu):
                def emit():
                    m_sel = {}
                    for tt in range(NT):
                        def mmy(e, tt=tt):
                            for fc in range(16):
                                for half in range(2):
                                    i = e.matmul(yp[tt % 2][:, half * 512:(half + 1) * 512],
                                                 gsb[:, fc, tt * 128:(tt + 1) * 128],
                                                 w2sb[:, fc, half * 512:(half + 1) * 512],
                                                 start=(fc == 0), stop=(fc == 15))
                            return i
                        sq.c("tensor", mmy,
                             at=(m_sel[tt - 2] if tt >= 2 else m_gelu))
                        m_mmy = sq.mark()

                        def sel(e, b=b, tt=tt, e_i=e_i):
                            return e.tensor_single_scalar(
                                ysel[:, tt % 2], yp[tt % 2][:],
                                mrsc_sb[:, b * NT + tt, e_i:e_i + 1], ALU.mult)
                        sq.c("vector", sel, at=m_mmy)
                        m_sel[tt] = sq.mark()
                        sq.d("gpsimd", lambda e, b=b, tt=tt, e_i=e_i: [
                            e.dma_start(out=dr["selT"][b, tt], in_=ysel[:, tt % 2],
                                        accum_op=(ALU.bypass if e_i == 0 else ALU.add))],
                             at=m_sel[tt])
                return emit
            pending = make_pending()
    if pending is not None:
        pending()
        pending = None
    es2.close()

    es3 = ExitStack()
    sb3 = lambda n, s, t: es3.enter_context(nc.sbuf_tensor(n, s, t))
    ps3 = lambda n, s: es3.enter_context(nc.psum_tensor(n, s, F32))
    xr32 = sb3("p7_xr32", [128, 8, T], F32)
    selsb = sb3("p7_sel", [128, 1024], F32)
    selfm = sb3("p7_selfm", [128, 8, T], F32)
    ffout = sb3("p7_ffout", [128, 8, T], F32)
    ohot = sb3("p7_ohot", [8, 128], BF16)
    n2g = sb3("p7_n2g", [128, 8], F32)
    n2b = sb3("p7_n2b", [128, 8], F32)
    P = LnScratch(es3, nc, "p7")
    tp2 = es3.enter_context(nc.psum_tensor("p7_tp2", [8, 128], BF16))
    yp2 = ps3("p7_yp2", [128, 1024])
    trp = ps3("p7_trp", [128, 512])

    sq.d("sync", lambda e: [
        e.dma_start(out=n2g[:], in_=dr["n2g"][:]),
        e.dma_start(out=n2b[:], in_=dr["n2b"][:]),
    ])
    for b in range(B):
        sq.d("sync", lambda e, b=b: [
            e.dma_start(out=xr32[:], in_=dr["xrT32"][b].rearrange("c p t -> p c t"))])
        for tt in range(NT):
            sq.d("sync", lambda e, b=b, tt=tt: [
                e.dma_start(out=selsb[:], in_=dr["selT"][b, tt])])

            def trm(e, b=b, tt=tt):
                return e.transpose(tp2[:], mbf_sb[:, b * NT + tt], identb[:])
            sq.c("tensor", trm)
            sq.c("vector", lambda e: e.tensor_copy(ohot[:], tp2[:]))

            def term(e):
                for half in range(2):
                    i = e.matmul(yp2[:, half * 512:(half + 1) * 512], ohot[:],
                                 b2rsc[:, half * 512:(half + 1) * 512],
                                 start=True, stop=True)
                return i
            sq.c("tensor", term)
            sq.c("vector", lambda e: e.tensor_tensor(selsb[:], selsb[:], yp2[:], ALU.add))

            for grp in range(2):
                def trsel(e, grp=grp):
                    for dc in range(4):
                        c = grp * 4 + dc
                        i = e.transpose(trp[:, dc * 128:(dc + 1) * 128],
                                        selsb[:, c * 128:(c + 1) * 128], ident[:])
                    return i
                sq.c("tensor", trsel)

                def evsel(e, grp=grp, tt=tt):
                    return e.tensor_copy(
                        selfm[:, grp * 4:(grp + 1) * 4, tt * 128:(tt + 1) * 128],
                        trp[:].rearrange("p (c t) -> p c t", c=4))
                sq.c("vector", evsel)

        def addxr(e):
            for c in range(8):
                i = e.scalar_tensor_tensor(out=selfm[:, c], in0=xr32[:, c],
                                           scalar=2.0, in1=selfm[:, c],
                                           op0=ALU.mult, op1=ALU.add)
            return i
        sq.c("vector", addxr)
        ln_fm(sq, P, consts, selfm, 8, T, ffout, None, gamma=n2g, beta=n2b)
        sq.d("sync", lambda e, b=b: [
            e.dma_start(out=dr["ffn0T"][b].rearrange("c p t -> p c t"), in_=ffout[:])])
    es3.close()



def p8(cx, sq, es):
    nc = cx.nc
    dr = cx.dram
    sb = lambda n, s, t: es.enter_context(nc.sbuf_tensor(n, s, t))

    x1 = sb("p8_x1", [128, 8, T], F32)
    t2 = sb("p8_t2", [128, 8, T], F32)
    xn1 = sb("p8_xn1", [128, 8, B, T], BF16)
    ones_col = sb("p8_onesc", [128, 1], F32)
    ones_row = sb("p8_onesr", [1, 128], F32)
    P = LnScratch(es, nc, "p8")
    consts = {"ones_col": ones_col, "ones_row": ones_row}
    xp = sb("p8_xp", [128, 4 * T], BF16)

    sq.d("sync", lambda e: [
        e.dma_start(out=ones_col[:], in_=dr["c_ones_col"][:]),
        e.dma_start(out=ones_row[:], in_=dr["c_ones_row"][:]),
    ])
    for b in range(B):
        sq.d("sync", lambda e, b=b: [
            e.dma_start(out=x1[:], in_=dr["xT32"][b].rearrange("c p t -> p c t")),
            e.dma_start(out=t2[:], in_=dr["attn0T"][b].rearrange("c p t -> p c t")),
        ])

        def add1(e):
            for c in range(8):
                i = e.tensor_tensor(x1[:, c], x1[:, c], t2[:, c], ALU.add)
            return i
        sq.c("vector", add1)
        sq.d("sync", lambda e, b=b: [
            e.dma_start(out=t2[:], in_=dr["ffn0T"][b].rearrange("c p t -> p c t"))])

        def add2(e):
            for c in range(8):
                i = e.tensor_tensor(x1[:, c], x1[:, c], t2[:, c], ALU.add)
            return i
        sq.c("vector", add2)
        sq.d("sync", lambda e, b=b: [
            e.dma_start(out=dr["x1T32"][b].rearrange("c p t -> p c t"), in_=x1[:])])
        ln_fm(sq, P, consts, x1, 8, T, None,
              xn1[:, :, b].rearrange("p c t -> p c t"))
        sq.d("sync", lambda e, b=b: [
            e.dma_start(out=dr["xn1bf"][b].rearrange("c p t -> p c t"),
                        in_=xn1[:, :, b])])
    for c in range(8):
        def pc(e, c=c):
            return e.tensor_copy(
                xp[:].rearrange("p (t i) -> p t i", i=4),
                xn1[:, c].rearrange("p i t -> p t i"))
        sq.c("vector", pc)
        sq.d("sync", lambda e, c=c: [e.dma_start(out=dr["xpbf"][c], in_=xp[:])])



def p9(cx, sq, es):
    nc = cx.nc
    dr = cx.dram
    sb = lambda n, s, t: es.enter_context(nc.sbuf_tensor(n, s, t))

    qk = sb("p9_qk", [128, 16, 4 * T], BF16)
    vsb = sb("p9_vsb", [128, 16, 16, 65], BF16)
    ones64 = sb("p9_o64", [1, 64], F32)
    l1m2 = sb("p9_l1m2", [128, 2, 128], BF16)
    bqk = sb("p9_bqk", [128, 16], F32)
    bqk8 = sb("p9_bqk8", [128, 16], F32)

    cast_load(sq, l1m2[:, 0], dr["c_l1mask"][:])
    cast_load(sq, l1m2[:, 1], dr["c_l1mask"][:])
    sq.d("sync", lambda e: [
        e.dma_start(out=bqk[:], in_=dr["ipb_qk"][:]),
        e.dma_start(out=ones64[:], in_=dr["c_ones64"][:]),
    ])
    sq.c("scalar", lambda e: e.mul(bqk8[:, 0:8], bqk[:, 0:8], 0.125))
    sq.c("scalar", lambda e: e.copy(bqk8[:, 8:16], bqk[:, 8:16]))
    sq.c("vector", lambda e: e.memset(vsb[:, :, :, 64:65], 1.0))

    es1 = ExitStack()
    sb1 = lambda n, s, t: es1.enter_context(nc.sbuf_tensor(n, s, t))
    ps1 = lambda n, s: es1.enter_context(nc.psum_tensor(n, s, F32))
    ip = sb1("p9_ip", [128, 8, 3 * D], BF16)
    xpb = sb1("p9_xpb", [128, 8, 4 * T], BF16)
    bvr = sb1("p9_bvr", [1, D], BF16)
    onesr = sb1("p9_onesr", [1, 128], BF16)
    qp = [ps1(f"p9_qp{i}", [128, 512]) for i in range(2)]
    vp = [ps1(f"p9_vp{i}", [128, 1024]) for i in range(2)]

    cast_load(sq, ip[:], dr["ipT"][:].rearrange("(c p) f -> p c f", p=128))
    cast_load(sq, bvr[:], dr["ipb_v"][:])
    cast_load(sq, onesr[:], dr["c_ones_row"][:])
    sq.d("sync", lambda e: [
        e.dma_start(out=xpb[:], in_=dr["xpbf"][:].rearrange("c p t -> p c t"))])

    m_mm = {}
    m_ev = {}
    for k in range(64):
        fc, g4 = k // 4, k % 4

        def mm(e, fc=fc, g4=g4, k=k):
            for dc in range(8):
                i = e.matmul(qp[k % 2][:], ip[:, dc, fc * 128:(fc + 1) * 128],
                             xpb[:, dc, g4 * 512:(g4 + 1) * 512],
                             start=(dc == 0), stop=(dc == 7))
            return i
        sq.c("tensor", mm, at=(m_ev[k - 2] if k >= 2 else None))
        m_mm[k] = sq.mark()

        def ev(e, fc=fc, g4=g4, k=k):
            sc = 0.125 if fc < 8 else 1.0
            return e.activation(out=qk[:, fc, g4 * 512:(g4 + 1) * 512],
                                in_=qp[k % 2][:],
                                func=AF.Identity, scale=sc, bias=bqk8[:, fc:fc + 1])
        sq.c("scalar", ev, at=m_mm[k])
        m_ev[k] = sq.mark()
    m_evv = {}
    for tt in range(16):
        def mmv(e, tt=tt):
            for dc in range(8):
                for half in range(2):
                    e.matmul(vp[tt % 2][:, half * 512:(half + 1) * 512],
                             xpb[:, dc, tt * 128:(tt + 1) * 128],
                             ip[:, dc, 2048 + half * 512:2048 + (half + 1) * 512],
                             start=(dc == 0), stop=False)
            for half in range(2):
                i = e.matmul(vp[tt % 2][:, half * 512:(half + 1) * 512], onesr[:],
                             bvr[:, half * 512:(half + 1) * 512], start=False, stop=True)
            return i
        sq.c("tensor", mmv, at=(m_evv[tt - 2] if tt >= 2 else None))
        m_mmv = sq.mark()

        def evv(e, tt=tt):
            return e.activation(out=vsb[:, tt, :, 0:64],
                                in_=vp[tt % 2][:].rearrange("p (h o) -> p h o", h=16),
                                func=AF.Copy)
        sq.c("scalar", evv, at=m_mmv)
        m_evv[tt] = sq.mark()
    es1.close()

    es2 = ExitStack()
    sb2 = lambda n, s, t: es2.enter_context(nc.sbuf_tensor(n, s, t))
    ps2 = lambda n, s: es2.enter_context(nc.psum_tensor(n, s, F32))
    att = sb2("p9_att", [128, 8, 4 * T], BF16)
    pt = sb2("p9_pt", [128, 2, 2, 128], BF16)
    rrow = sb2("p9_rrow", [1, 2, 128], F32)
    repsb = sb2("p9_reps", [64, 2, 128], F32)
    opw = sb2("p9_opw", [128, 8, 1024], BF16)
    opb = sb2("p9_opb", [128, 8], F32)
    obuf = sb2("p9_obuf", [128, 2, 512], F32)
    s2 = ps2("p9_s", [128, 2, 512])
    av2 = [ps2(f"p9_av{i}", [65, 2, 512]) for i in range(2)]
    repp = ps2("p9_repp", [64, 256])
    op = ps2("p9_op", [128, 512])

    cast_load(sq, opw[:], dr["opT"][:].rearrange("(c p) f -> p c f", p=128))
    sq.d("sync", lambda e: [e.dma_start(out=opb[:], in_=dr["opb"][:])])

    def mk_smm(tau):
        tt, hpp = tau // 8, tau % 8

        def smm(e, tt=tt, hpp=hpp):
            e.matmul(s2[:, 0, 0:128], qk[0:64, 8 + hpp, tt * 128:(tt + 1) * 128],
                     qk[0:64, hpp, tt * 128:(tt + 1) * 128],
                     start=True, stop=True, tile_position=(0, 0))
            return e.matmul(s2[:, 1, 0:128], qk[64:128, 8 + hpp, tt * 128:(tt + 1) * 128],
                            qk[64:128, hpp, tt * 128:(tt + 1) * 128],
                            start=True, stop=True, tile_position=(64, 0))
        return smm

    m_ex = {}
    m_smm = {}
    sq.c("tensor", mk_smm(0))
    m_smm[0] = sq.mark()
    for tau in range(128):
        tt, hpp = tau // 8, tau % 8
        par = tau % 2

        def ex(e, par=par):
            return e.activation(out=pt[:, par], in_=s2[:, :, 0:128], func=AF.Exp)
        sq.c("scalar", ex, at=(m_smm[tau] if tau >= 2 else None))
        m_ex[tau] = sq.mark()

        def mk(e, par=par):
            return e.tensor_tensor(pt[:, par].rearrange("p j t -> p (j t)"),
                                   pt[:, par].rearrange("p j t -> p (j t)"),
                                   l1m2[:].rearrange("p j t -> p (j t)"), ALU.mult)
        sq.c("gpsimd", mk, at=(m_ex[tau] if tau >= 2 else None))
        m_mk = sq.mark()
        if tau + 1 < 128:
            sq.c("tensor", mk_smm(tau + 1),
                 at=(m_ex[tau] if tau >= 1 else None))
            m_smm[tau + 1] = sq.mark()

        def avm(e, tt=tt, hpp=hpp, par=par):
            for j in range(2):
                h = 2 * hpp + j
                i = e.matmul(av2[par][:, j, 0:128], vsb[:, tt, h, :], pt[:, par, j],
                             start=True, stop=True)
            return i
        sq.c("tensor", avm, at=(m_mk if tau >= 2 else None))
        m_avm = sq.mark()

        sq.c("vector", lambda e, par=par: e.reciprocal(
            rrow[:], av2[par][64:65, :, 0:128]),
             at=(m_avm if tau >= 2 else None))
        m_rc = sq.mark()
        sq.c("tensor", lambda e: e.matmul(
            repp[:], ones64[:], rrow[:].rearrange("o j t -> o (j t)"),
            start=True, stop=True),
             at=(m_rc if tau >= 2 else None))
        m_rp = sq.mark()
        sq.c("scalar", lambda e: e.activation(
            out=repsb[:].rearrange("p j t -> p (j t)"),
            in_=repp[:], func=AF.Copy),
             at=(m_rp if tau >= 2 else None))
        m_cp = sq.mark()
        for j in range(2):
            def div(e, hpp=hpp, j=j, tt=tt, par=par):
                r0 = j * 64
                return e.tensor_tensor(att[r0:r0 + 64, hpp, tt * 128:(tt + 1) * 128],
                                       av2[par][0:64, j, 0:128], repsb[:, j], ALU.mult)
            sq.c("vector", div, at=(m_cp if tau >= 2 else None))

    m_mmo = {}
    m_evo = {}
    for k in range(32):
        fc, g4 = k // 4, k % 4

        def mmo(e, fc=fc, g4=g4):
            for dc in range(8):
                i = e.matmul(op[:], opw[:, dc, fc * 128:(fc + 1) * 128],
                             att[:, dc, g4 * 512:(g4 + 1) * 512],
                             start=(dc == 0), stop=(dc == 7))
            return i
        sq.c("tensor", mmo, at=(m_evo[k - 1] if k >= 1 else None))
        m_mmo[k] = sq.mark()
        sq.c("scalar", lambda e, fc=fc, k=k: e.activation(
            out=obuf[:, k % 2], in_=op[:], func=AF.Identity,
            bias=opb[:, fc:fc + 1]), at=m_mmo[k])
        m_evo[k] = sq.mark()
        sq.d("sync", lambda e, fc=fc, g4=g4, k=k: [
            e.dma_start(out=dr["attn1p"][fc, :, g4 * 512:(g4 + 1) * 512],
                        in_=obuf[:, k % 2])], at=m_evo[k])
    es2.close()



def p10(cx, sq, es):
    nc = cx.nc
    dr = cx.dram
    sb = lambda n, s, t: es.enter_context(nc.sbuf_tensor(n, s, t))

    ident = sb("p10_ident", [128, 128], F32)
    ones_row = sb("p10_onesr", [1, 128], F32)
    sq.d("sync", lambda e: [
        e.dma_start(out=ident[:], in_=dr["c_ident"][:]),
        e.dma_start(out=ones_row[:], in_=dr["c_ones_row"][:]),
    ])

    es1 = ExitStack()
    sb1 = lambda n, s, t: es1.enter_context(nc.sbuf_tensor(n, s, t))
    ps1 = lambda n, s: es1.enter_context(nc.psum_tensor(n, s, F32))
    wrow = sb1("p10_wrow", [128, 4096], F32)
    scl = sb1("p10_scl", [128, 32], F32)
    sclT = sb1("p10_sclT", [32, 128], F32)
    screp = sb1("p10_screp", [128, 4096], F32)
    rcrep = sb1("p10_rcrep", [128, 4096], F32)
    scrow = sb1("p10_scrow", [1, 4096], F32)
    rcrow = sb1("p10_rcrow", [1, 4096], F32)
    wt = sb1("p10_wt", [128, 4096], F32)
    z = sb1("p10_z", [128, 4096], F32)
    gtl = sb1("p10_gtl", [128, 4096], F32)
    weffb = sb1("p10_weff", [128, 4096], BF16)
    tpp = ps1("p10_tpp", [32, 128])
    repP = ps1("p10_repP", [128, 512])

    def weff_prep(wname, wTname, nrows, scname, slabname):
        inner = WEIGHT_SHAPES[wname][1]
        ntile = nrows // 128
        for rt in range(ntile):
            sq.d("sync", lambda e, rt=rt, inner=inner: [
                e.dma_start(out=wrow[:, 0:inner],
                            in_=dr[wname][rt * 128:(rt + 1) * 128, :])])

            sq.c("vector", lambda e, rt=rt, inner=inner: e.tensor_reduce(
                scl[:, rt:rt + 1], wrow[:, 0:inner], AX.X, ALU.add,
                apply_absolute_value=True))
            sq.c("vector", lambda e, rt=rt, inner=inner: e.tensor_single_scalar(
                scl[:, rt:rt + 1], scl[:, rt:rt + 1], 1.0 / inner, ALU.mult))
            sq.c("vector", lambda e, rt=rt: e.tensor_single_scalar(
                scl[:, rt:rt + 1], scl[:, rt:rt + 1], 1e-5, ALU.max))

        def trs(e, ntile=ntile):
            return e.transpose(tpp[0:ntile, :], scl[:, 0:ntile], ident[:])
        sq.c("tensor", trs)
        sq.c("vector", lambda e, ntile=ntile: e.tensor_copy(sclT[0:ntile, :],
                                                            tpp[0:ntile, :]))
        sq.d("sync", lambda e, ntile=ntile, nrows=nrows: [
            e.dma_start(out=dr[scname][0:1, 0:nrows].rearrange(
                "one (a b) -> (one a) b", a=ntile), in_=sclT[0:ntile, :])])
        sq.d("sync", lambda e, nrows=nrows: [
            e.dma_start(out=scrow[:, 0:nrows], in_=dr[scname][:, 0:nrows])])
        sq.c("vector", lambda e, nrows=nrows: e.reciprocal(rcrow[:, 0:nrows],
                                                           scrow[:, 0:nrows]))
        for half in range(nrows // 512):
            def repmm(e, half=half):
                return e.matmul(repP[:], ones_row[:],
                                scrow[:, half * 512:(half + 1) * 512],
                                start=True, stop=True)
            sq.c("tensor", repmm)
            sq.c("vector", lambda e, half=half: e.tensor_copy(
                screp[:, half * 512:(half + 1) * 512], repP[:]))

            def repmm2(e, half=half):
                return e.matmul(repP[:], ones_row[:],
                                rcrow[:, half * 512:(half + 1) * 512],
                                start=True, stop=True)
            sq.c("tensor", repmm2)
            sq.c("vector", lambda e, half=half: e.tensor_copy(
                rcrep[:, half * 512:(half + 1) * 512], repP[:]))
        nchT = WEIGHT_SHAPES[wTname][0] // 128
        for dc in range(nchT):
            sq.d("sync", lambda e, dc=dc, nrows=nrows: [
                e.dma_start(out=wt[:, 0:nrows],
                            in_=dr[wTname][dc * 128:(dc + 1) * 128, :])])

            sq.c("vector", lambda e, nrows=nrows: e.tensor_tensor(
                z[:, 0:nrows], wt[:, 0:nrows], rcrep[:, 0:nrows], ALU.mult))
            sq.c("vector", lambda e, nrows=nrows: e.tensor_single_scalar(
                gtl[:, 0:nrows], z[:, 0:nrows], 0.5, ALU.is_gt))
            sq.c("vector", lambda e, nrows=nrows: e.tensor_single_scalar(
                z[:, 0:nrows], z[:, 0:nrows], -0.5, ALU.is_lt))
            sq.c("vector", lambda e, nrows=nrows: e.tensor_tensor(
                z[:, 0:nrows], gtl[:, 0:nrows], z[:, 0:nrows], ALU.subtract))
            sq.c("vector", lambda e, nrows=nrows: e.tensor_tensor(
                weffb[:, 0:nrows], z[:, 0:nrows], screp[:, 0:nrows], ALU.mult))
            sq.d("sync", lambda e, dc=dc, nrows=nrows, slabname=slabname: [
                e.dma_start(out=dr[slabname][dc, :, 0:nrows], in_=weffb[:, 0:nrows])])

    weff_prep("wfc", "wfcT", 4096, "scfc", "wfcE")
    weff_prep("wpr", "wprT", 1024, "scpr", "wprE")
    es1.close()

    es2 = ExitStack()
    sb2 = lambda n, s, t: es2.enter_context(nc.sbuf_tensor(n, s, t))
    ps2 = lambda n, s: es2.enter_context(nc.psum_tensor(n, s, F32))
    wfce = sb2("p10_wfce", [128, 8, 4096], BF16)
    wpre = sb2("p10_wpre", [128, 32, 1024], BF16)
    vT = sb2("p10_vT", [128, 8, 16], BF16)
    uT = sb2("p10_uT", [16, 4096], BF16)
    fbT = sb2("p10_fbT", [16, 16], BF16)
    gate = sb2("p10_gate", [1, 1], F32)
    gcol = sb2("p10_gcol", [16, 1], F32)
    xnb = sb2("p10_xnb", [128, 8, T], BF16)
    t1 = sb2("p10_t1", [16, 512], BF16)
    t2 = sb2("p10_t2", [16, 512], BF16)
    r2 = sb2("p10_r2", [128, 32, 512], BF16)
    rbuf = sb2("p10_rbuf", [128, 512], BF16)
    obuf = sb2("p10_obuf", [128, 512], F32)
    t1p = ps2("p10_t1p", [16, 512])
    t2p = ps2("p10_t2p", [16, 512])
    hpp = [ps2(f"p10_hp{i}", [128, 512]) for i in range(2)]
    op = ps2("p10_op", [128, 512])
    gp = ps2("p10_gp", [16, 1])

    sq.d("sync", lambda e: [
        e.dma_start(out=wfce[:], in_=dr["wfcE"][:].rearrange("c p f -> p c f")),
        e.dma_start(out=wpre[:], in_=dr["wprE"][:].rearrange("c p f -> p c f")),
        e.dma_start(out=gate[:], in_=dr["l1gate"][:]),
    ])
    cast_load(sq, vT[:], dr["l1vT"][:].rearrange("(c p) f -> p c f", p=128))
    cast_load(sq, uT[:], dr["l1uT"][:])
    cast_load(sq, fbT[:], dr["fbT"][:])

    def grep(e):
        return e.matmul(gp[:], ones_row[0:1, 0:16], gate[:], start=True, stop=True)
    sq.c("tensor", grep)
    sq.c("vector", lambda e: e.tensor_copy(gcol[:], gp[:]))

    for b in range(B):
        sq.d("sync", lambda e, b=b: [
            e.dma_start(out=xnb[:], in_=dr["xn1bf"][b].rearrange("c p t -> p c t"))])

        def mmt1(e):
            for dc in range(8):
                i = e.matmul(t1p[:], vT[:, dc], xnb[:, dc], start=(dc == 0),
                             stop=(dc == 7))
            return i
        sq.c("tensor", mmt1)
        sq.c("vector", lambda e: e.tensor_single_scalar(t1[:], t1p[:], gcol[:],
                                                        ALU.mult))
        sq.c("tensor", lambda e: e.matmul(t2p[:], fbT[:], t1[:], start=True, stop=True))
        sq.c("vector", lambda e: e.tensor_copy(t2[:], t2p[:]))
        for hc in range(32):
            def mmh(e, hc=hc):
                for dc in range(8):
                    e.matmul(hpp[hc % 2][:], wfce[:, dc, hc * 128:(hc + 1) * 128],
                             xnb[:, dc], start=(dc == 0), stop=False)
                return e.matmul(hpp[hc % 2][:], uT[:, hc * 128:(hc + 1) * 128], t2[:],
                                start=False, stop=True)
            sq.c("tensor", mmh)

            def evh(e, hc=hc):
                return e.tensor_single_scalar(rbuf[:], hpp[hc % 2][:], 0.0, ALU.max)
            sq.c("vector", evh)
            sq.c("scalar", lambda e, hc=hc: e.activation(out=r2[:, hc], in_=rbuf[:],
                                                         func=AF.Square))
        for dc in range(8):
            def mmo(e, dc=dc):
                for hc in range(32):
                    i = e.matmul(op[:], wpre[:, hc, dc * 128:(dc + 1) * 128],
                                 r2[:, hc], start=(hc == 0), stop=(hc == 31))
                return i
            sq.c("tensor", mmo)
            sq.c("vector", lambda e: e.tensor_copy(obuf[:], op[:]))
            sq.d("sync", lambda e, b=b, dc=dc: [
                e.dma_start(out=dr["ffn1T"][b, dc], in_=obuf[:])])
    es2.close()



def p11(cx, sq, es):
    nc = cx.nc
    dr = cx.dram
    sb = lambda n, s, t: es.enter_context(nc.sbuf_tensor(n, s, t))
    ps = lambda n, s: es.enter_context(nc.psum_tensor(n, s, F32))

    ident = sb("p11_ident", [128, 128], F32)
    x2 = sb("p11_x2", [128, 8, T], F32)
    tmp = sb("p11_tmp", [128, 8, T], F32)
    ap = sb("p11_ap", [128, 8, 4 * T], F32)
    obuf = sb("p11_obuf", [128, 1024], F32)
    tp_p = ps("p11_tp", [128, 1024])

    sq.d("sync", lambda e: [e.dma_start(out=ident[:], in_=dr["c_ident"][:])])
    sq.d("sync", lambda e: [
        e.dma_start(out=ap[:], in_=dr["attn1p"][:].rearrange("c p t -> p c t"))])
    for b in range(B):
        sq.d("sync", lambda e, b=b: [
            e.dma_start(out=x2[:], in_=dr["x1T32"][b].rearrange("c p t -> p c t")),
            e.dma_start(out=tmp[:], in_=dr["ffn1T"][b].rearrange("c p t -> p c t")),
        ])

        def add1(e):
            for c in range(8):
                i = e.tensor_tensor(x2[:, c], x2[:, c], tmp[:, c], ALU.add)
            return i
        sq.c("vector", add1)

        def add2(e, b=b):
            for c in range(8):
                i = e.tensor_tensor(
                    x2[:, c], x2[:, c],
                    ap[:, c].rearrange("p (t i) -> p t i", i=4)[:, :, b], ALU.add)
            return i
        sq.c("vector", add2)
        for tt in range(NT):
            def tr(e, tt=tt):
                for c in range(8):
                    i = e.transpose(tp_p[:, c * 128:(c + 1) * 128],
                                    x2[:, c, tt * 128:(tt + 1) * 128], ident[:])
                return i
            sq.c("tensor", tr)
            sq.c("vector", lambda e: e.tensor_copy(obuf[:], tp_p[:]))
            sq.d("sync", lambda e, b=b, tt=tt: [
                e.dma_start(out=dr["outp"][b, tt * 128:(tt + 1) * 128, :], in_=obuf[:])])


PHASES = {"p1": p1, "p2": p2, "p2w": p2w, "p3": p3, "p4a": p4a, "p4f": p4f,
          "p4af": p4af, "p4ff": p4ff, "p5": p5, "p7": p7, "p8": p8, "p9": p9,
          "p10": p10, "p11": p11}



def build_program(phases, ext_in_slabs=(), ext_out_slabs=()):
    nc = bass.Bass("TRN2", target_bir_lowering=False)
    cx = Ctx(nc)
    sq = Seq(cx)

    need = set()
    for p in phases:
        need.update(PHASE_WEIGHTS.get(p, []))
    if "p1" in phases:
        cx.dram_t("xsl", [B, T, D], F32, "ExternalInput")
    for w in sorted(need):
        cx.dram_t(w, list(WEIGHT_SHAPES[w]), F32, "ExternalInput")

    if "p2w" in phases:
        cx.dram_t("xfull", [B, S, D], F32, "ExternalInput")
    used = set()
    for p in phases:
        r, w = PHASE_SLABS[p]
        used.update(r)
        used.update(w)
    for name in sorted(used):
        shape, dt = SLABS.get(name) or GATHERED.get(name) or WIDE[name]
        if name in ext_in_slabs:
            kind = "ExternalInput"
        elif name in ext_out_slabs:
            kind = "ExternalOutput"
        else:
            kind = "Internal"
        cx.dram_t(name, shape, dt, kind)
    if "p11" in phases:
        cx.dram_t("outp", [B, T, D], F32, "ExternalOutput")

    with nc.Block() as blk:
        for p in phases:
            es = ExitStack()
            PHASES[p](cx, sq, es)
            sq.flush(blk)
            es.close()
        sq.finish(blk)
    cx.es.close()
    return nc



def assemble_gathered(resA):
    kattn = np.concatenate([r["kattn"] for r in resA], axis=3)
    kffn = np.concatenate([r["kffn"] for r in resA], axis=3)
    vattn = np.concatenate([r["vattn"] for r in resA], axis=2)
    vffn = np.concatenate([r["vffn"] for r in resA], axis=2)
    out = []
    for c in range(NC):
        ko, vo = reorder_causal(kattn, vattn, c)
        out.append({"kattn_r": ko, "vattn_r": vo, "kffn_g": kffn, "vffn_g": vffn})
    return out


PH_A = ["p1", "p2"]
PH_B = ["p3", "p4a", "p4f", "p5", "p7", "p8", "p9", "p10", "p11"]
PH_FUSED = ["p1", "p2", "p2w", "p3", "p4af", "p4ff", "p5", "p7", "p8", "p9",
            "p10", "p11"]
A_OUT = ["xT32", "xnbf", "xnf32", "kattn", "kffn", "vattn", "vffn"]
B_IN = ["xT32", "xnbf", "xnf32", "kattn_r", "kffn_g", "vattn_r", "vffn_g"]


def kernel_unfused(**inputs):
    g = stage_common(inputs)
    x = np.ascontiguousarray(np.asarray(inputs["L0_attn_x"], np.float32))

    ncA = build_program(PH_A, ext_out_slabs=A_OUT)
    in_mapsA = []
    for c in range(NC):
        m = {"xsl": np.ascontiguousarray(x[:, c * T:(c + 1) * T, :])}
        for w in sorted(set(PHASE_WEIGHTS["p1"]) | set(PHASE_WEIGHTS["p2"])):
            m[w] = g[w]
        in_mapsA.append(m)
    resA = run_bass_kernel_spmd(ncA, in_mapsA, core_ids=list(range(NC))).results

    percore_kv = assemble_gathered(resA)

    ncB = build_program(PH_B, ext_in_slabs=B_IN)
    wB = set()
    for p in PH_B:
        wB.update(PHASE_WEIGHTS.get(p, []))
    in_mapsB = []
    for c in range(NC):
        m = {}
        for w in sorted(wB):
            m[w] = g[w]
        for nm in ("xT32", "xnbf", "xnf32"):
            m[nm] = resA[c][nm]
        m.update(percore_kv[c])
        in_mapsB.append(m)
    resB = run_bass_kernel_spmd(ncB, in_mapsB, core_ids=list(range(NC))).results

    out = np.empty((B, S, D), np.float32)
    for c in range(NC):
        out[:, c * T:(c + 1) * T, :] = resB[c]["outp"]
    return out


def kernel_fused(**inputs):
    g = stage_common(inputs)
    x = np.ascontiguousarray(np.asarray(inputs["L0_attn_x"], np.float32))
    ncF = build_program(PH_FUSED)
    wF = set()
    for p in PH_FUSED:
        wF.update(PHASE_WEIGHTS.get(p, []))
    wF.discard("valid8")
    in_maps = []
    for c in range(NC):
        m = {"xsl": np.ascontiguousarray(x[:, c * T:(c + 1) * T, :]), "xfull": x}
        for w in sorted(wF):
            m[w] = g[w]
        valid = np.zeros((128, 8), np.float32)
        valid[:, :c] = 1.0
        m["valid8"] = valid
        in_maps.append(m)
    res = run_bass_kernel_spmd(ncF, in_maps, core_ids=list(range(NC))).results
    out = np.empty((B, S, D), np.float32)
    for c in range(NC):
        out[:, c * T:(c + 1) * T, :] = res[c]["outp"]
    return out


kernel = kernel_fused
